# version 02
# baseline (speedup 1.0000x reference)
import numpy as np
from contextlib import ExitStack

import concourse.bass as bass
import concourse.mybir as mybir
from concourse.bass_utils import run_bass_kernel_spmd

F32 = mybir.dt.float32
BF16 = mybir.dt.bfloat16
AF = mybir.ActivationFunctionType
ALU = mybir.AluOpType

B, L, D = 2, 4096, 2048
TPC = 1024
NCH = D // 128
NTC = TPC // 128
HA, DH, NH = 8, 128, 2
SS = TPC * NH
NAC = SS // 128
HB, KD, VEFF = 16, 128, 256
NBC = TPC // 128

NEG = -1.0e30



def out_slice_fn(kind, h, t):
    if kind == "q":
        return t["aq"][:, h * 1024:(h + 1) * 1024]
    if kind == "k0":
        return t["ak"][:, h * 2048:h * 2048 + 2048:2]
    if kind == "k1":
        return t["ak"][:, h * 2048 + 1:h * 2048 + 2048:2]
    if kind == "v0":
        return t["vscr"][:, 0:2048:2]
    return t["vscr"][:, 1:2048:2]


class Sem:
    def __init__(self, h):
        self.h = h
        self.n = 0

    def inc(self, inst, k=1):
        inst.then_inc(self.h, k)
        self.n += k
        return self.n


class Builder:
    def __init__(self, debug_taps=(), stop=None):
        self.stop = stop
        self.debug_taps = list(debug_taps)
        nc = bass.Bass("TRN2", target_bir_lowering=False, detect_race_conditions=False)
        self.nc = nc
        self.ctx = ExitStack()
        self.t = {}

    def sb(self, name, shape, dtype, stack=None):
        h = (stack or self.ctx).enter_context(self.nc.sbuf_tensor(name, shape, dtype))
        self.t[name] = h
        return h

    def ps(self, name, shape, dtype=F32):
        h = self.ctx.enter_context(self.nc.psum_tensor(name, shape, dtype))
        self.t[name] = h
        return h

    def sem(self, name):
        s = Sem(self.ctx.enter_context(self.nc.semaphore(name)))
        self.t[name] = s
        return s

    def tap(self, name, ap_fn, shape, dtype=F32):
        if name not in self.debug_taps:
            return
        nc = self.nc
        out = nc.dram_tensor("tap_" + name, shape, dtype, kind="ExternalOutput")
        sem = Sem(self.ctx.enter_context(nc.semaphore("tapsem_" + name)))
        with nc.Block() as blk:
            @blk.sync
            def _(s):
                s.dma_start(out=out.ap(), in_=ap_fn()).then_inc(sem.h, 16)
                s.wait_ge(sem.h, 16)

    def proj_lhsT(self, wname, n_oc, ocols, kcs, rhs_fn, evac_fn, nt=2, tw=512,
                  scalar_fn=None, store_fn=None):
        nc, t = self.nc, self.t
        s_w, s_wc, s_mm, s_ev = t["s_w"], t["s_wc"], t["s_mm"], t["s_ev"]
        nk = len(kcs)
        wslab = t["wslab"]
        slabw = nk * ocols
        assert slabw <= 4096
        d = self.d
        w_base, wc_base, mm_base, ev_base = s_w.n, s_wc.n, s_mm.n, s_ev.n
        ngr = n_oc * nt

        with nc.Block() as blk:
            @blk.gpsimd
            def _(g):
                for oc in range(n_oc):
                    if oc >= 3:
                        g.wait_ge(s_ev.h, ev_base + (oc - 2) * nt)
                    g.dma_start(
                        out=wslab[:, (oc % 3) * 4096:(oc % 3) * 4096 + slabw],
                        in_=d[wname].ap()[oc],
                    ).then_inc(s_w.h, 16)
            s_w.n += 16 * n_oc

            @blk.tensor
            def _(te):
                for oc in range(n_oc):
                    te.wait_ge(s_w.h, w_base + 16 * (oc + 1))
                    for tb in range(nt):
                        gi = oc * nt + tb
                        ps = t["ps_p0" if gi % 2 == 0 else "ps_p1"][0:ocols, 0:tw]
                        if gi >= 2:
                            te.wait_ge(s_ev.h, ev_base + gi - 1)
                        for i, kc in enumerate(kcs):
                            ins = te.matmul(
                                ps,
                                wslab[:, (oc % 3) * 4096 + i * ocols:(oc % 3) * 4096 + (i + 1) * ocols],
                                rhs_fn(i, tb),
                                start=(i == 0), stop=(i == nk - 1))
                        ins.then_inc(s_mm.h, 1)
            s_mm.n += ngr

            s_st = t["s_st"]
            st_base = s_st.n

            @blk.vector
            def _(v):
                for oc in range(n_oc):
                    for tb in range(nt):
                        gi = oc * nt + tb
                        v.wait_ge(s_mm.h, mm_base + gi + 1)
                        if store_fn is not None and gi >= 4:
                            v.wait_ge(s_st.h, st_base + 16 * (gi - 3))
                        ps = t["ps_p0" if gi % 2 == 0 else "ps_p1"][0:ocols, 0:tw]
                        evac_fn(v, oc, tb, ps)
                        v.engine_nop()
                        v.engine_nop().then_inc(s_ev.h, 1)
            s_ev.n += ngr

            if scalar_fn is not None:
                @blk.scalar
                def _(sc):
                    for oc in range(n_oc):
                        for tb in range(nt):
                            gi = oc * nt + tb
                            scalar_fn(sc, oc, tb, mm_base + gi + 1)

            if store_fn is not None:
                @blk.sync
                def _(se):
                    for oc in range(n_oc):
                        for tb in range(nt):
                            gi = oc * nt + tb
                            store_fn(se, oc, tb, ev_base + gi + 1)
                s_st.n += 16 * ngr

    def build(self):
        nc = self.nc
        d = {}

        def din(name, shape):
            d[name] = nc.dram_tensor(name, shape, F32, kind="ExternalInput")
            return d[name]

        din("xs", [D, TPC])
        din("rankinfo", [1, 8])
        din("Wq_a", [8, 128, NCH * 128])
        din("Wk_a", [16, 128, NCH * 128])
        din("Wv_a", [16, 128, NCH * 128])
        din("Wba_w", [1, 128, NCH * 40])
        din("Wo_a", [16, 128, 8 * 128])
        din("ffn_w", [16, 128, NCH * 128])
        din("Wq_b", [16, 128, NCH * 128])
        din("Wk_b", [16, 128, NCH * 128])
        din("Wgb_w", [1, 128, NCH * 48])
        din("Wo_b", [16, 128, 32 * 128])
        din("Wg_a", [2, 128, NCH * 512])
        din("Wv_b", [8, 128, NCH * 512])
        din("conv_q_a", [128, 8 * 4])
        din("conv_k_a", [128, 16 * 4])
        din("conv_v_a", [128, 16 * 4])
        din("convW_q_b", [4, 128, 2048])
        din("convW_k_b", [4, 128, 2048])
        din("bq_conv_b", [128, 16])
        din("bk_conv_b", [128, 16])
        din("bg_b", [1, 16])
        din("bb_b", [1, 16])
        din("A_log", [8, 1])
        din("dt_bias", [8, 1])
        din("o_norm_w", [1, 128])
        din("ffn_scale", [128, 16])
        din("ffn_bias", [128, 16])
        self.d = d

        self.y = nc.dram_tensor("y", [D, TPC], F32, kind="ExternalOutput")
        self.x_park = nc.dram_tensor("x_park", [D, TPC], F32)
        self.cc = {}
        for nm, shape in [
            ("halo_a_in", [128, 120]), ("halo_a_out", [512, 120]),
            ("rel_a_in", [128, HA * 256]), ("rel_a_out", [512, HA * 256]),
            ("halo_b_in", [128, 96]), ("halo_b_out", [512, 96]),
            ("rel_b_in", [128, HB * VEFF + 16]), ("rel_b_out", [512, HB * VEFF + 16]),
        ]:
            self.cc[nm] = nc.dram_tensor("cc_" + nm, shape, F32)

        import ml_dtypes
        ident_bf = np.eye(128, dtype=ml_dtypes.bfloat16)
        self.ident_dram = nc.inline_tensor(ident_bf.view(np.uint16), name="c_ident")
        ii, jj = np.arange(128)[:, None], np.arange(128)[None, :]
        self.maskS_dram = nc.inline_tensor(np.where(jj > ii, 0.0, NEG).astype(np.float32), name="c_maskS")
        self.maskI_dram = nc.inline_tensor(np.where(jj >= ii, 0.0, NEG).astype(np.float32), name="c_maskI")
        E = np.zeros((128, 64), np.float32)
        for h in range(8):
            E[:, 8 * h + h] = 1.0
        self.E_dram = nc.inline_tensor(E.astype(ml_dtypes.bfloat16).view(np.uint16), name="c_E")
        self.identf_dram = nc.inline_tensor(np.eye(128, dtype=np.float32), name="c_identf")
        mq = np.where(np.arange(128)[:, None] <= 2 * np.arange(64)[None, :] + 1, 0.0, NEG)
        self.maskQ_dram = nc.inline_tensor(mq.astype(np.float32), name="c_maskQ")

        self.sb("ident", [128, 128], BF16)
        self.sb("maskS", [128, 128], F32)
        self.sb("maskI", [128, 128], F32)
        self.sb("Ecols", [128, 64], BF16)
        self.sb("ones_col", [128, 1], BF16)
        self.sb("ones_row", [1, 128], BF16)
        self.sb("ones_rowf", [1, 128], F32)
        self.sb("maskQ", [128, 64], F32)
        self.sb("rmask", [128, 8], F32)
        self.sb("cst", [128, 4], F32)
        self.sb("cq_a", [128, 8 * 4], F32)
        self.sb("ck_a", [128, 16 * 4], F32)
        self.sb("cv_a", [128, 16 * 4], F32)
        self.sb("bqk_conv_b", [128, 2 * 16], F32)
        self.sb("A_log_c", [8, 1], F32)
        self.sb("dt_bias_c", [8, 1], F32)
        self.sb("negexpA", [8, 1], F32)
        self.sb("o_nw_row", [1, 128], F32)
        self.sb("Wonw_bc", [128, 128], F32)
        self.sb("ffn_sc", [128, 16], F32)
        self.sb("ffn_bi", [128, 16], F32)
        self.sb("gb16", [1, 16], F32)
        self.sb("bb16", [1, 16], F32)

        self.sb("identf", [128, 128], F32)
        self.sb("row_scr", [1, 4 * 512], F32)
        self.sb("ab_rows", [1, 2 * TPC], F32)
        self.sb("ab_rows_bf", [1, 2 * TPC], BF16)
        self.sb("sq_scr", [128, 4 * 512], BF16)
        self.sb("wslab", [128, 3 * 4096], BF16)
        self.abase = self.nc._sbuf_addr_for_side("left")
        self.asize = 229344 - self.abase - 64
        def at(name, shape, dtype, off):
            nbytes = int(np.prod(shape[1:])) * (2 if dtype == BF16 else 4)
            assert off + nbytes <= self.asize, (name, off, nbytes, self.asize)
            h = self.nc.alloc_sbuf_tensor_at(name, shape, dtype, offset=self.abase + off)
            self.t[name] = h
            return h
        self.at = at
        K = 1024
        at("x_res", [128, NCH * TPC], F32, 0)
        at("x_bf", [128, NCH * TPC], BF16, 65 * K)
        at("xn", [128, NCH * TPC], BF16, 97 * K)

        self.ps("ps_p0", [128, 512])
        self.ps("ps_p1", [128, 512])
        self.ps("ps_a", [128, 512])
        self.ps("ps_b", [128, 512])
        self.ps("ps_c", [128, 512])
        self.ps("ps_row", [128, 512])
        self.ps("ps_t0", [128, 512], BF16)
        self.ps("ps_t1", [128, 512], BF16)

        for s in ["s_ld", "s_w", "s_wc", "s_mm", "s_ev", "s_ev2", "s_pk", "s_cc",
                  "s_st", "s_x", "s_t0", "s_t1", "s_pa", "s_pb", "s_pc",
                  "s_row", "s_act", "s_ve", "s_g"]:
            self.sem(s)

        self.p0_loads()
        self.ln("xn1")
        self.attn_a()
        if self.stop is None:
            self.ln("xn2")
            self.ffn()
            self.ln("xn3")
            self.attn_b()
            self.ln("xn4")
            self.ffn()
        self.finish()
        self.ctx.close()
        return self.nc

    def p0_loads(self):
        nc, t, d = self.nc, self.t, self.d
        s_ld = t["s_ld"]

        with nc.Block() as blk:
            @blk.sync
            def _(s):
                loads = [
                    (t["x_res"][:], d["xs"].ap().rearrange("(c p) t -> p c t", p=128)),
                    (t["ident"][:].bitcast(mybir.dt.uint16), self.ident_dram.ap()),
                    (t["identf"][:], self.identf_dram.ap()),
                    (t["maskQ"][:], self.maskQ_dram.ap()),
                    (t["maskS"][:], self.maskS_dram.ap()),
                    (t["maskI"][:], self.maskI_dram.ap()),
                    (t["Ecols"][:].bitcast(mybir.dt.uint16), self.E_dram.ap()),
                    (t["cq_a"][:], d["conv_q_a"].ap()),
                    (t["ck_a"][:], d["conv_k_a"].ap()),
                    (t["cv_a"][:], d["conv_v_a"].ap()),
                    (t["A_log_c"][:], d["A_log"].ap()),
                    (t["dt_bias_c"][:], d["dt_bias"].ap()),
                    (t["o_nw_row"][:], d["o_norm_w"].ap()),
                    (t["ffn_sc"][:], d["ffn_scale"].ap()),
                    (t["ffn_bi"][:], d["ffn_bias"].ap()),
                    (t["gb16"][:], d["bg_b"].ap()),
                    (t["bb16"][:], d["bb_b"].ap()),
                    (t["bqk_conv_b"][:, 0:16], d["bq_conv_b"].ap()),
                    (t["bqk_conv_b"][:, 16:32], d["bk_conv_b"].ap()),
                    (t["row_scr"][0:1, 0:8], d["rankinfo"].ap()),
                ]
                for o_, i_ in loads:
                    s.dma_start(out=o_, in_=i_).then_inc(s_ld.h, 16)

            @blk.gpsimd
            def _(g):
                g.memset(t["ones_col"][:], 1.0)
                g.memset(t["ones_row"][:], 1.0)
                g.memset(t["ones_rowf"][:], 1.0)
                g.memset(t["cst"][:, 0:1], 1e-5)
                g.memset(t["cst"][:, 1:2], 1.0 / 128.0)
                g.memset(t["cst"][:, 2:3], 1.0)
            s_ld.n += 16 * 20

        with nc.Block() as blk:
            n_ld = s_ld.n
            s_mm, s_act = t["s_mm"], t["s_act"]

            @blk.scalar
            def _(sc):
                sc.wait_ge(s_ld.h, n_ld)
                for c in range(NCH):
                    sc.copy(t["x_bf"][:, c * TPC:(c + 1) * TPC],
                            t["x_res"][:, c * TPC:(c + 1) * TPC])
                sc.activation(t["negexpA"][:], t["A_log_c"][:], AF.Exp).then_inc(s_act.h, 1)

            @blk.vector
            def _(v):
                v.wait_ge(s_act.h, s_act.n + 1)
                v.tensor_scalar_mul(t["negexpA"][:], t["negexpA"][:], -1.0)
                v.wait_ge(s_ld.h, n_ld)
                v.tensor_copy(t["ab_rows_bf"][0:1, 0:8], t["row_scr"][0:1, 0:8])
                v.tensor_copy(t["ab_rows_bf"][0:1, 8:136], t["o_nw_row"][0:1, 0:128]).then_inc(t["s_ve"].h, 1)
            s_act.n += 1
            s_ve_n = t["s_ve"].n = t["s_ve"].n + 1

            @blk.tensor
            def _(te):
                te.wait_ge(t["s_ve"].h, s_ve_n)
                te.matmul(t["ps_row"][:, 0:8], t["ones_row"][:], t["ab_rows_bf"][0:1, 0:8],
                          start=True, stop=True)
                te.matmul(t["ps_row"][:, 8:136], t["ones_row"][:], t["ab_rows_bf"][0:1, 8:136],
                          start=True, stop=True).then_inc(s_mm.h, 1)
            s_mm.n += 1

        with nc.Block() as blk:
            n_mm = t["s_mm"].n

            @blk.scalar
            def _(sc):
                sc.wait_ge(t["s_mm"].h, n_mm)
                sc.copy(t["rmask"][:], t["ps_row"][:, 0:8])
                sc.copy(t["Wonw_bc"][:], t["ps_row"][:, 8:136])

    def ln(self, tapn):
        nc, t = self.nc, self.t
        s_mm, s_act, s_ve, s_row, s_g = t["s_mm"], t["s_act"], t["s_ve"], t["s_row"], t["s_g"]
        src, dst = "x_bf", "xn"

        with nc.Block() as blk:
            mm_base, act_base = s_mm.n, s_act.n

            @blk.scalar
            def _(sc):
                for i in range(2 * NCH):
                    c, tb = i // 2, i % 2
                    if i >= 4:
                        sc.wait_ge(s_mm.h, mm_base + (i - 4) + 1)
                    sc.activation(t["sq_scr"][:, (i % 4) * 512:(i % 4) * 512 + 512],
                                  t[src][:, c * TPC + tb * 512: c * TPC + tb * 512 + 512],
                                  AF.Square).then_inc(s_act.h, 1)
            s_act.n += 2 * NCH

            @blk.tensor
            def _(te):
                for i in range(2 * NCH):
                    c, tb = i // 2, i % 2
                    te.matmul(t["ps_a" if tb == 0 else "ps_b"][0:1, :],
                              t["ones_col"][:],
                              t[src][:, c * TPC + tb * 512: c * TPC + tb * 512 + 512],
                              start=(c == 0), stop=(c == NCH - 1))
                    te.wait_ge(s_act.h, act_base + i + 1)
                    te.matmul(t["ps_c" if tb == 0 else "ps_row"][0:1, :],
                              t["ones_col"][:],
                              t["sq_scr"][:, (i % 4) * 512:(i % 4) * 512 + 512],
                              start=(c == 0), stop=(c == NCH - 1)).then_inc(s_mm.h, 1)
            s_mm.n += 2 * NCH

        with nc.Block() as blk:
            @blk.vector
            def _(v):
                v.tensor_copy(t["row_scr"][0:1, 0:512], t["ps_a"][0:1, :])
                v.tensor_copy(t["row_scr"][0:1, 512:1024], t["ps_b"][0:1, :])
                v.tensor_copy(t["row_scr"][0:1, 1024:1536], t["ps_c"][0:1, :])
                v.tensor_copy(t["row_scr"][0:1, 1536:2048], t["ps_row"][0:1, :])
                v.tensor_scalar_mul(t["ab_rows"][0:1, 0:1024], t["row_scr"][0:1, 0:1024], 1.0 / D)
                v.tensor_scalar_mul(t["ab_rows"][0:1, 1024:2048], t["row_scr"][0:1, 1024:2048], 1.0 / D)
                v.tensor_mul(t["row_scr"][0:1, 0:1024], t["ab_rows"][0:1, 0:1024], t["ab_rows"][0:1, 0:1024])
                v.engine_nop()
                v.engine_nop()
                v.tensor_sub(t["row_scr"][0:1, 1024:2048], t["ab_rows"][0:1, 1024:2048],
                             t["row_scr"][0:1, 0:1024])
                v.engine_nop()
                v.engine_nop().then_inc(s_row.h, 1)
                v.wait_ge(s_act.h, s_act.n + 1)
                v.reciprocal(t["row_scr"][0:1, 1024:2048], t["row_scr"][0:1, 0:1024])
                v.engine_nop()
                v.engine_nop().then_inc(s_row.h, 1)
            s_row.n += 2

            @blk.scalar
            def _(sc):
                sc.wait_ge(s_row.h, s_row.n - 1)
                sc.activation(t["row_scr"][0:1, 0:1024], t["row_scr"][0:1, 1024:2048],
                              AF.Sqrt, bias=t["cst"][0:1, 0:1]).then_inc(s_act.h, 1)
            s_act.n += 1

            @blk.gpsimd
            def _(g):
                g.wait_ge(s_row.h, s_row.n)
                g.tensor_mul(t["ab_rows"][0:1, 1024:2048], t["ab_rows"][0:1, 0:1024],
                             t["row_scr"][0:1, 1024:2048])
                g.engine_nop()
                g.engine_nop()
                g.tensor_scalar_mul(t["ab_rows"][0:1, 1024:2048], t["ab_rows"][0:1, 1024:2048], -1.0)
                g.tensor_copy(t["ab_rows"][0:1, 0:1024], t["row_scr"][0:1, 1024:2048])
                g.engine_nop()
                g.engine_nop()
                g.tensor_copy(t["ab_rows_bf"][0:1, 0:2048], t["ab_rows"][0:1, 0:2048])
                g.engine_nop()
                g.engine_nop().then_inc(s_g.h, 1)
            s_g.n += 1

        with nc.Block() as blk:
            banks = ["ps_a", "ps_b", "ps_c", "ps_row"]

            @blk.tensor
            def _(te):
                te.wait_ge(s_g.h, s_g.n)
                for half in range(4):
                    te.matmul(t[banks[half]][:, 0:512], t["ones_row"][:],
                              t["ab_rows_bf"][0:1, half * 512:(half + 1) * 512],
                              start=True, stop=True).then_inc(s_mm.h, 1)
            s_mm.n += 4

        with nc.Block() as blk:
            @blk.vector
            def _(v):
                for c in range(NCH):
                    for tb in range(2):
                        sl = slice(c * TPC + tb * 512, c * TPC + tb * 512 + 512)
                        v.tensor_mul(t[dst][:, sl], t[src][:, sl], t[banks[tb]][:, 0:512])
                        v.tensor_add(t[dst][:, sl], t[dst][:, sl], t[banks[2 + tb]][:, 0:512])

        self.tap(tapn, lambda: t[dst][:], [128, NCH * TPC], BF16)

    def ffn(self):
        t = self.t

        def rhs(i, tb):
            return t["xn"][:, i * TPC + tb * 512: i * TPC + tb * 512 + 512]

        def evac(v, oc, tb, ps):
            xsl = t["x_res"][:, oc * TPC + tb * 512: oc * TPC + tb * 512 + 512]
            v.scalar_tensor_tensor(xsl, ps, t["ffn_sc"][:, oc:oc + 1], xsl,
                                   ALU.mult, ALU.add)
            v.tensor_scalar_add(xsl, xsl, t["ffn_bi"][:, oc:oc + 1])
            v.tensor_copy(t["x_bf"][:, oc * TPC + tb * 512: oc * TPC + tb * 512 + 512], xsl)

        self.proj_lhsT("ffn_w", NCH, 128, list(range(NCH)), rhs, evac)

    def attn_a(self):
        nc, t, d = self.nc, self.t, self.d
        K = 1024

        def at(name, shape, dtype, offk):
            off = int(offk * K)
            nbytes = int(np.prod(shape[1:])) * (2 if dtype == BF16 else 4)
            assert off + nbytes <= self.asize, (name, off, nbytes, self.asize)
            h = self.nc.alloc_sbuf_tensor_at(name, shape, dtype, offset=self.abase + off)
            self.t[name] = h
            return h
        self.aat = at

        at("aq", [128, 8 * 1024], BF16, 0)
        at("ak", [128, 8 * 2048], BF16, 16)
        at("bpad", [16, 1024], F32, 48)
        at("grow", [8, 1024], F32, 52)
        at("Lp", [8, 2080], F32, 56)
        at("rowA", [8, 2048], F32, 65)
        at("colA", [8, 2048], F32, 73)
        at("rowAq", [8, 1024], F32, 81)
        at("b2il", [8, 2048], BF16, 85)
        at("toks", [128, 6 * 128], F32, 89)
        at("lamCbc", [128, 128], F32, 92)
        at("rows2", [8, 64], F32, 92.5)
        at("rowAfl", [1, 2 * 1024], F32, 93)
        at("rowAqfl", [1, 2 * 512], F32, 101)
        at("pscr", [128, 2 * 1024], BF16, 105)
        at("cacc", [128, 2 * 1024], F32, 109)
        at("vscr", [128, 2048], BF16, 117)
        at("bpadB", [8, 1024], F32, 117)
        at("sqs", [128, 2 * 1024], BF16, 121)
        at("tailpk", [128, 128], F32, 142)
        at("halo", [128, 128], F32, 142.5)
        at("evsc", [128, 4 * 512], BF16, 0)
        at("c6", [128, 16], F32, 130)
        at("vin", [128, 2 * 128], BF16, 131)
        at("ktk", [128, 2 * 2 * 128], BF16, 131.5)
        at("bvt", [128, 2 * 128], BF16, 132.5)
        at("earg", [128, 2 * 2 * 128], F32, 133)
        at("abf", [128, 2 * 2 * 128], BF16, 135)
        at("sol", [128, 2 * 10 * 128], BF16, 136)
        at("vw", [128, 2 * 2 * 128], BF16, 141)
        at("FM", [128, 2 * 8 * 128], F32, 142)
        at("FMbf", [128, 2 * 8 * 128], BF16, 48)
        at("Sbf", [128, 8 * 128], BF16, 52)
        at("relst", [128, 2048], F32, 105)
        at("Sin", [128, 8 * 128], F32, 150)
        at("ofm_a", [128, 4 * 1024], BF16, 105)
        at("ofm_b", [128, 4 * 1024], BF16, 117)
        at("ggb", [128, 2 * 1024], BF16, 113)
        at("vwin", [128, 2 * 2 * 128], BF16, 126)
        at("vp2", [128, 2 * 128], BF16, 127)
        at("otf", [64, 2 * 128], F32, 127.5)
        at("gtf", [64, 2 * 128], F32, 128.5)
        at("obf", [64, 2 * 128], BF16, 129.5)
        at("rmsc", [64, 8], F32, 130)

        def ofm_tile(i):
            return (t["ofm_a"][:, i * 1024:(i + 1) * 1024] if i < 4
                    else t["ofm_b"][:, (i - 4) * 1024:(i - 3) * 1024])
        self.ofm_tile = ofm_tile

        self.asc = {
            "qp": nc.dram_tensor("qp_st", [8, 128, 1024], BF16),
            "kp": nc.dram_tensor("kp_st", [16, 128, 1024], BF16),
            "vp": nc.dram_tensor("vp_st", [16, 128, 1024], BF16),
            "gg": nc.dram_tensor("gg_st", [8, 128, 1024], BF16),
            "va": nc.dram_tensor("va_st", [8, 128, 2048], BF16),
            "vw": nc.dram_tensor("vw_st", [NAC * HA, 128, 256], BF16),
        }

        self.a_park_and_proj()
        if self.stop == "proj":
            return
        self.a_halo()
        self.a_conv()
        self.tap("aq", lambda: t["aq"][:], [128, 8 * 1024], BF16)
        self.tap("ak", lambda: t["ak"][:], [128, 8 * 2048], BF16)
        if self.stop == "conv":
            return
        self.a_pads()
        self.tap("Lp", lambda: t["Lp"][0:8, :], [8, 2080], F32)
        self.tap("rowA", lambda: t["rowA"][0:8, :], [8, 2048], F32)
        self.tap("rowAq", lambda: t["rowAq"][0:8, :], [8, 1024], F32)
        self.tap("colA", lambda: t["colA"][0:8, :], [8, 2048], F32)
        self.tap("toks", lambda: t["toks"][:], [128, 6 * 128], F32)
        if self.stop == "pads":
            return
        self.a_pass1()
        self.tap("FM", lambda: t["FM"][:], [128, 2048], F32)
        if "vwst" in self.debug_taps:
            out = nc.dram_tensor("tap_vwst", [NAC * HA, 128, 256], BF16, kind="ExternalOutput")
            sem = Sem(self.ctx.enter_context(nc.semaphore("tapsem_vwst")))
            with nc.Block() as blk:
                @blk.sync
                def _(s):
                    s.dma_start(out=out.ap(), in_=self.asc["vw"].ap()).then_inc(sem.h, 16)
                    s.wait_ge(sem.h, 16)
        if self.stop == "pass1":
            return
        self.a_relay()
        self.tap("Sin", lambda: t["Sin"][:], [128, 1024], F32)
        if self.stop == "relay":
            return
        self.a_pass2()
        self.tap("ofm_a", lambda: t["ofm_a"][:], [128, 4096], BF16)
        self.tap("ofm_b", lambda: t["ofm_b"][:], [128, 4096], BF16)
        if self.stop == "pass2":
            return
        self.a_wo()
        self.tap("x1", lambda: t["x_res"][:], [128, NCH * TPC], F32)

    def a_park_and_proj(self):
        nc, t = self.nc, self.t
        s_pk, s_ev, s_st, s_act = t["s_pk"], t["s_ev"], t["s_st"], t["s_act"]

        with nc.Block() as blk:
            @blk.scalar
            def _(se):
                for c in range(NCH):
                    se.dma_start(out=self.x_park.ap()[c * 128:(c + 1) * 128, :],
                                 in_=t["x_res"][:, c * TPC:(c + 1) * TPC]).then_inc(s_pk.h, 16)
                se.wait_ge(s_pk.h, s_pk.n + 16 * NCH)
            s_pk.n += 16 * NCH

        def rhs(i, tb):
            return t["xn"][:, i * TPC + tb * 512: i * TPC + tb * 512 + 512]

        def mk_evac(base_slot):
            def evac(v, oc, tb, ps):
                gi = oc * 2 + tb
                v.tensor_copy(t["evsc"][:, (gi % 4) * 512:(gi % 4) * 512 + 512], ps)
                if tb == 1:
                    v.tensor_copy(t["tailpk"][:, (base_slot + oc) * 3:(base_slot + oc) * 3 + 3],
                                  ps[:, 509:512])
            return evac

        def mk_store(stname):
            def store(se, oc, tb, ev_t):
                gi = oc * 2 + tb
                se.wait_ge(s_ev.h, ev_t)
                se.dma_start(out=self.asc[stname].ap()[oc][:, tb * 512:tb * 512 + 512],
                             in_=t["evsc"][:, (gi % 4) * 512:(gi % 4) * 512 + 512]).then_inc(s_st.h, 16)
            return store

        self.proj_lhsT("Wq_a", 8, 128, list(range(NCH)), rhs, mk_evac(0), store_fn=mk_store("qp"))
        self.proj_lhsT("Wk_a", 16, 128, list(range(NCH)), rhs, mk_evac(8), store_fn=mk_store("kp"))
        self.proj_lhsT("Wv_a", 16, 128, list(range(NCH)), rhs, mk_evac(24), store_fn=mk_store("vp"))

        def evac_ba(v, oc, tb, ps):
            pass

        def scalar_ba(sc, oc, tb, mm_t):
            sc.wait_ge(t["s_mm"].h, mm_t)
            ps = t["ps_p0" if tb % 2 == 0 else "ps_p1"]
            sl = slice(tb * 512, tb * 512 + 512)
            sc.activation(t["bpad"][0:16, sl], ps[0:16, 0:512], AF.Sigmoid)
            sc.activation(t["grow"][0:8, sl], ps[32:40, 0:512], AF.Relu,
                          bias=t["dt_bias_c"][:])
            sc.activation(t["rowAq"][0:8, sl], ps[32:40, 0:512], AF.Abs,
                          bias=t["dt_bias_c"][:])
            sc.activation(t["rowAq"][0:8, sl], t["rowAq"][0:8, sl], AF.Exp, scale=-1.0)
            sc.activation(t["rowAq"][0:8, sl], t["rowAq"][0:8, sl], AF.Ln,
                          bias=t["cst"][0:8, 2:3]).then_inc(s_act.h, 1)
        self.proj_lhsT("Wba_w", 1, 40, list(range(NCH)), rhs, evac_ba, scalar_fn=scalar_ba)
        s_act.n += 2

        with nc.Block() as blk:
            n_act = s_act.n

            @blk.vector
            def _(v):
                v.wait_ge(s_act.h, n_act)
                v.tensor_scalar_mul(t["bpad"][:], t["bpad"][:], 2.0)
                v.tensor_add(t["grow"][:], t["grow"][:], t["rowAq"][0:8, :])
                v.tensor_scalar(t["grow"][:], t["grow"][:], t["negexpA"][:], None, ALU.mult)

        self.proj_rhs("Wg_a", 2, "gg")

    def proj_rhs(self, wname, n_og, stname, slabname="ak", evname="evsc", xname="xn"):
        nc, t, d = self.nc, self.t, self.d
        s_w, s_mm, s_ev, s_st = t["s_w"], t["s_mm"], t["s_ev"], t["s_st"]
        w_base, mm_base, ev_base, st_base = s_w.n, s_mm.n, s_ev.n, s_st.n
        slab = t[slabname]

        with nc.Block() as blk:
            @blk.gpsimd
            def _(g):
                for og in range(n_og):
                    if og >= 2:
                        g.wait_ge(s_ev.h, ev_base + (og - 1) * NTC)
                    g.dma_start(out=slab[:, (og % 2) * 8192:(og % 2) * 8192 + 8192],
                                in_=d[wname].ap()[og]).then_inc(s_w.h, 16)
            s_w.n += 16 * n_og

            @blk.tensor
            def _(te):
                for og in range(n_og):
                    te.wait_ge(s_w.h, w_base + 16 * (og + 1))
                    for tc in range(NTC):
                        gi = og * NTC + tc
                        ps = t["ps_p0" if gi % 2 == 0 else "ps_p1"][:, 0:512]
                        if gi >= 2:
                            te.wait_ge(s_ev.h, ev_base + gi - 1)
                        for kc in range(NCH):
                            ins = te.matmul(
                                ps,
                                t[xname][:, kc * TPC + tc * 128: kc * TPC + (tc + 1) * 128],
                                slab[:, (og % 2) * 8192 + kc * 512:(og % 2) * 8192 + (kc + 1) * 512],
                                start=(kc == 0), stop=(kc == NCH - 1))
                        ins.then_inc(s_mm.h, 1)
            s_mm.n += n_og * NTC

            @blk.vector
            def _(v):
                for gi in range(n_og * NTC):
                    v.wait_ge(s_mm.h, mm_base + gi + 1)
                    if gi >= 4:
                        v.wait_ge(s_st.h, st_base + 16 * (gi - 3))
                    ps = t["ps_p0" if gi % 2 == 0 else "ps_p1"][:, 0:512]
                    v.tensor_copy(t[evname][:, (gi % 4) * 512:(gi % 4) * 512 + 512], ps).then_inc(s_ev.h, 1)
            s_ev.n += n_og * NTC

            @blk.sync
            def _(se):
                for og in range(n_og):
                    for tc in range(NTC):
                        gi = og * NTC + tc
                        se.wait_ge(s_ev.h, ev_base + gi + 1)
                        dst = self.asc.get(stname) if hasattr(self, "asc") and stname in getattr(self, "asc", {}) else self.bsc[stname[:-2] if stname.endswith("_b") else stname]
                        se.dma_start(out=dst.ap()[tc][:, og * 512:og * 512 + 512],
                                     in_=t[evname][:, (gi % 4) * 512:(gi % 4) * 512 + 512]).then_inc(s_st.h, 16)
            s_st.n += 16 * n_og * NTC

    def a_halo(self):
        nc, t = self.nc, self.t
        s_st, s_cc, s_ld = t["s_st"], t["s_cc"], t["s_ld"]
        with nc.Block() as blk:
            @blk.sync
            def _(s):
                s.dma_start(out=self.cc["halo_a_in"].ap(), in_=t["tailpk"][:, 0:120]).then_inc(s_st.h, 16)
            s_st.n += 16

            @blk.gpsimd
            def _(g):
                g.wait_ge(s_st.h, s_st.n)
                g.collective_compute(
                    "AllGather", ALU.bypass,
                    replica_groups=[[0, 1, 2, 3], [4, 5, 6, 7]],
                    ins=[self.cc["halo_a_in"].ap()],
                    outs=[self.cc["halo_a_out"].ap()],
                ).then_inc(s_cc.h, 1)
            s_cc.n += 1

        with nc.Block() as blk:
            @blk.sync
            def _(s):
                s.wait_ge(s_cc.h, s_cc.n)
                for j in range(3):
                    s.dma_start(out=t["relst"][:, j * 128:j * 128 + 120],
                                in_=self.cc["halo_a_out"].ap()[j * 128:(j + 1) * 128, :]).then_inc(s_ld.h, 16)
            s_ld.n += 48

            @blk.vector
            def _(v):
                v.memset(t["halo"][:], 0.0)
                v.wait_ge(s_ld.h, s_ld.n)
                for j in range(3):
                    v.engine_nop()
                    v.engine_nop()
                    v.scalar_tensor_tensor(t["halo"][:, 0:120], t["relst"][:, j * 128:j * 128 + 120],
                                           t["rmask"][:, 3 + j:4 + j], t["halo"][:, 0:120],
                                           ALU.mult, ALU.add)

    def a_conv(self):
        nc, t = self.nc, self.t

        def out_slice(kind, h):
            if kind == "q":
                return t["aq"][:, h * 1024:(h + 1) * 1024]
            if kind == "k0":
                return t["ak"][:, h * 2048:h * 2048 + 2048:2]
            if kind == "k1":
                return t["ak"][:, h * 2048 + 1:h * 2048 + 2048:2]
            if kind == "v0":
                return t["vscr"][:, 0:2048:2]
            return t["vscr"][:, 1:2048:2]

        items_a = ([("q", "qp", h, h, h, h) for h in range(8)]
                   + [("k0", "kp", h, h, h, 8 + h) for h in range(8)])
        items_b = ([("k1", "kp", 8 + h, 8 + h, h, 16 + h) for h in range(8)]
                   + sum([[("v0", "vp", h, h, h, 24 + h), ("v1", "vp", 8 + h, 8 + h, h, 32 + h)]
                          for h in range(8)], []))

        self._conv_block(items_a, {"q": ("ps_a", "ps_b"), "k0": ("ps_c", "ps_row")}, "cq_ck")
        with nc.Block() as blk:
            @blk.vector
            def _(v):
                v.tensor_copy(t["rowAq"][0:8, 0:512], t["ps_a"][0:8, 0:512])
                v.tensor_copy(t["rowAq"][0:8, 512:1024], t["ps_b"][0:8, 0:512])
                v.tensor_copy(t["colA"][0:8, 0:1024:2], t["ps_c"][0:8, 0:512])
                v.tensor_copy(t["colA"][0:8, 1024:2048:2], t["ps_row"][0:8, 0:512])
        self._conv_block(items_b, {"k1": ("ps_a", "ps_b")}, "k1_v")
        with nc.Block() as blk:
            @blk.vector
            def _(v):
                v.tensor_copy(t["colA"][0:8, 1:1024:2], t["ps_a"][0:8, 0:512])
                v.tensor_copy(t["colA"][0:8, 1025:2048:2], t["ps_b"][0:8, 0:512])

    def _conv_block(self, items, normbanks, tag):
        nc, t = self.nc, self.t
        s_ld, s_ve, s_act, s_st = t["s_ld"], t["s_ve"], t["s_act"], t["s_st"]
        n_it = len(items)
        ld_b, ve_b, act_b, st_b = s_ld.n, s_ve.n, s_act.n, s_st.n
        act_after = []
        a = act_b
        for (kind, stn, sti, cwc, h, slot) in items:
            a += 1
            if kind in ("q", "k0", "k1"):
                a += 1
            act_after.append(a)
        s_act.n = a
        cwname = {"q": "cq_a", "k0": "ck_a", "k1": "ck_a", "v0": "cv_a", "v1": "cv_a"}

        with nc.Block() as blk:
            @blk.sync
            def _(s):
                for i, (kind, stn, sti, cwc, h, slot) in enumerate(items):
                    if i >= 2:
                        s.wait_ge(s_ve.h, ve_b + (i - 1))
                    s.dma_start(out=t["pscr"][:, (i % 2) * 1024:(i % 2) * 1024 + 1024],
                                in_=self.asc[stn].ap()[sti]).then_inc(s_ld.h, 16)
            s_ld.n += 16 * n_it

            @blk.vector
            def _(v):
                for i, (kind, stn, sti, cwc, h, slot) in enumerate(items):
                    v.wait_ge(s_ld.h, ld_b + 16 * (i + 1))
                    if i >= 2:
                        v.wait_ge(s_act.h, act_after[i - 2])
                    if kind == "v1":
                        pass
                    src = t["pscr"][:, (i % 2) * 1024:(i % 2) * 1024 + 1024]
                    acc = t["cacc"][:, (i % 2) * 1024:(i % 2) * 1024 + 1024]
                    wv = t[cwname[kind]]
                    v.tensor_scalar(acc[:, 3:1024], src[:, 0:1021],
                                    wv[:, cwc * 4:cwc * 4 + 1], None, ALU.mult)
                    for tau in range(1, 4):
                        v.scalar_tensor_tensor(acc[:, 3:1024], src[:, tau:tau + 1021],
                                               wv[:, cwc * 4 + tau:cwc * 4 + tau + 1],
                                               acc[:, 3:1024], ALU.mult, ALU.add)
                    v.tensor_copy(t["c6"][:, 0:3], t["halo"][:, slot * 3:slot * 3 + 3])
                    v.tensor_copy(t["c6"][:, 3:6], src[:, 0:3])
                    v.tensor_scalar(acc[:, 0:3], t["c6"][:, 0:3],
                                    wv[:, cwc * 4:cwc * 4 + 1], None, ALU.mult)
                    for tau in range(1, 4):
                        v.scalar_tensor_tensor(acc[:, 0:3], t["c6"][:, tau:tau + 3],
                                               wv[:, cwc * 4 + tau:cwc * 4 + tau + 1],
                                               acc[:, 0:3], ALU.mult, ALU.add)
                    v.engine_nop()
                    v.engine_nop().then_inc(s_ve.h, 1)
            s_ve.n += n_it

            @blk.scalar
            def _(sc):
                nst = 0
                for i, (kind, stn, sti, cwc, h, slot) in enumerate(items):
                    sc.wait_ge(s_ve.h, ve_b + i + 1)
                    if kind == "v0" and h >= 1:
                        sc.wait_ge(s_st.h, st_b + 16 * h)
                    acc = t["cacc"][:, (i % 2) * 1024:(i % 2) * 1024 + 1024]
                    sc.activation(out_slice_ := out_slice_fn(kind, h, t), acc, AF.Silu).then_inc(s_act.h, 1)
                    if kind in ("q", "k0", "k1"):
                        sc.activation(t["sqs"][:, (i % 2) * 1024:(i % 2) * 1024 + 1024],
                                      out_slice_, AF.Square).then_inc(s_act.h, 1)
                    if kind == "v1":
                        sc.wait_ge(s_act.h, act_after[i])
                        sc.dma_start(out=self.asc["va"].ap()[h], in_=t["vscr"][:]).then_inc(s_st.h, 16)
                        nst += 1
            if any(k == "v1" for k, *_ in items):
                s_st.n += 16 * 8

            @blk.tensor
            def _(te):
                for i, (kind, stn, sti, cwc, h, slot) in enumerate(items):
                    if kind not in normbanks:
                        continue
                    b0, b1 = normbanks[kind]
                    first, last = (h == 0), (h == 7)
                    te.wait_ge(s_act.h, act_after[i])
                    for tb in range(2):
                        te.matmul(t[b0 if tb == 0 else b1][0:8, 0:512],
                                  t["Ecols"][:, 8 * h:8 * h + 8],
                                  t["sqs"][:, (i % 2) * 1024 + tb * 512:(i % 2) * 1024 + tb * 512 + 512],
                                  start=first, stop=last)

    def a_pads(self):
        nc, t = self.nc, self.t
        s_ld, s_ve, s_act, s_mm, s_g = t["s_ld"], t["s_ve"], t["s_act"], t["s_mm"], t["s_g"]
        LN128 = float(np.log(128.0))

        with nc.Block() as blk:
            @blk.sync
            def _(s):
                s.dma_start(out=t["bpadB"][:], in_=t["bpad"][8:16, :]).then_inc(s_ld.h, 16)
            s_ld.n += 16

            @blk.vector
            def _(v):
                v.tensor_copy(t["b2il"][0:8, 0:2048:2], t["bpad"][0:8, :])
                v.wait_ge(s_ld.h, s_ld.n)
                v.tensor_copy(t["b2il"][0:8, 1:2048:2], t["bpadB"][:])
                v.memset(t["rowA"][0:8, :], 0.0)
                v.tensor_copy(t["rowA"][0:8, 0:2048:2], t["grow"][0:8, :])
                v.memset(t["Lp"][0:8, :], 0.0)
                v.tensor_tensor_scan(t["Lp"][0:8, 1:2049], t["rowA"][0:8, 0:2048],
                                     t["rowA"][0:8, 0:2048], 0.0, ALU.add, ALU.bypass)
                v.engine_nop()
                v.engine_nop().then_inc(s_ve.h, 1)
            s_ve.n += 1

        with nc.Block() as blk:
            @blk.vector
            def _(v):
                v.tensor_copy(t["rows2"][0:8, 32:48], t["Lp"][0:8, 128:2049:128])
                v.tensor_scalar_mul(t["rows2"][0:8, 16:32], t["Lp"][0:8, 0:1921:128], -1.0)
                v.engine_nop()
                v.engine_nop()
                v.engine_nop()
                v.tensor_add(t["rows2"][0:8, 48:64], t["rows2"][0:8, 32:48], t["rows2"][0:8, 16:32])
                v.tensor_scalar_max(t["rowAq"][0:8, :], t["rowAq"][0:8, :], 1e-24)
                v.tensor_scalar_max(t["colA"][0:8, :], t["colA"][0:8, :], 1e-24)
                v.engine_nop()
                v.engine_nop().then_inc(s_ve.h, 1)
            s_ve.n += 1

            @blk.scalar
            def _(sc):
                sc.wait_ge(s_ve.h, s_ve.n)
                sc.activation(t["rows2"][0:8, 0:16], t["rows2"][0:8, 48:64], AF.Exp)
                sc.activation(t["grow"][0:8, :], t["rowAq"][0:8, :], AF.Ln)
                sc.activation(t["cacc"][0:8, 0:2048], t["colA"][0:8, :], AF.Ln).then_inc(s_act.h, 1)
            s_act.n += 1

            @blk.gpsimd
            def _(g):
                g.wait_ge(s_act.h, s_act.n)
                g.engine_nop()
                g.engine_nop().then_inc(s_g.h, 1)
            s_g.n += 1

        with nc.Block() as blk:
            @blk.scalar
            def _(sc):
                sc.activation(t["rowA"][0:8, :], t["b2il"][0:8, :], AF.Ln).then_inc(s_act.h, 1)
            s_act.n += 1

            @blk.vector
            def _(v):
                v.scalar_tensor_tensor(t["rowAq"][0:8, :], t["grow"][0:8, :], -0.5,
                                       t["Lp"][0:8, 2:2049:2], ALU.mult, ALU.add)
                v.tensor_scalar_add(t["rowAq"][0:8, :], t["rowAq"][0:8, :], -0.5 * LN128)
                v.scalar_tensor_tensor(t["colA"][0:8, :], t["cacc"][0:8, 0:2048], -0.5,
                                       t["Lp"][0:8, 1:2049], ALU.mult, ALU.subtract)
                v.wait_ge(s_act.h, s_act.n)
                v.tensor_add(t["rowA"][0:8, :], t["rowA"][0:8, :], t["colA"][0:8, :])
                v.engine_nop()
                v.engine_nop()
                v.scalar_tensor_tensor(t["rowA"][0:8, :], t["Lp"][0:8, 1:2049], 2.0,
                                       t["rowA"][0:8, :], ALU.mult, ALU.add)

        with nc.Block() as blk:
            @blk.sync
            def _(s):
                s.dma_start(out=t["rowAqfl"][0:1, 512:640], in_=t["rows2"][0:8, 0:16]).then_inc(s_ld.h, 16)
            s_ld.n += 16

            @blk.tensor
            def _(te):
                te.wait_ge(s_ld.h, s_ld.n)
                te.matmul(t["ps_row"][:, 0:128], t["ones_rowf"][:], t["rowAqfl"][0:1, 512:640],
                          start=True, stop=True).then_inc(s_mm.h, 1)
            s_mm.n += 1

            @blk.vector
            def _(v):
                v.wait_ge(s_mm.h, s_mm.n)
                v.tensor_copy(t["lamCbc"][:], t["ps_row"][:, 0:128])

        with nc.Block() as blk:
            act_b, mm_b, ve_b = s_act.n, s_mm.n, s_ve.n

            @blk.scalar
            def _(sc):
                for c in range(NAC):
                    if c >= 1:
                        sc.wait_ge(s_mm.h, mm_b + 6 * c)
                    sc.activation(t["grow"][0:8, 0:128], t["rowA"][0:8, c * 128:(c + 1) * 128],
                                  AF.Exp, bias=t["rows2"][0:8, 16 + c:17 + c])
                    sc.activation(t["grow"][0:8, 128:256], t["colA"][0:8, c * 128:(c + 1) * 128],
                                  AF.Exp, bias=t["rows2"][0:8, 32 + c:33 + c])
                    sc.activation(t["grow"][0:8, 256:320], t["rowAq"][0:8, c * 64:(c + 1) * 64],
                                  AF.Exp, bias=t["rows2"][0:8, 16 + c:17 + c]).then_inc(s_act.h, 1)
            s_act.n += NAC

            @blk.tensor
            def _(te):
                for c in range(NAC):
                    te.wait_ge(s_act.h, act_b + c + 1)
                    if c >= 1:
                        te.wait_ge(s_ve.h, ve_b + 6 * c)
                    srcs = [
                        (t["rowA"][0:8, c * 128:(c + 1) * 128], "ps_c", False),
                        (t["colA"][0:8, c * 128:(c + 1) * 128], "ps_row", False),
                        (t["b2il"][0:8, c * 128:(c + 1) * 128], "ps_t0", True),
                        (t["grow"][0:8, 0:128], "ps_c", False),
                        (t["grow"][0:8, 128:256], "ps_row", False),
                        (t["grow"][0:8, 256:320], "ps_c", False),
                    ]
                    for qi, (ap, bank, isbf) in enumerate(srcs):
                        ident = t["ident"][0:8, 0:8] if isbf else t["identf"][0:8, 0:8]
                        nrow = 64 if qi == 5 else 128
                        te.wait_ge(s_ve.h, ve_b + 6 * c + max(0, qi - 1))
                        te.transpose(t[bank][0:nrow, 0:8], ap, ident).then_inc(s_mm.h, 1)
            s_mm.n += 6 * NAC

            @blk.vector
            def _(v):
                for c in range(NAC):
                    for qi in range(6):
                        v.wait_ge(s_mm.h, mm_b + 6 * c + qi + 1)
                        bank = ["ps_c", "ps_row", "ps_t0", "ps_c", "ps_row", "ps_c"][qi]
                        nrow = 64 if qi == 5 else 128
                        v.tensor_copy(t["toks"][0:nrow, qi * 128 + c * 8: qi * 128 + c * 8 + 8],
                                      t[bank][0:nrow, 0:8])
                        v.engine_nop()
                        v.engine_nop().then_inc(s_ve.h, 1)
            s_ve.n += 6 * NAC

    def a_pass1(self):
        import os
        NACL = int(os.environ.get("ATTNA_NAC", NAC))
        nc, t = self.nc, self.t
        s_ld, s_ve, s_act, s_mm, s_st, s_g = (t["s_ld"], t["s_ve"], t["s_act"],
                                              t["s_mm"], t["s_st"], t["s_g"])
        NQ = NAC * HA

        with nc.Block() as blk:
            @blk.vector
            def _(v):
                for h in range(HA):
                    v.tensor_copy(t["FM"][:, h * 128:(h + 1) * 128], t["identf"][:])
                    v.tensor_copy(t["FMbf"][:, h * 128:(h + 1) * 128], t["identf"][:])
                v.memset(t["FM"][:, 1024:2048], 0.0)
                v.memset(t["FMbf"][:, 1024:2048], 0.0)

        ev = {}
        cmm = s_mm.n
        cve = s_ve.n
        cact = s_act.n
        cld = s_ld.n
        cst = t["s_x"].n
        cg = s_g.n
        for c in range(NACL):
            cld += 16; ev[(c, "rowfl")] = cld
            cmm += 2; ev[(c, "rbc")] = cmm
            for h in range(HA):
                q = c * HA + h
                cld += 16; ev[(q, "vload")] = cld
                cmm += 1; ev[(q, "ktr")] = cmm
                cve += 1; ev[(q, "ktil")] = cve
                cact += 1; ev[(q, "w2k")] = cact
                cmm += 1; ev[(q, "vtr")] = cmm
                cve += 1; ev[(q, "bv")] = cve
                cmm += 1; ev[(q, "kk")] = cmm
                cve += 1; ev[(q, "arg")] = cve
                cg += 1; ev[(q, "argdone")] = cg
                cact += 1; ev[(q, "exp")] = cact
                cve += 1; ev[(q, "abf")] = cve
                cmm += 1; ev[(q, "r0tr")] = cmm
                cve += 1; ev[(q, "r0")] = cve
                for j in range(1, 7):
                    cmm += 1; ev[(q, f"sq{j}")] = cmm
                    cve += 1; ev[(q, f"sev{j}")] = cve
                    if j < 6:
                        cact += 1; ev[(q, f"rev{j}")] = cact
                cve += 1; ev[(q, "ginit")] = cve
                for j in range(5, 0, -1):
                    cmm += 1; ev[(q, f"g{j}")] = cmm
                    cve += 1; ev[(q, f"gev{j}")] = cve
                cmm += 1; ev[(q, "tfin")] = cmm
                cve += 1; ev[(q, "tev")] = cve
                cmm += 1; ev[(q, "v0")] = cmm
                cve += 1; ev[(q, "v0ev")] = cve
                cmm += 1; ev[(q, "w2t")] = cmm
                cact += 1; ev[(q, "w2tev")] = cact
                cmm += 1; ev[(q, "w2tok")] = cmm
                cve += 1; ev[(q, "w2tokev")] = cve
                cmm += 1; ev[(q, "pct")] = cmm
                cact += 1; ev[(q, "pctev")] = cact
                cmm += 1; ev[(q, "mc")] = cmm
                cmm += 1; ev[(q, "pm")] = cmm
                cve += 1; ev[(q, "mcomb")] = cve
                cmm += 1; ev[(q, "pf")] = cmm
                cve += 1; ev[(q, "fcomb")] = cve
                cst += 16; ev[(q, "vwst")] = cst
        self._p1ev = ev

        def tk(q, name):
            return ev[(q, name)]

        def bank(p):
            return t["ps_a" if p == 0 else "ps_b"]

        def cbank(p):
            return t["ps_c" if p == 0 else "ps_row"]

        def tbank(p, which):
            return t["ps_t0" if which == 0 else "ps_t1"]

        def sol(p, slot):
            return t["sol"][:, (p * 10 + slot) * 128:(p * 10 + slot) * 128 + 128]

        def abf(p, i):
            return t["abf"][:, (p * 2 + i) * 128:(p * 2 + i) * 128 + 128]

        def earg(p, i):
            return t["earg"][:, (p * 2 + i) * 128:(p * 2 + i) * 128 + 128]

        def ktk(p, i):
            return t["ktk"][:, (p * 2 + i) * 128:(p * 2 + i) * 128 + 128]

        def vw(p, i):
            return t["vw"][:, (p * 2 + i) * 128:(p * 2 + i) * 128 + 128]

        def tokcol(qty, c, h, nrow=128):
            return t["toks"][0:nrow, qty * 128 + c * 8 + h: qty * 128 + c * 8 + h + 1]

        def lamcol(c, h):
            return t["lamCbc"][:, h * 16 + c: h * 16 + c + 1]

        with nc.Block() as blk:
            @blk.sync
            def _(s):
                for c in range(NACL):
                    if c >= 2:
                        s.wait_ge(s_mm.h, tk(c - 1, "rbc"))
                    s.dma_start(out=t["rowAfl"][0:1, (c % 2) * 1024:(c % 2) * 1024 + 1024],
                                in_=t["rowA"][0:8, c * 128:(c + 1) * 128]).then_inc(s_ld.h, 16)
                    for h in range(HA):
                        q = c * HA + h
                        if q >= 2:
                            s.wait_ge(s_mm.h, tk(q - 2, "vtr"))
                        s.dma_start(out=t["vin"][:, (q % 2) * 128:(q % 2) * 128 + 128],
                                    in_=self.asc["va"].ap()[h][:, c * 128:(c + 1) * 128]).then_inc(s_ld.h, 16)
            s_ld.n = cld
            t["s_x"].n = cst

            @blk.tensor
            def _(te):
                for c in range(NACL):
                    te.wait_ge(s_ld.h, tk(c, "rowfl"))
                    if c >= 1:
                        te.wait_ge(s_g.h, tk(c * HA - 1, "argdone"))
                    for half in range(2):
                        te.matmul(t["ps_p0" if half == 0 else "ps_p1"][:, 0:512],
                                  t["ones_rowf"][:],
                                  t["rowAfl"][0:1, (c % 2) * 1024 + half * 512:(c % 2) * 1024 + half * 512 + 512],
                                  start=True, stop=True).then_inc(s_mm.h, 1)
                    for h in range(HA):
                        q = c * HA + h
                        p = q % 2
                        if q >= 1:
                            te.wait_ge(s_act.h, tk(q - 1, "w2k"))
                            te.wait_ge(s_ve.h, tk(q - 1, "r0"))
                        te.transpose(tbank(p, 0)[:, 0:128],
                                     t["ak"][:, h * 2048 + c * 128:h * 2048 + (c + 1) * 128],
                                     t["ident"][:]).then_inc(s_mm.h, 1)
                        te.wait_ge(s_ld.h, tk(q, "vload"))
                        if q >= 1:
                            te.wait_ge(s_ve.h, tk(q - 1, "bv"))
                        te.transpose(tbank(p, 1)[:, 0:128],
                                     t["vin"][:, (q % 2) * 128:(q % 2) * 128 + 128],
                                     t["ident"][:]).then_inc(s_mm.h, 1)
                        if q >= 2:
                            te.wait_ge(s_ve.h, tk(q - 2, "fcomb"))
                        te.matmul(bank(p)[:, 0:128],
                                  t["ak"][:, h * 2048 + c * 128:h * 2048 + (c + 1) * 128],
                                  t["ak"][:, h * 2048 + c * 128:h * 2048 + (c + 1) * 128],
                                  start=True, stop=True).then_inc(s_mm.h, 1)
                        te.wait_ge(s_ve.h, tk(q, "abf"))
                        te.wait_ge(s_act.h, tk(q, "w2k"))
                        te.transpose(tbank(p, 0)[:, 0:128], abf(p, 0), t["ident"][:]).then_inc(s_mm.h, 1)
                        for j in range(1, 7):
                            te.wait_ge(s_ve.h, tk(q, "r0" if j == 1 else f"sev{j-1}"))
                            if j > 1:
                                te.wait_ge(s_act.h, tk(q, f"rev{j-1}"))
                            Sprev = abf(p, 0) if j == 1 else sol(p, 8)
                            Rprev = abf(p, 1) if j == 1 else sol(p, j - 2)
                            ins = te.matmul(bank(p)[:, 0:128], Rprev, Sprev, start=True, stop=True)
                            if j < 6:
                                ins = te.matmul(cbank(p)[:, 0:128], Sprev, Rprev, start=True, stop=True)
                            ins.then_inc(s_mm.h, 1)
                        te.wait_ge(s_ve.h, tk(q, "ginit"))
                        for j in range(5, 0, -1):
                            if j < 5:
                                te.wait_ge(s_ve.h, tk(q, f"gev{j+1}"))
                            te.wait_ge(s_act.h, tk(q, f"rev{j}"))
                            te.matmul(bank(p)[:, 0:128], sol(p, j - 1), sol(p, 6),
                                      start=True, stop=True).then_inc(s_mm.h, 1)
                        te.wait_ge(s_ve.h, tk(q, "gev1"))
                        te.matmul(bank(p)[:, 0:128], abf(p, 1), sol(p, 6),
                                  start=True, stop=True).then_inc(s_mm.h, 1)
                        te.wait_ge(s_ve.h, tk(q, "tev"))
                        te.wait_ge(s_act.h, tk(q, "rev5"))
                        if q >= 2:
                            te.wait_ge(t["s_x"].h, tk(q - 2, "vwst"))
                            te.wait_ge(s_ve.h, tk(q - 2, "mcomb"))
                        te.matmul(cbank(p)[:, 0:128], sol(p, 7), t["bvt"][:, p * 128:p * 128 + 128],
                                  start=True, stop=True).then_inc(s_mm.h, 1)
                        te.wait_ge(s_ve.h, tk(q, "v0ev"))
                        te.matmul(cbank(p)[:, 0:128], ktk(p, 1), sol(p, 7),
                                  start=True, stop=True).then_inc(s_mm.h, 1)
                        te.wait_ge(s_act.h, tk(q, "w2tev"))
                        te.matmul(cbank(p)[:, 0:128], sol(p, 7), ktk(p, 1),
                                  start=True, stop=True).then_inc(s_mm.h, 1)
                        te.wait_ge(s_ve.h, tk(q, "w2tokev"))
                        te.matmul(cbank(p)[:, 0:128], abf(p, 0), ktk(p, 0),
                                  start=True, stop=True).then_inc(s_mm.h, 1)
                        te.wait_ge(s_act.h, tk(q, "pctev"))
                        te.matmul(cbank(p)[:, 0:128], ktk(p, 0), vw(p, 0),
                                  start=True, stop=True).then_inc(s_mm.h, 1)
                        te.matmul(bank(p)[:, 0:128], sol(p, 8), t["FMbf"][:, 1024 + h * 128:1024 + (h + 1) * 128],
                                  start=True, stop=True).then_inc(s_mm.h, 1)
                        te.wait_ge(s_ve.h, tk(q, "mcomb"))
                        te.matmul(bank(p)[:, 0:128], sol(p, 8), t["FMbf"][:, h * 128:(h + 1) * 128],
                                  start=True, stop=True).then_inc(s_mm.h, 1)
            s_mm.n = cmm

            @blk.vector
            def _(v):
                for c in range(NACL):
                    for h in range(HA):
                        q = c * HA + h
                        p = q % 2
                        v.wait_ge(s_mm.h, tk(q, "ktr"))
                        v.tensor_scalar(ktk(p, 0), tbank(p, 0)[:, 0:128], tokcol(4, c, h), None, ALU.mult)
                        v.engine_nop()
                        v.engine_nop().then_inc(s_ve.h, 1)
                        v.wait_ge(s_mm.h, tk(q, "vtr"))
                        v.tensor_scalar(t["bvt"][:, p * 128:p * 128 + 128], tbank(p, 1)[:, 0:128],
                                        tokcol(2, c, h), None, ALU.mult)
                        v.engine_nop()
                        v.engine_nop().then_inc(s_ve.h, 1)
                        v.wait_ge(s_mm.h, tk(c, "rbc"))
                        rows = t["ps_p0" if h < 4 else "ps_p1"]
                        v.tensor_add(earg(p, 0), rows[:, (h % 4) * 128:(h % 4) * 128 + 128], t["maskS"][:])
                        v.engine_nop()
                        v.engine_nop().then_inc(s_ve.h, 1)
                        v.wait_ge(s_act.h, tk(q, "exp"))
                        v.wait_ge(s_mm.h, tk(q, "kk"))
                        v.tensor_mul(abf(p, 0), bank(p)[:, 0:128], earg(p, 1))
                        v.engine_nop()
                        v.engine_nop().then_inc(s_ve.h, 1)
                        v.wait_ge(s_mm.h, tk(q, "r0tr"))
                        v.tensor_copy(abf(p, 1), tbank(p, 0)[:, 0:128])
                        v.engine_nop()
                        v.engine_nop().then_inc(s_ve.h, 1)
                        for j in range(1, 7):
                            v.wait_ge(s_mm.h, tk(q, f"sq{j}"))
                            v.tensor_copy(sol(p, 8), bank(p)[:, 0:128])
                            v.engine_nop()
                            v.engine_nop().then_inc(s_ve.h, 1)
                        v.tensor_add(sol(p, 6), sol(p, 8), t["ident"][:])
                        v.engine_nop()
                        v.engine_nop().then_inc(s_ve.h, 1)
                        for j in range(5, 0, -1):
                            v.wait_ge(s_mm.h, tk(q, f"g{j}"))
                            v.tensor_add(sol(p, 6), sol(p, 6), bank(p)[:, 0:128])
                            v.engine_nop()
                            v.engine_nop().then_inc(s_ve.h, 1)
                        v.wait_ge(s_mm.h, tk(q, "tfin"))
                        v.tensor_sub(sol(p, 7), sol(p, 6), bank(p)[:, 0:128])
                        v.engine_nop()
                        v.engine_nop().then_inc(s_ve.h, 1)
                        v.wait_ge(s_mm.h, tk(q, "v0"))
                        v.tensor_copy(vw(p, 0), cbank(p)[:, 0:128])
                        v.engine_nop()
                        v.engine_nop().then_inc(s_ve.h, 1)
                        v.wait_ge(s_mm.h, tk(q, "w2tok"))
                        v.tensor_copy(abf(p, 0), cbank(p)[:, 0:128])
                        v.engine_nop()
                        v.engine_nop().then_inc(s_ve.h, 1)
                        v.wait_ge(s_mm.h, tk(q, "pm"))
                        v.scalar_tensor_tensor(t["FM"][:, 1024 + h * 128:1024 + (h + 1) * 128],
                                               t["FM"][:, 1024 + h * 128:1024 + (h + 1) * 128],
                                               lamcol(c, h), bank(p)[:, 0:128], ALU.mult, ALU.subtract)
                        v.engine_nop()
                        v.engine_nop()
                        v.tensor_add(t["FM"][:, 1024 + h * 128:1024 + (h + 1) * 128],
                                     t["FM"][:, 1024 + h * 128:1024 + (h + 1) * 128],
                                     cbank(p)[:, 0:128])
                        v.engine_nop()
                        v.engine_nop()
                        v.tensor_copy(t["FMbf"][:, 1024 + h * 128:1024 + (h + 1) * 128],
                                      t["FM"][:, 1024 + h * 128:1024 + (h + 1) * 128])
                        v.engine_nop()
                        v.engine_nop().then_inc(s_ve.h, 1)
                        v.wait_ge(s_mm.h, tk(q, "pf"))
                        v.scalar_tensor_tensor(t["FM"][:, h * 128:(h + 1) * 128],
                                               t["FM"][:, h * 128:(h + 1) * 128],
                                               lamcol(c, h), bank(p)[:, 0:128], ALU.mult, ALU.subtract)
                        v.engine_nop()
                        v.engine_nop()
                        v.tensor_copy(t["FMbf"][:, h * 128:(h + 1) * 128], t["FM"][:, h * 128:(h + 1) * 128])
                        v.engine_nop()
                        v.engine_nop().then_inc(s_ve.h, 1)
            s_ve.n = cve

            @blk.scalar
            def _(sc):
                for c in range(NACL):
                    for h in range(HA):
                        q = c * HA + h
                        p = q % 2
                        sc.wait_ge(s_ve.h, tk(q, "ktil"))
                        sc.activation(ktk(p, 1), tbank(p, 0)[:, 0:128], AF.Copy,
                                      scale=tokcol(3, c, h)).then_inc(s_act.h, 1)
                        sc.wait_ge(s_ve.h, tk(q, "arg"))
                        sc.activation(earg(p, 1), earg(p, 0), AF.Exp,
                                      bias=tokcol(1, c, h)).then_inc(s_act.h, 1)
                        for j in range(1, 6):
                            sc.wait_ge(s_mm.h, tk(q, f"sq{j}"))
                            sc.activation(sol(p, j - 1), cbank(p)[:, 0:128], AF.Copy).then_inc(s_act.h, 1)
                        sc.wait_ge(s_mm.h, tk(q, "w2t"))
                        sc.activation(vw(p, 1), cbank(p)[:, 0:128], AF.Copy).then_inc(s_act.h, 1)
                        sc.wait_ge(s_mm.h, tk(q, "pct"))
                        sc.activation(sol(p, 8), cbank(p)[:, 0:128], AF.Copy).then_inc(s_act.h, 1)
            s_act.n = cact

            @blk.gpsimd
            def _(g):
                for c in range(NACL):
                    for h in range(HA):
                        q = c * HA + h
                        g.wait_ge(s_ve.h, tk(q, "arg"))
                        g.engine_nop()
                        g.engine_nop().then_inc(s_g.h, 1)
                        g.wait_ge(s_act.h, tk(q, "w2tev"))
                        g.dma_start(out=self.asc["vw"].ap()[q],
                                    in_=t["vw"][:, (q % 2) * 256:(q % 2) * 256 + 256]).then_inc(t["s_x"].h, 16)
            s_g.n = cg

    def a_relay(self):
        nc, t = self.nc, self.t
        s_mm, s_ve, s_st, s_cc, s_ld = t["s_mm"], t["s_ve"], t["s_st"], t["s_cc"], t["s_ld"]
        with nc.Block() as blk:
            mm_b, ve_b = s_mm.n, s_ve.n

            @blk.tensor
            def _(te):
                for h in range(HA):
                    if h >= 2:
                        te.wait_ge(s_ve.h, ve_b + h - 1)
                    te.transpose(t["ps_c" if h % 2 == 0 else "ps_row"][:, 0:128],
                                 t["FM"][:, h * 128:(h + 1) * 128], t["identf"][:]).then_inc(s_mm.h, 1)
            s_mm.n += HA

            @blk.vector
            def _(v):
                for h in range(HA):
                    v.wait_ge(s_mm.h, mm_b + h + 1)
                    v.tensor_copy(t["relst"][:, h * 128:(h + 1) * 128],
                                  t["ps_c" if h % 2 == 0 else "ps_row"][:, 0:128])
                    v.engine_nop()
                    v.engine_nop().then_inc(s_ve.h, 1)
                    v.tensor_copy(t["relst"][:, 1024 + h * 128:1024 + (h + 1) * 128],
                                  t["FM"][:, 1024 + h * 128:1024 + (h + 1) * 128])
            s_ve.n += HA

        with nc.Block() as blk:
            @blk.sync
            def _(s):
                s.dma_start(out=self.cc["rel_a_in"].ap(), in_=t["relst"][:]).then_inc(s_st.h, 16)
            s_st.n += 16

            @blk.gpsimd
            def _(g):
                g.wait_ge(s_st.h, s_st.n)
                g.collective_compute(
                    "AllGather", ALU.bypass,
                    replica_groups=[[0, 1, 2, 3], [4, 5, 6, 7]],
                    ins=[self.cc["rel_a_in"].ap()],
                    outs=[self.cc["rel_a_out"].ap()],
                ).then_inc(s_cc.h, 1)
            s_cc.n += 1

        with nc.Block() as blk:
            @blk.vector
            def _(v):
                v.memset(t["Sin"][:], 0.0)
                v.memset(t["Sbf"][:], 0.0)
        for j in range(3):
            with nc.Block() as blk:
                ld_b, mm_b, ve_b = s_ld.n, s_mm.n, s_ve.n

                @blk.sync
                def _(s):
                    s.wait_ge(s_cc.h, s_cc.n)
                    s.dma_start(out=t["relst"][:],
                                in_=self.cc["rel_a_out"].ap()[j * 128:(j + 1) * 128, :]).then_inc(s_ld.h, 16)
                s_ld.n += 16

                @blk.tensor
                def _(te):
                    te.wait_ge(s_ld.h, ld_b + 16)
                    for h in range(HA):
                        if h >= 2:
                            te.wait_ge(s_ve.h, ve_b + h - 1)
                        te.matmul(t["ps_c" if h % 2 == 0 else "ps_row"][:, 0:128],
                                  t["relst"][:, h * 128:(h + 1) * 128],
                                  t["Sin"][:, h * 128:(h + 1) * 128],
                                  start=True, stop=True).then_inc(s_mm.h, 1)
                s_mm.n += HA

                @blk.vector
                def _(v):
                    for h in range(HA):
                        v.wait_ge(s_mm.h, mm_b + h + 1)
                        ps = t["ps_c" if h % 2 == 0 else "ps_row"][:, 0:128]
                        dcol = t["earg"][:, 0:128]
                        v.tensor_add(dcol, ps, t["relst"][:, 1024 + h * 128:1024 + (h + 1) * 128])
                        v.engine_nop()
                        v.engine_nop()
                        v.tensor_sub(dcol, dcol, t["Sin"][:, h * 128:(h + 1) * 128])
                        v.engine_nop()
                        v.engine_nop()
                        v.scalar_tensor_tensor(t["Sin"][:, h * 128:(h + 1) * 128], dcol,
                                               t["rmask"][:, j:j + 1],
                                               t["Sin"][:, h * 128:(h + 1) * 128],
                                               ALU.mult, ALU.add)
                        v.engine_nop()
                        v.engine_nop().then_inc(s_ve.h, 1)
                s_ve.n += HA
        with nc.Block() as blk:
            @blk.vector
            def _(v):
                v.tensor_copy(t["Sbf"][:], t["Sin"][:])

    def a_pass2(self):
        nc, t = self.nc, self.t
        s_ld, s_ve, s_act, s_mm, s_g = t["s_ld"], t["s_ve"], t["s_act"], t["s_mm"], t["s_g"]
        NQ = NAC * HA
        ev = {}
        cmm, cve, cact, cld, cg = s_mm.n, s_ve.n, s_act.n, s_ld.n, s_g.n
        for c in range(NAC):
            cld += 16; ev[(c, "qfl")] = cld
            cmm += 1; ev[(c, "qbc")] = cmm
            cld += 16; ev[(c, "gg")] = cld
            for h in range(HA):
                q = c * HA + h
                cld += 16; ev[(q, "vwin")] = cld
                cmm += 1; ev[(q, "w2s")] = cmm
                cve += 1; ev[(q, "vp")] = cve
                cmm += 1; ev[(q, "kq")] = cmm
                cve += 1; ev[(q, "qarg")] = cve
                cg += 1; ev[(q, "qargdone")] = cg
                cact += 1; ev[(q, "qexp")] = cact
                cve += 1; ev[(q, "aqbf")] = cve
                cmm += 1; ev[(q, "o12")] = cmm
                cve += 1; ev[(q, "ocmb")] = cve
                cact += 1; ev[(q, "osq")] = cact
                cve += 1; ev[(q, "ossum")] = cve
                cact += 1; ev[(q, "oexp")] = cact
                cact += 1; ev[(q, "gexp")] = cact
                cve += 1; ev[(q, "obf")] = cve
                cmm += 1; ev[(q, "otr")] = cmm
                cve += 1; ev[(q, "oev")] = cve
                cmm += 1; ev[(q, "ktr")] = cmm
                cve += 1; ev[(q, "ktil")] = cve
                cmm += 1; ev[(q, "smm")] = cmm
                cve += 1; ev[(q, "supd")] = cve
        self._p2ev = ev

        def tk(a, name):
            return ev[(a, name)]

        def bank(p):
            return t["ps_a" if p == 0 else "ps_b"]

        def obank(p):
            return t["ps_c" if p == 0 else "ps_row"]

        def tbank(p):
            return t["ps_t0" if p == 0 else "ps_t1"]

        def tokcol(qty, c, h, nrow=128):
            return t["toks"][0:nrow, qty * 128 + c * 8 + h: qty * 128 + c * 8 + h + 1]

        def lamcol(c, h):
            return t["lamCbc"][:, h * 16 + c: h * 16 + c + 1]

        def earg(p, i):
            return t["earg"][:, (p * 2 + i) * 128:(p * 2 + i) * 128 + 128]

        def abf(p, i):
            return t["abf"][:, (p * 2 + i) * 128:(p * 2 + i) * 128 + 128]

        def ktk(p, i):
            return t["ktk"][:, (p * 2 + i) * 128:(p * 2 + i) * 128 + 128]

        with nc.Block() as blk:
            @blk.sync
            def _(s):
                for c in range(NAC):
                    if c >= 2:
                        s.wait_ge(s_mm.h, tk(c - 1, "qbc"))
                    s.dma_start(out=t["rowAqfl"][0:1, (c % 2) * 512:(c % 2) * 512 + 512],
                                in_=t["rowAq"][0:8, c * 64:(c + 1) * 64]).then_inc(s_ld.h, 16)
                    if c >= 2:
                        s.wait_ge(s_ve.h, tk((c - 1) * HA - 1, "obf"))
                    s.dma_start(out=t["ggb"][0:64, (c % 2) * 1024:(c % 2) * 1024 + 1024],
                                in_=self.asc["gg"].ap()[c // 2][(c % 2) * 64:(c % 2) * 64 + 64, :]
                                ).then_inc(s_ld.h, 16)
                    for h in range(HA):
                        q = c * HA + h
                        if q >= 2:
                            s.wait_ge(s_ve.h, tk(q - 2, "vp"))
                        s.dma_start(out=t["vwin"][:, (q % 2) * 256:(q % 2) * 256 + 256],
                                    in_=self.asc["vw"].ap()[q]).then_inc(s_ld.h, 16)
            s_ld.n = cld

            @blk.tensor
            def _(te):
                for c in range(NAC):
                    te.wait_ge(s_ld.h, tk(c, "qfl"))
                    if c >= 1:
                        te.wait_ge(s_g.h, tk(c * HA - 1, "qargdone"))
                    te.matmul(t["ps_p0"][:, 0:512], t["ones_rowf"][:],
                              t["rowAqfl"][0:1, (c % 2) * 512:(c % 2) * 512 + 512],
                              start=True, stop=True).then_inc(s_mm.h, 1)
                    for h in range(HA):
                        q = c * HA + h
                        p = q % 2
                        ksl = t["ak"][:, h * 2048 + c * 128:h * 2048 + (c + 1) * 128]
                        qsl = t["aq"][:, h * 1024 + c * 64:h * 1024 + (c + 1) * 64]
                        te.wait_ge(s_ld.h, tk(q, "vwin"))
                        if q >= 2:
                            te.wait_ge(s_ve.h, tk(q - 2, "supd"))
                        te.matmul(bank(p)[:, 0:128],
                                  t["vwin"][:, (q % 2) * 256 + 128:(q % 2) * 256 + 256],
                                  t["Sbf"][:, h * 128:(h + 1) * 128],
                                  start=True, stop=True).then_inc(s_mm.h, 1)
                        te.wait_ge(s_ve.h, tk(q, "vp"))
                        if q >= 2:
                            te.wait_ge(s_ve.h, tk(q - 2, "aqbf"))
                        te.matmul(bank(p)[:, 128:192], ksl, qsl, start=True, stop=True).then_inc(s_mm.h, 1)
                        te.wait_ge(s_ve.h, tk(q, "aqbf"))
                        if q >= 2:
                            te.wait_ge(s_ve.h, tk(q - 2, "ocmb"))
                        te.matmul(obank(p)[0:64, 0:128], abf(p, 0)[:, 0:64],
                                  t["vp2"][:, p * 128:p * 128 + 128], start=True, stop=True)
                        te.matmul(obank(p)[0:64, 128:256], qsl, t["Sbf"][:, h * 128:(h + 1) * 128],
                                  start=True, stop=True).then_inc(s_mm.h, 1)
                        te.wait_ge(s_ve.h, tk(q, "obf"))
                        if q >= 2:
                            te.wait_ge(s_ve.h, tk(q - 2, "oev"))
                        te.transpose(tbank(p)[0:128, 0:64], t["obf"][0:64, p * 128:p * 128 + 128],
                                     t["ident"][0:64, 0:64]).then_inc(s_mm.h, 1)
                        if q >= 2:
                            te.wait_ge(s_ve.h, tk(q - 2, "ktil"))
                        te.wait_ge(s_ve.h, tk(q, "oev"))
                        te.transpose(tbank(p)[0:128, 64:192], ksl, t["ident"][:]).then_inc(s_mm.h, 1)
                        te.wait_ge(s_ve.h, tk(q, "ktil"))
                        te.matmul(bank(p)[:, 256:384], ktk(p, 0), t["vp2"][:, p * 128:p * 128 + 128],
                                  start=True, stop=True).then_inc(s_mm.h, 1)
            s_mm.n = cmm

            @blk.vector
            def _(v):
                for c in range(NAC):
                    for h in range(HA):
                        q = c * HA + h
                        p = q % 2
                        v.wait_ge(s_mm.h, tk(q, "w2s"))
                        v.tensor_sub(t["vp2"][:, p * 128:p * 128 + 128],
                                     t["vwin"][:, (q % 2) * 256:(q % 2) * 256 + 128],
                                     bank(p)[:, 0:128])
                        v.engine_nop()
                        v.engine_nop().then_inc(s_ve.h, 1)
                        v.wait_ge(s_mm.h, tk(c, "qbc"))
                        v.tensor_add(earg(p, 0)[:, 0:64], t["ps_p0"][:, h * 64:(h + 1) * 64], t["maskQ"][:, 0:64])
                        v.engine_nop()
                        v.engine_nop().then_inc(s_ve.h, 1)
                        v.wait_ge(s_act.h, tk(q, "qexp"))
                        v.wait_ge(s_mm.h, tk(q, "kq"))
                        v.tensor_mul(abf(p, 0)[:, 0:64], bank(p)[:, 128:192], earg(p, 1)[:, 0:64])
                        v.engine_nop()
                        v.engine_nop().then_inc(s_ve.h, 1)
                        v.wait_ge(s_mm.h, tk(q, "o12"))
                        v.tensor_copy(t["otf"][0:64, p * 128:p * 128 + 128], obank(p)[0:64, 0:128])
                        v.engine_nop()
                        v.engine_nop()
                        v.scalar_tensor_tensor(t["otf"][0:64, p * 128:p * 128 + 128],
                                               obank(p)[0:64, 128:256], tokcol(5, c, h, 64),
                                               t["otf"][0:64, p * 128:p * 128 + 128], ALU.mult, ALU.add)
                        v.engine_nop()
                        v.engine_nop().then_inc(s_ve.h, 1)
                        v.wait_ge(s_act.h, tk(q, "osq"))
                        v.reduce_sum(t["rmsc"][0:64, p * 4:p * 4 + 1], t["gtf"][0:64, p * 128:p * 128 + 128],
                                     axis=mybir.AxisListType.X)
                        v.engine_nop()
                        v.engine_nop().then_inc(s_ve.h, 1)
                        v.wait_ge(s_act.h, tk(q, "gexp"))
                        ggsl = t["ggb"][0:64, (c % 2) * 1024 + h * 128:(c % 2) * 1024 + (h + 1) * 128]
                        gsl = t["gtf"][0:64, p * 128:p * 128 + 128]
                        osl = t["otf"][0:64, p * 128:p * 128 + 128]
                        v.tensor_scalar_add(gsl, gsl, 1.0)
                        v.engine_nop()
                        v.engine_nop()
                        v.reciprocal(gsl, gsl)
                        v.engine_nop()
                        v.engine_nop()
                        v.tensor_mul(gsl, gsl, ggsl)
                        v.engine_nop()
                        v.engine_nop()
                        v.tensor_mul(gsl, gsl, t["Wonw_bc"][0:64, :])
                        v.tensor_scalar(osl, osl, t["rmsc"][0:64, p * 4 + 1:p * 4 + 2], None, ALU.mult)
                        v.engine_nop()
                        v.engine_nop()
                        v.tensor_mul(t["obf"][0:64, p * 128:p * 128 + 128], osl, gsl)
                        v.engine_nop()
                        v.engine_nop().then_inc(s_ve.h, 1)
                        v.wait_ge(s_mm.h, tk(q, "otr"))
                        v.tensor_copy(self.ofm_tile(h)[:, c * 64:(c + 1) * 64], tbank(p)[0:128, 0:64])
                        v.engine_nop()
                        v.engine_nop().then_inc(s_ve.h, 1)
                        v.wait_ge(s_mm.h, tk(q, "ktr"))
                        v.tensor_scalar(ktk(p, 0), tbank(p)[0:128, 64:192], tokcol(4, c, h), None, ALU.mult)
                        v.engine_nop()
                        v.engine_nop().then_inc(s_ve.h, 1)
                        v.wait_ge(s_mm.h, tk(q, "smm"))
                        v.scalar_tensor_tensor(t["Sin"][:, h * 128:(h + 1) * 128],
                                               t["Sin"][:, h * 128:(h + 1) * 128], lamcol(c, h),
                                               bank(p)[:, 256:384], ALU.mult, ALU.add)
                        v.engine_nop()
                        v.engine_nop()
                        v.tensor_copy(t["Sbf"][:, h * 128:(h + 1) * 128], t["Sin"][:, h * 128:(h + 1) * 128])
                        v.engine_nop()
                        v.engine_nop().then_inc(s_ve.h, 1)
            s_ve.n = cve

            @blk.scalar
            def _(sc):
                for c in range(NAC):
                    for h in range(HA):
                        q = c * HA + h
                        p = q % 2
                        sc.wait_ge(s_ve.h, tk(q, "qarg"))
                        sc.activation(earg(p, 1)[:, 0:64], earg(p, 0)[:, 0:64], AF.Exp,
                                      bias=tokcol(1, c, h)).then_inc(s_act.h, 1)
                        sc.wait_ge(s_ve.h, tk(q, "ocmb"))
                        sc.activation(t["gtf"][0:64, p * 128:p * 128 + 128],
                                      t["otf"][0:64, p * 128:p * 128 + 128],
                                      AF.Square).then_inc(s_act.h, 1)
                        sc.wait_ge(s_ve.h, tk(q, "ossum"))
                        sc.activation(t["rmsc"][0:64, p * 4 + 2:p * 4 + 3],
                                      t["rmsc"][0:64, p * 4:p * 4 + 1], AF.Ln,
                                      bias=t["cst"][0:64, 0:1], scale=t["cst"][0:64, 1:2])
                        sc.activation(t["rmsc"][0:64, p * 4 + 1:p * 4 + 2],
                                      t["rmsc"][0:64, p * 4 + 2:p * 4 + 3], AF.Exp,
                                      scale=-0.5).then_inc(s_act.h, 1)
                        sc.wait_ge(s_ld.h, tk(c, "gg"))
                        sc.activation(t["gtf"][0:64, p * 128:p * 128 + 128],
                                      t["ggb"][0:64, (c % 2) * 1024 + h * 128:(c % 2) * 1024 + (h + 1) * 128],
                                      AF.Exp, scale=-1.0).then_inc(s_act.h, 1)
            s_act.n = cact

            @blk.gpsimd
            def _(g):
                for c in range(NAC):
                    for h in range(HA):
                        q = c * HA + h
                        g.wait_ge(s_ve.h, tk(q, "qarg"))
                        g.engine_nop()
                        g.engine_nop().then_inc(s_g.h, 1)
            s_g.n = cg

    def a_wo(self):
        nc, t = self.nc, self.t
        s_ld = t["s_ld"]
        with nc.Block() as blk:
            @blk.sync
            def _(s):
                for c in range(NCH):
                    s.dma_start(out=t["x_res"][:, c * TPC:(c + 1) * TPC],
                                in_=self.x_park.ap()[c * 128:(c + 1) * 128, :]).then_inc(s_ld.h, 16)
                s.wait_ge(s_ld.h, s_ld.n + 16 * NCH)
            s_ld.n += 16 * NCH

        def rhs(i, tb):
            return self.ofm_tile(i)[:, tb * 512:tb * 512 + 512]

        def evac(v, oc, tb, ps):
            xsl = t["x_res"][:, oc * TPC + tb * 512: oc * TPC + tb * 512 + 512]
            v.tensor_add(xsl, xsl, ps)
            v.tensor_copy(t["x_bf"][:, oc * TPC + tb * 512: oc * TPC + tb * 512 + 512], xsl)

        self.proj_lhsT("Wo_a", 16, 128, list(range(8)), rhs, evac)

    def attn_b(self):
        nc, t, d = self.nc, self.t, self.d
        K = 1024

        def at(name, shape, dtype, offk):
            off = int(offk * K)
            nbytes = int(np.prod(shape[1:])) * (2 if dtype == BF16 else 4)
            assert off + nbytes <= self.asize, (name, off, nbytes, self.asize)
            h = self.nc.alloc_sbuf_tensor_at(name, shape, dtype, offset=self.abase + off)
            self.t[name] = h
            return h

        at("qb", [128, 16 * 1024], BF16, 0)
        at("kb", [128, 16 * 1024], BF16, 32)
        at("evscb", [128, 4 * 512], BF16, 32)
        at("Mb", [128, 16 * 256], F32, 64)
        at("Sbb", [128, 16 * 256], BF16, 80)
        at("vbin", [128, 2 * 4096], BF16, 88)
        at("xpin", [128, 2 * 512], F32, 88)
        at("xout", [128, 2 * 512], F32, 92)
        at("ofmb", [128, 32 * 512], BF16, 104)
        at("convwb", [128, 2 * 4 * 2048], BF16, 104)
        at("relbuf", [128, 4096], F32, 104)
        at("eargb", [128, 2 * 128], F32, 136)
        at("abfb", [128, 2 * 128], BF16, 137)
        at("ktkb", [128, 2 * 128], BF16, 137.5)
        at("otfb", [128, 2 * 256], F32, 138)
        at("obfb", [128, 2 * 256], BF16, 140)
        at("Lbp", [16, 1056], F32, 141)
        at("toksb", [128, 4 * 128], F32, 145.25)
        at("lamCbb", [128, 128], F32, 147.25)
        at("rows2b", [16, 32], F32, 147.75)
        at("bfl2", [1, 1024], F32, 148)
        at("lbfl", [1, 128], F32, 152)
        at("pscrb", [128, 2 * 1056], BF16, 97)
        at("gateb", [16, 1024], F32, 64)
        at("halob", [128, 128], F32, 101.25)
        at("tailb", [128, 96], F32, 36)

        self.bsc = {
            "vb": nc.dram_tensor("vb_st", [NBC, 128, 4096], BF16),
            "qpb": nc.dram_tensor("qpb_st", [16, 128, 1024], BF16),
            "kpb": nc.dram_tensor("kpb_st", [16, 128, 1024], BF16),
            "xp2": nc.dram_tensor("xp2_st", [D, TPC], F32),
        }

        self.b_park_and_proj()
        self.b_halo()
        self.b_conv()
        self.tap("qbt", lambda: t["qb"][:], [128, 16 * 1024], BF16)
        self.tap("kbt", lambda: t["kb"][:], [128, 16 * 1024], BF16)
        self.b_pads()
        self.tap("Lbp", lambda: t["Lbp"][0:16, :], [16, 1056], F32)
        self.tap("toksb", lambda: t["toksb"][:], [128, 512], F32)
        self.tap("lamCbb", lambda: t["lamCbb"][:], [128, 128], F32)
        self.tap("rows2b", lambda: t["rows2b"][0:16, :], [16, 32], F32)
        self.b_pass1()
        self.tap("Mb", lambda: t["Mb"][:], [128, 4096], F32)
        self.b_relay()
        self.tap("Sinb", lambda: t["Mb"][:], [128, 4096], F32)
        self.b_pass2()
        self.b_restore()

    def b_park_and_proj(self):
        nc, t, d = self.nc, self.t, self.d
        s_pk, s_ev, s_st, s_act, s_mm = t["s_pk"], t["s_ev"], t["s_st"], t["s_act"], t["s_mm"]

        with nc.Block() as blk:
            @blk.scalar
            def _(se):
                for c in range(NCH):
                    se.dma_start(out=self.x_park.ap()[c * 128:(c + 1) * 128, :],
                                 in_=t["x_res"][:, c * TPC:(c + 1) * TPC]).then_inc(s_pk.h, 16)
                se.wait_ge(s_pk.h, s_pk.n + 16 * NCH)
            s_pk.n += 16 * NCH

        self.proj_rhs("Wv_b", 8, "vb_b", slabname="qb", evname="evscb")

        def rhs(i, tb):
            return t["xn"][:, i * TPC + tb * 512: i * TPC + tb * 512 + 512]

        def mk_evac(base_slot):
            def evac(v, oc, tb, ps):
                gi = oc * 2 + tb
                v.tensor_copy(t["evscb"][:, (gi % 4) * 512:(gi % 4) * 512 + 512], ps)
                if tb == 1:
                    v.tensor_copy(t["tailb"][:, (base_slot + oc) * 3:(base_slot + oc) * 3 + 3],
                                  ps[:, 509:512])
            return evac

        def mk_store(stname):
            def store(se, oc, tb, ev_t):
                gi = oc * 2 + tb
                se.wait_ge(s_ev.h, ev_t)
                se.dma_start(out=self.bsc[stname].ap()[oc][:, tb * 512:tb * 512 + 512],
                             in_=t["evscb"][:, (gi % 4) * 512:(gi % 4) * 512 + 512]).then_inc(s_st.h, 16)
            return store

        self.proj_lhsT("Wq_b", 16, 128, list(range(NCH)), rhs, mk_evac(0), store_fn=mk_store("qpb"))
        self.proj_lhsT("Wk_b", 16, 128, list(range(NCH)), rhs, mk_evac(16), store_fn=mk_store("kpb"))

        def evac_g(v, oc, tb, ps):
            pass

        def scalar_g(sc, oc, tb, mm_t):
            sc.wait_ge(t["s_mm"].h, mm_t)
            ps = t["ps_p0" if tb % 2 == 0 else "ps_p1"]
            sl = slice(tb * 512, tb * 512 + 512)
            sc.activation(t["gateb"][0:16, sl], ps[0:16, 0:512], AF.Sigmoid,
                          bias=t["gbcol"][:])
            sc.activation(t["Lbp"][0:16, 1 + tb * 512:1 + tb * 512 + 512], ps[32:48, 0:512],
                          AF.Relu, bias=t["bbncol"][:], scale=-1.0)
            sc.activation(t["lnscr"][0:16, sl], ps[32:48, 0:512], AF.Abs, bias=t["bbcol"][:])
            sc.activation(t["lnscr"][0:16, sl], t["lnscr"][0:16, sl], AF.Exp, scale=-1.0)
            sc.activation(t["lnscr"][0:16, sl], t["lnscr"][0:16, sl], AF.Ln,
                          bias=t["cst"][0:16, 2:3]).then_inc(s_act.h, 1)

        self.t["gbcol"] = self.nc.alloc_sbuf_tensor_at("gbcol", [16, 1], F32, offset=self.abase + int(147.875 * 1024))
        self.t["bbcol"] = self.nc.alloc_sbuf_tensor_at("bbcol", [16, 1], F32, offset=self.abase + int(147.90625 * 1024))
        self.t["bbncol"] = self.nc.alloc_sbuf_tensor_at("bbncol", [16, 1], F32, offset=self.abase + int(147.9375 * 1024))
        self.t["lnscr"] = self.nc.alloc_sbuf_tensor_at("lnscr", [16, 1024], BF16 if False else F32,
                                                       offset=self.abase + int(88 * 1024))
        with nc.Block() as blk:
            s_ld = t["s_ld"]

            @blk.sync
            def _(s):
                s.dma_start(out=t["gbcol"][:], in_=d["bg_b"].ap().rearrange("o h -> h o")).then_inc(s_ld.h, 16)
                s.dma_start(out=t["bbcol"][:], in_=d["bb_b"].ap().rearrange("o h -> h o")).then_inc(s_ld.h, 16)
                s.wait_ge(s_ld.h, s_ld.n + 32)
            s_ld.n += 32

            @blk.vector
            def _(v):
                v.wait_ge(s_ld.h, s_ld.n)
                v.tensor_scalar_mul(t["bbncol"][:], t["bbcol"][:], -1.0)

        self.proj_lhsT("Wgb_w", 1, 48, list(range(NCH)), rhs, evac_g, scalar_fn=scalar_g)
        s_act.n += 2

        with nc.Block() as blk:
            s_ve = t["s_ve"]
            n_act = s_act.n

            @blk.vector
            def _(v):
                v.wait_ge(s_act.h, n_act)
                v.tensor_add(t["Lbp"][0:16, 1:1025], t["Lbp"][0:16, 1:1025], t["lnscr"][0:16, :])
                v.engine_nop()
                v.engine_nop()
                v.tensor_scalar_mul(t["Lbp"][0:16, 1:1025], t["Lbp"][0:16, 1:1025], -1.0)
                v.memset(t["Lbp"][0:16, 0:1], 0.0)
                v.memset(t["Lbp"][0:16, 1025:1056], 0.0)
                v.engine_nop()
                v.engine_nop()
                v.tensor_tensor_scan(t["Lbp"][0:16, 1:1025], t["Lbp"][0:16, 1:1025],
                                     t["Lbp"][0:16, 1:1025], 0.0, ALU.add, ALU.bypass)

    def b_halo(self):
        nc, t = self.nc, self.t
        s_st, s_cc, s_ld = t["s_st"], t["s_cc"], t["s_ld"]
        with nc.Block() as blk:
            @blk.sync
            def _(s):
                s.dma_start(out=self.cc["halo_b_in"].ap(), in_=t["tailb"][:, 0:96]).then_inc(s_st.h, 16)
            s_st.n += 16

            @blk.gpsimd
            def _(g):
                g.wait_ge(s_st.h, s_st.n)
                g.collective_compute(
                    "AllGather", ALU.bypass,
                    replica_groups=[[0, 1, 2, 3], [4, 5, 6, 7]],
                    ins=[self.cc["halo_b_in"].ap()],
                    outs=[self.cc["halo_b_out"].ap()],
                ).then_inc(s_cc.h, 1)
            s_cc.n += 1

        with nc.Block() as blk:
            @blk.sync
            def _(s):
                s.wait_ge(s_cc.h, s_cc.n)
                for j in range(3):
                    s.dma_start(out=t["otfb"][:, j * 128:j * 128 + 96],
                                in_=self.cc["halo_b_out"].ap()[j * 128:(j + 1) * 128, :]).then_inc(s_ld.h, 16)
            s_ld.n += 48

            @blk.vector
            def _(v):
                v.memset(t["halob"][:], 0.0)
                v.wait_ge(s_ld.h, s_ld.n)
                for j in range(3):
                    v.engine_nop()
                    v.engine_nop()
                    v.scalar_tensor_tensor(t["halob"][:, 0:96],
                                           t["otfb"][:, j * 128:j * 128 + 96],
                                           t["rmask"][:, 3 + j:4 + j], t["halob"][:, 0:96],
                                           ALU.mult, ALU.add)

    def b_conv(self):
        nc, t, d = self.nc, self.t, self.d
        s_ld, s_ve, s_act, s_mm, s_w = t["s_ld"], t["s_ve"], t["s_act"], t["s_mm"], t["s_w"]
        w_base = s_w.n
        with nc.Block() as blk:
            @blk.gpsimd
            def _(g):
                g.dma_start(out=t["convwb"][:, 0:8192],
                            in_=d["convW_q_b"].ap().rearrange("t i o -> i t o")).then_inc(s_w.h, 16)
                g.dma_start(out=t["convwb"][:, 8192:16384],
                            in_=d["convW_k_b"].ap().rearrange("t i o -> i t o")).then_inc(s_w.h, 16)
            s_w.n += 32

        items = [("q", g) for g in range(16)] + [("k", g) for g in range(16)]
        ld_b, ve_b, act_b, mm_b = s_ld.n, s_ve.n, s_act.n, s_mm.n
        with nc.Block() as blk:
            @blk.sync
            def _(s):
                for i, (qk, g_) in enumerate(items):
                    if i >= 2:
                        s.wait_ge(s_mm.h, mm_b + (i - 1) * 2)
                    st = self.bsc["qpb" if qk == "q" else "kpb"]
                    s.dma_start(out=t["pscrb"][:, (i % 2) * 1056 + 3:(i % 2) * 1056 + 3 + 1024],
                                in_=st.ap()[g_]).then_inc(s_ld.h, 16)
            s_ld.n += 16 * len(items)

            @blk.vector
            def _(v):
                for i, (qk, g_) in enumerate(items):
                    v.wait_ge(s_ld.h, ld_b + 16 * (i + 1))
                    if i >= 2:
                        v.wait_ge(s_mm.h, mm_b + (i - 1) * 2)
                    slot = (i % 2) * 1056
                    hs = (g_ if qk == "q" else 16 + g_) * 3
                    v.tensor_copy(t["pscrb"][:, slot:slot + 3], t["halob"][:, hs:hs + 3])
                    v.engine_nop()
                    v.engine_nop().then_inc(s_ve.h, 1)
            s_ve.n += len(items)

            @blk.tensor
            def _(te):
                te.wait_ge(s_w.h, w_base + 32)
                for i, (qk, g_) in enumerate(items):
                    te.wait_ge(s_ve.h, ve_b + i + 1)
                    if i >= 1:
                        te.wait_ge(s_act.h, act_b + (i - 1) * 2 + 2)
                    slot = (i % 2) * 1056
                    woff = (0 if qk == "q" else 8192) + g_ * 128
                    for tb in range(2):
                        ps = t["ps_p0" if tb == 0 else "ps_p1"][:, 0:512]
                        for tau in range(4):
                            ins = te.matmul(ps,
                                            t["convwb"][:, woff + tau * 2048:woff + tau * 2048 + 128],
                                            t["pscrb"][:, slot + tb * 512 + tau:slot + tb * 512 + tau + 512],
                                            start=(tau == 0), stop=(tau == 3))
                        ins.then_inc(s_mm.h, 1)
            s_mm.n += 2 * len(items)

            @blk.scalar
            def _(sc):
                for i, (qk, g_) in enumerate(items):
                    dst = t["qb" if qk == "q" else "kb"]
                    bcol = t["bqk_conv_b"][:, (0 if qk == "q" else 16) + g_:(0 if qk == "q" else 16) + g_ + 1]
                    for tb in range(2):
                        sc.wait_ge(s_mm.h, mm_b + i * 2 + tb + 1)
                        ps = t["ps_p0" if tb == 0 else "ps_p1"][:, 0:512]
                        sc.activation(dst[:, g_ * 1024 + tb * 512:g_ * 1024 + tb * 512 + 512],
                                      ps, AF.Silu, bias=bcol).then_inc(s_act.h, 1)
            s_act.n += 2 * len(items)


    def b_pads(self):
        nc, t = self.nc, self.t
        s_ld, s_ve, s_act, s_mm = t["s_ld"], t["s_ve"], t["s_act"], t["s_mm"]
        with nc.Block() as blk:
            @blk.vector
            def _(v):
                v.tensor_scalar_mul(t["rows2b"][0:16, 0:8], t["Lbp"][0:16, 0:1024:128], -1.0)
                v.tensor_copy(t["rows2b"][0:16, 8:16], t["Lbp"][0:16, 128:1025:128])
                v.engine_nop()
                v.engine_nop()
                v.engine_nop()
                v.tensor_add(t["rows2b"][0:16, 16:24], t["rows2b"][0:16, 8:16], t["rows2b"][0:16, 0:8])
                v.engine_nop()
                v.engine_nop().then_inc(t["s_ve"].h, 1)
            s_ve.n += 1

            @blk.scalar
            def _(sc):
                sc.wait_ge(s_ve.h, s_ve.n)
                sc.activation(t["rows2b"][0:16, 16:24], t["rows2b"][0:16, 16:24], AF.Exp).then_inc(s_act.h, 1)
            s_act.n += 1

        with nc.Block() as blk:
            @blk.sync
            def _(s):
                s.wait_ge(s_act.h, s_act.n)
                s.dma_start(out=t["lbfl"][0:1, :], in_=t["rows2b"][0:16, 16:24]).then_inc(s_ld.h, 16)
            s_ld.n += 16

            @blk.tensor
            def _(te):
                te.wait_ge(s_ld.h, s_ld.n)
                te.matmul(t["ps_row"][:, 0:128], t["ones_rowf"][:], t["lbfl"][0:1, :],
                          start=True, stop=True).then_inc(s_mm.h, 1)
            s_mm.n += 1

            @blk.vector
            def _(v):
                v.wait_ge(s_mm.h, s_mm.n)
                v.tensor_copy(t["lamCbb"][:], t["ps_row"][:, 0:128])

        with nc.Block() as blk:
            act_b, mm_b, ve_b = s_act.n, s_mm.n, s_ve.n

            @blk.scalar
            def _(sc):
                for c in range(NBC):
                    if c >= 1:
                        sc.wait_ge(s_mm.h, mm_b + 4 * c)
                    sc.activation(t["lnscr"][0:16, 0:128], t["Lbp"][0:16, 1 + c * 128:1 + (c + 1) * 128],
                                  AF.Exp, bias=t["rows2b"][0:16, c:c + 1])
                    sc.activation(t["lnscr"][0:16, 128:256], t["Lbp"][0:16, 1 + c * 128:1 + (c + 1) * 128],
                                  AF.Exp, scale=-1.0, bias=t["rows2b"][0:16, 8 + c:9 + c]).then_inc(s_act.h, 1)
            s_act.n += NBC

            @blk.vector
            def _(v0):
                for c in range(NBC):
                    v0.wait_ge(s_act.h, act_b + c + 1)
                    v0.tensor_scalar_mul(t["lnscr"][0:16, 256:384],
                                         t["Lbp"][0:16, 1 + c * 128:1 + (c + 1) * 128], -1.0)
                    v0.engine_nop()
                    v0.engine_nop().then_inc(s_ve.h, 1)
                    for qi in range(4):
                        v0.wait_ge(s_mm.h, mm_b + 4 * c + qi + 1)
                        bank = ["ps_c", "ps_row", "ps_c", "ps_row"][qi]
                        v0.tensor_copy(t["toksb"][:, qi * 128 + c * 16: qi * 128 + c * 16 + 16],
                                       t[bank][0:128, 0:16])
                        v0.engine_nop()
                        v0.engine_nop().then_inc(s_ve.h, 1)
            s_ve.n += 5 * NBC

            @blk.tensor
            def _(te):
                for c in range(NBC):
                    te.wait_ge(s_ve.h, ve_b + 5 * c + 1)
                    srcs = [
                        (t["gateb"][0:16, c * 128:(c + 1) * 128], "ps_c"),
                        (t["lnscr"][0:16, 0:128], "ps_row"),
                        (t["lnscr"][0:16, 128:256], "ps_c"),
                        (t["lnscr"][0:16, 256:384], "ps_row"),
                    ]
                    te.wait_ge(s_act.h, act_b + c + 1)
                    for qi, (ap, bank) in enumerate(srcs):
                        te.wait_ge(s_ve.h, ve_b + 5 * c + 1 + max(0, qi - 1))
                        te.transpose(t[bank][0:128, 0:16], ap, t["identf"][0:16, 0:16]).then_inc(s_mm.h, 1)
            s_mm.n += 4 * NBC

    def b_pass1(self):
        nc, t = self.nc, self.t
        s_ld, s_ve, s_mm = t["s_ld"], t["s_ve"], t["s_mm"]
        NQ = NBC * HB
        ev = {}
        cmm, cve, cld = s_mm.n, s_ve.n, s_ld.n
        for c in range(NBC):
            cld += 16; ev[(c, "vb")] = cld
            for h in range(HB):
                q = c * HB + h
                cmm += 1; ev[(q, "ktr")] = cmm
                cve += 1; ev[(q, "ktil")] = cve
                cmm += 1; ev[(q, "smm")] = cmm
                cve += 1; ev[(q, "mupd")] = cve
        self._bp1ev = ev

        def tk(a, name):
            return ev[(a, name)]

        def tokcol(qty, c, h):
            return t["toksb"][:, qty * 128 + c * 16 + h: qty * 128 + c * 16 + h + 1]

        with nc.Block() as blk:
            @blk.vector
            def _(v):
                v.memset(t["Mb"][:], 0.0)

        with nc.Block() as blk:
            @blk.sync
            def _(s):
                for c in range(NBC):
                    if c >= 2:
                        s.wait_ge(s_mm.h, tk((c - 1) * HB - 1, "smm"))
                    s.dma_start(out=t["vbin"][:, (c % 2) * 4096:(c % 2) * 4096 + 4096],
                                in_=self.bsc["vb"].ap()[c]).then_inc(s_ld.h, 16)
            s_ld.n = cld

            @blk.tensor
            def _(te):
                for c in range(NBC):
                    te.wait_ge(s_ld.h, tk(c, "vb"))
                    for h in range(HB):
                        q = c * HB + h
                        p = q % 2
                        ksl = t["kb"][:, h * 1024 + c * 128:h * 1024 + (c + 1) * 128]
                        if q >= 1:
                            te.wait_ge(s_ve.h, tk(q - 1, "ktil"))
                        te.transpose(t["ps_t0" if p == 0 else "ps_t1"][:, 0:128], ksl,
                                     t["ident"][:]).then_inc(s_mm.h, 1)
                        te.wait_ge(s_ve.h, tk(q, "ktil"))
                        if q >= 2:
                            te.wait_ge(s_ve.h, tk(q - 2, "mupd"))
                        te.matmul((t["ps_a"] if p == 0 else t["ps_b"])[:, 0:256],
                                  t["ktkb"][:, p * 128:p * 128 + 128],
                                  t["vbin"][:, (c % 2) * 4096 + h * 256:(c % 2) * 4096 + (h + 1) * 256],
                                  start=True, stop=True).then_inc(s_mm.h, 1)
            s_mm.n = cmm

            @blk.vector
            def _(v):
                for c in range(NBC):
                    for h in range(HB):
                        q = c * HB + h
                        p = q % 2
                        v.wait_ge(s_mm.h, tk(q, "ktr"))
                        v.tensor_scalar(t["ktkb"][:, p * 128:p * 128 + 128],
                                        t["ps_t0" if p == 0 else "ps_t1"][:, 0:128],
                                        tokcol(2, c, h), None, ALU.mult)
                        v.engine_nop()
                        v.engine_nop().then_inc(s_ve.h, 1)
                        v.wait_ge(s_mm.h, tk(q, "smm"))
                        v.scalar_tensor_tensor(t["Mb"][:, h * 256:(h + 1) * 256],
                                               t["Mb"][:, h * 256:(h + 1) * 256],
                                               t["lamCbb"][:, h * 8 + c:h * 8 + c + 1],
                                               (t["ps_a"] if p == 0 else t["ps_b"])[:, 0:256],
                                               ALU.mult, ALU.add)
                        v.engine_nop()
                        v.engine_nop().then_inc(s_ve.h, 1)
            s_ve.n = cve

    def b_relay(self):
        nc, t = self.nc, self.t
        s_st, s_cc, s_ld, s_ve, s_mm = t["s_st"], t["s_cc"], t["s_ld"], t["s_ve"], t["s_mm"]
        with nc.Block() as blk:
            @blk.sync
            def _(s):
                s.dma_start(out=t["lbfl"][0:1, 0:16],
                            in_=t["Lbp"][0:16, 1024:1025]).then_inc(s_ld.h, 16)
            s_ld.n += 16

            @blk.scalar
            def _(sc):
                sc.wait_ge(s_ld.h, s_ld.n)
                sc.activation(t["lbfl"][0:1, 16:32], t["lbfl"][0:1, 0:16], AF.Exp).then_inc(t["s_act"].h, 1)
            t["s_act"].n += 1

            @blk.tensor
            def _(te):
                te.wait_ge(t["s_act"].h, t["s_act"].n)
                te.matmul(t["ps_row"][:, 0:16], t["ones_rowf"][:], t["lbfl"][0:1, 16:32],
                          start=True, stop=True).then_inc(s_mm.h, 1)
            s_mm.n += 1

            @blk.vector
            def _(v):
                v.wait_ge(s_mm.h, s_mm.n)
                v.tensor_copy(t["eargb"][:, 0:16], t["ps_row"][:, 0:16])
        with nc.Block() as blk:
            @blk.sync
            def _(s):
                s.dma_start(out=self.cc["rel_b_in"].ap()[:, 0:4096], in_=t["Mb"][:]).then_inc(s_st.h, 16)
                s.dma_start(out=self.cc["rel_b_in"].ap()[:, 4096:4112], in_=t["eargb"][:, 0:16]).then_inc(s_st.h, 16)
            s_st.n += 32

            @blk.gpsimd
            def _(g):
                g.wait_ge(s_st.h, s_st.n)
                g.collective_compute(
                    "AllGather", ALU.bypass,
                    replica_groups=[[0, 1, 2, 3], [4, 5, 6, 7]],
                    ins=[self.cc["rel_b_in"].ap()],
                    outs=[self.cc["rel_b_out"].ap()],
                ).then_inc(s_cc.h, 1)
            s_cc.n += 1

        with nc.Block() as blk:
            @blk.vector
            def _(v):
                v.memset(t["Mb"][:], 0.0)
                v.memset(t["Sbb"][:], 0.0)
        for j in range(3):
            with nc.Block() as blk:
                ld_b = s_ld.n

                @blk.sync
                def _(s):
                    s.wait_ge(s_cc.h, s_cc.n)
                    s.dma_start(out=t["relbuf"][:],
                                in_=self.cc["rel_b_out"].ap()[j * 128:(j + 1) * 128, 0:4096]).then_inc(s_ld.h, 16)
                    s.dma_start(out=t["eargb"][:, 0:16],
                                in_=self.cc["rel_b_out"].ap()[j * 128:(j + 1) * 128, 4096:4112]).then_inc(s_ld.h, 16)
                s_ld.n += 32

                @blk.vector
                def _(v):
                    v.wait_ge(s_ld.h, s_ld.n)
                    for h in range(HB):
                        Ssl = t["Mb"][:, h * 256:(h + 1) * 256]
                        d1 = t["otfb"][:, 0:256]
                        v.scalar_tensor_tensor(d1, Ssl, t["eargb"][:, h:h + 1],
                                               t["relbuf"][:, h * 256:(h + 1) * 256], ALU.mult, ALU.add)
                        v.engine_nop()
                        v.engine_nop()
                        v.tensor_sub(d1, d1, Ssl)
                        v.engine_nop()
                        v.engine_nop()
                        v.scalar_tensor_tensor(Ssl, d1, t["rmask"][:, j:j + 1], Ssl, ALU.mult, ALU.add)
        with nc.Block() as blk:
            @blk.vector
            def _(v):
                v.tensor_copy(t["Sbb"][:], t["Mb"][:])

    def b_pass2(self):
        nc, t = self.nc, self.t
        for blkc in range(2):
            self._b_pass2_block(blkc)
            self.b_wo(blkc)

    def _b_pass2_block(self, blkc):
        nc, t = self.nc, self.t
        s_ld, s_ve, s_act, s_mm, s_g = t["s_ld"], t["s_ve"], t["s_act"], t["s_mm"], t["s_g"]
        cs = list(range(blkc * 4, blkc * 4 + 4))
        ev = {}
        cmm, cve, cact, cld, cg = s_mm.n, s_ve.n, s_act.n, s_ld.n, s_g.n
        for c in cs:
            cld += 16; ev[(c, "vb")] = cld
            cld += 16; ev[(c, "lb0")] = cld
            ev[(c, "lb1")] = None
            for grp in range(4):
                if grp == 2:
                    cld += 16; ev[(c, "lb1")] = cld
                cmm += 1; ev[(c, f"rbc{grp}")] = cmm
                for h in range(grp * 4, grp * 4 + 4):
                    q = c * HB + h
                    cmm += 1; ev[(q, "amm")] = cmm
                    cve += 1; ev[(q, "aarg")] = cve
                    cg += 1; ev[(q, "aargdone")] = cg
                    cact += 1; ev[(q, "aexp")] = cact
                    cve += 1; ev[(q, "abf")] = cve
                    cmm += 1; ev[(q, "o12")] = cmm
                    cve += 1; ev[(q, "ocmb")] = cve
                    cmm += 1; ev[(q, "otr")] = cmm
                    cve += 1; ev[(q, "oev")] = cve
                    cmm += 1; ev[(q, "ktr")] = cmm
                    cve += 1; ev[(q, "ktil")] = cve
                    cmm += 1; ev[(q, "smm")] = cmm
                    cve += 1; ev[(q, "supd")] = cve

        def tk(a, name):
            return ev[(a, name)]

        def tokcol(qty, c, h):
            return t["toksb"][:, qty * 128 + c * 16 + h: qty * 128 + c * 16 + h + 1]

        def bank(p):
            return t["ps_a" if p == 0 else "ps_b"]

        def obank(p):
            return t["ps_c" if p == 0 else "ps_row"]

        def tb_(p):
            return t["ps_t0" if p == 0 else "ps_t1"]

        def heads(c):
            for grp in range(4):
                for h in range(grp * 4, grp * 4 + 4):
                    yield grp, h

        with nc.Block() as blk:
            @blk.sync
            def _(s):
                for c in cs:
                    if c - blkc * 4 >= 2:
                        s.wait_ge(s_ve.h, tk((c - 1) * HB - 1, "supd"))
                    s.dma_start(out=t["vbin"][:, (c % 2) * 4096:(c % 2) * 4096 + 4096],
                                in_=self.bsc["vb"].ap()[c]).then_inc(s_ld.h, 16)
                    if c - blkc * 4 >= 1:
                        s.wait_ge(s_mm.h, tk(c - 1, "rbc3"))
                    s.dma_start(out=t["bfl2"][0:1, 0:1024],
                                in_=t["Lbp"][0:8, 1 + c * 128:1 + (c + 1) * 128]).then_inc(s_ld.h, 16)
                    s.wait_ge(s_mm.h, tk(c, "rbc1"))
                    s.dma_start(out=t["bfl2"][0:1, 0:1024],
                                in_=t["Lbp"][8:16, 1 + c * 128:1 + (c + 1) * 128]).then_inc(s_ld.h, 16)
            s_ld.n = cld

            @blk.tensor
            def _(te):
                for c in cs:
                    te.wait_ge(s_ld.h, tk(c, "vb"))
                    for grp, h in heads(c):
                        q = c * HB + h
                        p = q % 2
                        if h % 4 == 0:
                            te.wait_ge(s_ld.h, tk(c, "lb0" if grp < 2 else "lb1"))
                            if (c, grp) != (cs[0], 0):
                                prevq = c * HB + grp * 4 - 1 if grp > 0 else c * HB - 1
                                if (prevq, "aargdone") in ev:
                                    te.wait_ge(s_g.h, tk(prevq, "aargdone"))
                            te.matmul(t["ps_p0"][:, 0:512], t["ones_rowf"][:],
                                      t["bfl2"][0:1, (grp % 2) * 512:(grp % 2) * 512 + 512],
                                      start=True, stop=True).then_inc(s_mm.h, 1)
                        ksl = t["kb"][:, h * 1024 + c * 128:h * 1024 + (c + 1) * 128]
                        qsl = t["qb"][:, h * 1024 + c * 128:h * 1024 + (c + 1) * 128]
                        vsl = t["vbin"][:, (c % 2) * 4096 + h * 256:(c % 2) * 4096 + (h + 1) * 256]
                        if q - cs[0] * HB >= 2:
                            te.wait_ge(s_ve.h, tk(q - 2, "supd"))
                        te.matmul(bank(p)[:, 0:128], ksl, qsl, start=True, stop=True).then_inc(s_mm.h, 1)
                        te.wait_ge(s_ve.h, tk(q, "abf"))
                        if q - cs[0] * HB >= 2:
                            te.wait_ge(s_ve.h, tk(q - 2, "ocmb"))
                        te.matmul(obank(p)[:, 0:256], t["abfb"][:, p * 128:p * 128 + 128],
                                  vsl, start=True, stop=True)
                        te.matmul(obank(p)[:, 256:512], qsl, t["Sbb"][:, h * 256:(h + 1) * 256],
                                  start=True, stop=True).then_inc(s_mm.h, 1)
                        te.wait_ge(s_ve.h, tk(q, "ocmb"))
                        if q - cs[0] * HB >= 2:
                            te.wait_ge(s_ve.h, tk(q - 2, "oev"))
                        if q - cs[0] * HB >= 1:
                            te.wait_ge(s_ve.h, tk(q - 1, "ktil"))
                        for half in range(2):
                            ins = te.transpose(tb_(p)[:, half * 128:half * 128 + 128],
                                               t["obfb"][:, p * 256 + half * 128:p * 256 + (half + 1) * 128],
                                               t["ident"][:])
                        ins.then_inc(s_mm.h, 1)
                        te.wait_ge(s_ve.h, tk(q, "oev"))
                        te.transpose(tb_(p)[:, 256:384], ksl, t["ident"][:]).then_inc(s_mm.h, 1)
                        te.wait_ge(s_ve.h, tk(q, "ktil"))
                        te.matmul(bank(p)[:, 128:384], t["ktkb"][:, p * 128:p * 128 + 128], vsl,
                                  start=True, stop=True).then_inc(s_mm.h, 1)
            s_mm.n = cmm

            @blk.vector
            def _(v):
                for c in cs:
                    for grp, h in heads(c):
                        q = c * HB + h
                        p = q % 2
                        v.wait_ge(s_mm.h, tk(c, f"rbc{grp}"))
                        v.tensor_add(t["eargb"][:, p * 128:p * 128 + 128],
                                     t["ps_p0"][:, (h % 4) * 128:(h % 4) * 128 + 128], t["maskI"][:])
                        v.engine_nop()
                        v.engine_nop().then_inc(s_ve.h, 1)
                        v.wait_ge(s_act.h, tk(q, "aexp"))
                        v.wait_ge(s_mm.h, tk(q, "amm"))
                        v.tensor_mul(t["abfb"][:, p * 128:p * 128 + 128], bank(p)[:, 0:128],
                                     t["eargb"][:, p * 128:p * 128 + 128])
                        v.engine_nop()
                        v.engine_nop().then_inc(s_ve.h, 1)
                        v.wait_ge(s_mm.h, tk(q, "o12"))
                        osl = t["otfb"][:, p * 256:p * 256 + 256]
                        v.tensor_copy(osl, obank(p)[:, 0:256])
                        v.engine_nop()
                        v.engine_nop()
                        v.scalar_tensor_tensor(osl, obank(p)[:, 256:512], tokcol(1, c, h),
                                               osl, ALU.mult, ALU.add)
                        v.engine_nop()
                        v.engine_nop()
                        v.tensor_scalar(t["obfb"][:, p * 256:p * 256 + 256], osl,
                                        tokcol(0, c, h), None, ALU.mult)
                        v.engine_nop()
                        v.engine_nop().then_inc(s_ve.h, 1)
                        v.wait_ge(s_mm.h, tk(q, "otr"))
                        base = (2 * h) * 512 + (c % 4) * 128
                        v.tensor_copy(t["ofmb"][:, base:base + 128], tb_(p)[:, 0:128])
                        v.tensor_copy(t["ofmb"][:, base + 512:base + 512 + 128], tb_(p)[:, 128:256])
                        v.engine_nop()
                        v.engine_nop().then_inc(s_ve.h, 1)
                        v.wait_ge(s_mm.h, tk(q, "ktr"))
                        v.tensor_scalar(t["ktkb"][:, p * 128:p * 128 + 128], tb_(p)[:, 256:384],
                                        tokcol(2, c, h), None, ALU.mult)
                        v.engine_nop()
                        v.engine_nop().then_inc(s_ve.h, 1)
                        v.wait_ge(s_mm.h, tk(q, "smm"))
                        Ssl = t["Mb"][:, h * 256:(h + 1) * 256]
                        v.scalar_tensor_tensor(Ssl, Ssl, t["lamCbb"][:, h * 8 + c:h * 8 + c + 1],
                                               bank(p)[:, 128:384], ALU.mult, ALU.add)
                        v.engine_nop()
                        v.engine_nop()
                        v.tensor_copy(t["Sbb"][:, h * 256:(h + 1) * 256], Ssl)
                        v.engine_nop()
                        v.engine_nop().then_inc(s_ve.h, 1)
            s_ve.n = cve

            @blk.scalar
            def _(sc):
                for c in cs:
                    for grp, h in heads(c):
                        q = c * HB + h
                        p = q % 2
                        sc.wait_ge(s_ve.h, tk(q, "aarg"))
                        sc.activation(t["eargb"][:, p * 128:p * 128 + 128],
                                      t["eargb"][:, p * 128:p * 128 + 128], AF.Exp,
                                      bias=tokcol(3, c, h)).then_inc(s_act.h, 1)
            s_act.n = cact

            @blk.gpsimd
            def _(g):
                for c in cs:
                    for grp, h in heads(c):
                        q = c * HB + h
                        g.wait_ge(s_ve.h, tk(q, "aarg"))
                        g.engine_nop()
                        g.engine_nop().then_inc(s_g.h, 1)
            s_g.n = cg

    def b_wo(self, blkc):
        nc, t = self.nc, self.t
        s_pk, s_ev, s_st = t["s_pk"], t["s_ev"], t["s_st"]

        def rhs(i, tb):
            return t["ofmb"][:, i * 512:(i + 1) * 512]

        ev_base = s_ev.n
        pk_base = s_pk.n
        st_base = s_st.n

        def scalar_xp(sc, oc, tb, mm_t):
            if oc >= 2:
                sc.wait_ge(s_ev.h, ev_base + oc - 1)
            sc.dma_start(out=t["xpin"][:, (oc % 2) * 512:(oc % 2) * 512 + 512],
                         in_=self.x_park.ap()[oc * 128:(oc + 1) * 128,
                                              blkc * 512:blkc * 512 + 512]).then_inc(s_pk.h, 16)

        def evac2(v, oc, tb, ps):
            v.wait_ge(s_pk.h, pk_base + 16 * (oc + 1))
            if oc >= 2:
                v.wait_ge(s_st.h, st_base + 16 * (oc - 1))
            v.tensor_add(t["xout"][:, (oc % 2) * 512:(oc % 2) * 512 + 512],
                         t["xpin"][:, (oc % 2) * 512:(oc % 2) * 512 + 512], ps)

        def store(se, oc, tb, ev_t):
            se.wait_ge(s_ev.h, ev_t)
            se.dma_start(out=self.bsc["xp2"].ap()[oc * 128:(oc + 1) * 128,
                                                  blkc * 512:blkc * 512 + 512],
                         in_=t["xout"][:, (oc % 2) * 512:(oc % 2) * 512 + 512]).then_inc(s_st.h, 16)

        self.proj_lhsT("Wo_b", 16, 128, list(range(32)), rhs, evac2, nt=1, tw=512,
                       scalar_fn=scalar_xp, store_fn=store)
        s_pk.n += 16 * 16

    def b_restore(self):
        nc, t = self.nc, self.t
        s_ld, s_act = t["s_ld"], t["s_act"]
        with nc.Block() as blk:
            ld_b = s_ld.n

            @blk.sync
            def _(s):
                for c in range(NCH):
                    s.dma_start(out=t["x_res"][:, c * TPC:(c + 1) * TPC],
                                in_=self.bsc["xp2"].ap()[c * 128:(c + 1) * 128, :]).then_inc(s_ld.h, 16)
            s_ld.n += 16 * NCH

            @blk.scalar
            def _(sc):
                for c in range(NCH):
                    sc.wait_ge(s_ld.h, ld_b + 16 * (c + 1))
                    sc.copy(t["x_bf"][:, c * TPC:(c + 1) * TPC], t["x_res"][:, c * TPC:(c + 1) * TPC])

    def finish(self):
        nc, t = self.nc, self.t
        s_st = t["s_st"]
        with nc.Block() as blk:
            @blk.sync
            def _(s):
                for c in range(NCH):
                    s.dma_start(out=self.y.ap()[c * 128:(c + 1) * 128, :],
                                in_=t["x_res"][:, c * TPC:(c + 1) * TPC]).then_inc(s_st.h, 16)
                s.wait_ge(s_st.h, s_st.n + 16 * NCH)
            s_st.n += 16 * NCH


def _lhsT_swizzle(w, ocols=128):
    O, Din = w.shape
    n_oc = max(1, O // ocols)
    wt = np.ascontiguousarray(w.T)
    h = wt.reshape(Din // 128, 128, n_oc, ocols).transpose(2, 1, 0, 3)
    return np.ascontiguousarray(h.reshape(n_oc, 128, (Din // 128) * ocols))


def _colchunks(a):
    C, k = a.shape
    return np.ascontiguousarray(a.reshape(C // 128, 128, k).transpose(1, 0, 2).reshape(128, -1))


def _colvec(a):
    return np.ascontiguousarray(a.reshape(-1, 128).T)


def _rhs_swizzle(w, og=512):
    O, Din = w.shape
    wt = np.ascontiguousarray(w.T)
    h = wt.reshape(Din // 128, 128, O // og, og).transpose(2, 1, 0, 3)
    return np.ascontiguousarray(h.reshape(O // og, 128, (Din // 128) * og))


def host_prep(inputs):
    f = np.ascontiguousarray
    x = inputs["L0_attn_a_x"]
    Wba = np.concatenate([inputs["L0_attn_a_Wb"],
                          np.zeros((16, 2048), np.float32),
                          inputs["L0_attn_a_Wa"]], 0)
    Wgb = np.concatenate([inputs["L0_attn_b_Wg"],
                          np.zeros((16, 2048), np.float32),
                          inputs["L0_attn_b_Wb"]], 0)
    shared = {
        "Wq_a": _lhsT_swizzle(inputs["L0_attn_a_Wq"]),
        "Wk_a": _lhsT_swizzle(inputs["L0_attn_a_Wk"]),
        "Wv_a": _lhsT_swizzle(inputs["L0_attn_a_Wv"]),
        "Wba_w": _lhsT_swizzle(Wba, 40),
        "Wo_a": _lhsT_swizzle(inputs["L0_attn_a_Wo"]),
        "ffn_w": _lhsT_swizzle(inputs["L0_ffn_weight"]),
        "Wq_b": _lhsT_swizzle(inputs["L0_attn_b_Wq"]),
        "Wk_b": _lhsT_swizzle(inputs["L0_attn_b_Wk"]),
        "Wgb_w": _lhsT_swizzle(Wgb, 48),
        "Wo_b": _lhsT_swizzle(inputs["L0_attn_b_Wo"]),
        "Wg_a": _rhs_swizzle(inputs["L0_attn_a_Wg"]),
        "Wv_b": _rhs_swizzle(inputs["L0_attn_b_Wv"]),
        "conv_q_a": _colchunks(inputs["L0_attn_a_conv_q"]),
        "conv_k_a": _colchunks(inputs["L0_attn_a_conv_k"]),
        "conv_v_a": _colchunks(inputs["L0_attn_a_conv_v"]),
        "convW_q_b": f(inputs["L0_attn_b_Wq_conv"].transpose(2, 1, 0)),
        "convW_k_b": f(inputs["L0_attn_b_Wk_conv"].transpose(2, 1, 0)),
        "bq_conv_b": _colvec(inputs["L0_attn_b_bq_conv"]),
        "bk_conv_b": _colvec(inputs["L0_attn_b_bk_conv"]),
        "bg_b": f(inputs["L0_attn_b_bg"][None]),
        "bb_b": f(inputs["L0_attn_b_bb"][None]),
        "A_log": f(inputs["L0_attn_a_A_log"][:, None]),
        "dt_bias": f(inputs["L0_attn_a_dt_bias"][:, None]),
        "o_norm_w": f(inputs["L0_attn_a_o_norm_w"][None]),
        "ffn_scale": _colvec(inputs["L0_ffn_scale_weight"]),
        "ffn_bias": _colvec(inputs["L0_ffn_bias"]),
    }
    in_maps = []
    for core in range(8):
        b, r = core // 4, core % 4
        m = dict(shared)
        m["xs"] = f(x[b, r * TPC:(r + 1) * TPC, :].T)
        ri = np.zeros(8, np.float32)
        for j in range(3):
            ri[j] = 1.0 if j < r else 0.0
            ri[3 + j] = 1.0 if j == r - 1 else 0.0
        ri[6], ri[7] = r, b
        m["rankinfo"] = ri[None]
        in_maps.append(m)
    return in_maps


def kernel(**inputs):
    in_maps = host_prep(inputs)
    bld = Builder()
    nc = bld.build()
    res = run_bass_kernel_spmd(nc, in_maps, core_ids=list(range(8)))
    out = np.empty((B, L, D), np.float32)
    for core in range(8):
        b, r = core // 4, core % 4
        out[b, r * TPC:(r + 1) * TPC, :] = res.results[core]["y"].T
    return out


# version 03
# speedup vs baseline: 1.1570x; 1.1570x over previous
import numpy as np
from contextlib import ExitStack

import concourse.bass as bass
import concourse.mybir as mybir
from concourse.bass_utils import run_bass_kernel_spmd

F32 = mybir.dt.float32
BF16 = mybir.dt.bfloat16
AF = mybir.ActivationFunctionType
ALU = mybir.AluOpType

B, L, D = 2, 4096, 2048
TPC = 1024
NCH = D // 128
NTC = TPC // 128
HA, DH, NH = 8, 128, 2
SS = TPC * NH
NAC = SS // 128
HB, KD, VEFF = 16, 128, 256
NBC = TPC // 128

NEG = -1.0e30



def out_slice_fn(kind, h, t):
    if kind == "q":
        return t["aq"][:, h * 1024:(h + 1) * 1024]
    if kind == "k0":
        return t["ak"][:, h * 2048:h * 2048 + 2048:2]
    if kind == "k1":
        return t["ak"][:, h * 2048 + 1:h * 2048 + 2048:2]
    if kind == "v0":
        return t["vscr"][:, 0:2048:2]
    return t["vscr"][:, 1:2048:2]


class Sem:
    def __init__(self, h):
        self.h = h
        self.n = 0

    def inc(self, inst, k=1):
        inst.then_inc(self.h, k)
        self.n += k
        return self.n


class Builder:
    def __init__(self, debug_taps=(), stop=None):
        self.stop = stop
        self.debug_taps = list(debug_taps)
        nc = bass.Bass("TRN2", target_bir_lowering=False, detect_race_conditions=False)
        self.nc = nc
        self.ctx = ExitStack()
        self.t = {}

    def sb(self, name, shape, dtype, stack=None):
        h = (stack or self.ctx).enter_context(self.nc.sbuf_tensor(name, shape, dtype))
        self.t[name] = h
        return h

    def ps(self, name, shape, dtype=F32):
        h = self.ctx.enter_context(self.nc.psum_tensor(name, shape, dtype))
        self.t[name] = h
        return h

    def sem(self, name):
        s = Sem(self.ctx.enter_context(self.nc.semaphore(name)))
        self.t[name] = s
        return s

    def tap(self, name, ap_fn, shape, dtype=F32):
        if name not in self.debug_taps:
            return
        nc = self.nc
        out = nc.dram_tensor("tap_" + name, shape, dtype, kind="ExternalOutput")
        sem = Sem(self.ctx.enter_context(nc.semaphore("tapsem_" + name)))
        with nc.Block() as blk:
            @blk.sync
            def _(s):
                s.dma_start(out=out.ap(), in_=ap_fn()).then_inc(sem.h, 16)
                s.wait_ge(sem.h, 16)

    def proj_lhsT(self, wname, n_oc, ocols, kcs, rhs_fn, evac_fn, nt=2, tw=512,
                  scalar_fn=None, store_fn=None):
        nc, t = self.nc, self.t
        s_w, s_wc, s_mm, s_ev = t["s_w"], t["s_wc"], t["s_mm"], t["s_ev"]
        nk = len(kcs)
        wslab = t["wslab"]
        slabw = nk * ocols
        assert slabw <= 4096
        d = self.d
        w_base, wc_base, mm_base, ev_base = s_w.n, s_wc.n, s_mm.n, s_ev.n
        ngr = n_oc * nt

        with nc.Block() as blk:
            @blk.gpsimd
            def _(g):
                for oc in range(n_oc):
                    if oc >= 3:
                        g.wait_ge(s_ev.h, ev_base + (oc - 2) * nt)
                    g.dma_start(
                        out=wslab[:, (oc % 3) * 4096:(oc % 3) * 4096 + slabw],
                        in_=d[wname].ap()[oc],
                    ).then_inc(s_w.h, 16)
            s_w.n += 16 * n_oc

            @blk.tensor
            def _(te):
                for oc in range(n_oc):
                    te.wait_ge(s_w.h, w_base + 16 * (oc + 1))
                    for tb in range(nt):
                        gi = oc * nt + tb
                        ps = t["ps_p0" if gi % 2 == 0 else "ps_p1"][0:ocols, 0:tw]
                        if gi >= 2:
                            te.wait_ge(s_ev.h, ev_base + gi - 1)
                        for i, kc in enumerate(kcs):
                            ins = te.matmul(
                                ps,
                                wslab[:, (oc % 3) * 4096 + i * ocols:(oc % 3) * 4096 + (i + 1) * ocols],
                                rhs_fn(i, tb),
                                start=(i == 0), stop=(i == nk - 1))
                        ins.then_inc(s_mm.h, 1)
            s_mm.n += ngr

            s_st = t["s_st"]
            st_base = s_st.n

            @blk.vector
            def _(v):
                for oc in range(n_oc):
                    for tb in range(nt):
                        gi = oc * nt + tb
                        v.wait_ge(s_mm.h, mm_base + gi + 1)
                        if store_fn is not None and gi >= 4:
                            v.wait_ge(s_st.h, st_base + 16 * (gi - 3))
                        ps = t["ps_p0" if gi % 2 == 0 else "ps_p1"][0:ocols, 0:tw]
                        evac_fn(v, oc, tb, ps)
                        v.engine_nop()
                        v.engine_nop().then_inc(s_ev.h, 1)
            s_ev.n += ngr

            if scalar_fn is not None:
                @blk.scalar
                def _(sc):
                    for oc in range(n_oc):
                        for tb in range(nt):
                            gi = oc * nt + tb
                            scalar_fn(sc, oc, tb, mm_base + gi + 1)

            if store_fn is not None:
                @blk.sync
                def _(se):
                    for oc in range(n_oc):
                        for tb in range(nt):
                            gi = oc * nt + tb
                            store_fn(se, oc, tb, ev_base + gi + 1)
                s_st.n += 16 * ngr

    def build(self):
        nc = self.nc
        d = {}

        def din(name, shape):
            d[name] = nc.dram_tensor(name, shape, F32, kind="ExternalInput")
            return d[name]

        din("xs", [D, TPC])
        din("rankinfo", [1, 8])
        din("Wq_a", [8, 128, NCH * 128])
        din("Wk_a", [16, 128, NCH * 128])
        din("Wv_a", [16, 128, NCH * 128])
        din("Wba_w", [1, 128, NCH * 40])
        din("Wo_a", [16, 128, 8 * 128])
        din("ffn_w", [16, 128, NCH * 128])
        din("Wq_b", [16, 128, NCH * 128])
        din("Wk_b", [16, 128, NCH * 128])
        din("Wgb_w", [1, 128, NCH * 48])
        din("Wo_b", [16, 128, 32 * 128])
        din("Wg_a", [2, 128, NCH * 512])
        din("Wv_b", [8, 128, NCH * 512])
        din("conv_q_a", [128, 8 * 4])
        din("conv_k_a", [128, 16 * 4])
        din("conv_v_a", [128, 16 * 4])
        din("convW_q_b", [4, 128, 2048])
        din("convW_k_b", [4, 128, 2048])
        din("bq_conv_b", [128, 16])
        din("bk_conv_b", [128, 16])
        din("bg_b", [1, 16])
        din("bb_b", [1, 16])
        din("A_log", [8, 1])
        din("dt_bias", [8, 1])
        din("o_norm_w", [1, 128])
        din("ffn_scale", [128, 16])
        din("ffn_bias", [128, 16])
        self.d = d

        self.y = nc.dram_tensor("y", [D, TPC], F32, kind="ExternalOutput")
        self.x_park = nc.dram_tensor("x_park", [D, TPC], F32)
        self.cc = {}
        for nm, shape in [
            ("halo_a_in", [128, 120]), ("halo_a_out", [512, 120]),
            ("rel_a_in", [128, HA * 256]), ("rel_a_out", [512, HA * 256]),
            ("halo_b_in", [128, 96]), ("halo_b_out", [512, 96]),
            ("rel_b_in", [128, HB * VEFF + 16]), ("rel_b_out", [512, HB * VEFF + 16]),
        ]:
            self.cc[nm] = nc.dram_tensor("cc_" + nm, shape, F32)

        import ml_dtypes
        ident_bf = np.eye(128, dtype=ml_dtypes.bfloat16)
        self.ident_dram = nc.inline_tensor(ident_bf.view(np.uint16), name="c_ident")
        ii, jj = np.arange(128)[:, None], np.arange(128)[None, :]
        self.maskS_dram = nc.inline_tensor(np.where(jj > ii, 0.0, NEG).astype(np.float32), name="c_maskS")
        self.maskI_dram = nc.inline_tensor(np.where(jj >= ii, 0.0, NEG).astype(np.float32), name="c_maskI")
        E = np.zeros((128, 64), np.float32)
        for h in range(8):
            E[:, 8 * h + h] = 1.0
        self.E_dram = nc.inline_tensor(E.astype(ml_dtypes.bfloat16).view(np.uint16), name="c_E")
        self.identf_dram = nc.inline_tensor(np.eye(128, dtype=np.float32), name="c_identf")
        mq = np.where(np.arange(128)[:, None] <= 2 * np.arange(64)[None, :] + 1, 0.0, NEG)
        self.maskQ_dram = nc.inline_tensor(mq.astype(np.float32), name="c_maskQ")

        self.sb("ident", [128, 128], BF16)
        self.sb("maskS", [128, 128], F32)
        self.sb("maskI", [128, 128], F32)
        self.sb("Ecols", [128, 64], BF16)
        self.sb("ones_col", [128, 1], BF16)
        self.sb("ones_row", [1, 128], BF16)
        self.sb("ones_rowf", [1, 128], F32)
        self.sb("maskQ", [128, 64], F32)
        self.sb("rmask", [128, 8], F32)
        self.sb("cst", [128, 4], F32)
        self.sb("cq_a", [128, 8 * 4], F32)
        self.sb("ck_a", [128, 16 * 4], F32)
        self.sb("cv_a", [128, 16 * 4], F32)
        self.sb("bqk_conv_b", [128, 2 * 16], F32)
        self.sb("A_log_c", [8, 1], F32)
        self.sb("dt_bias_c", [8, 1], F32)
        self.sb("negexpA", [8, 1], F32)
        self.sb("o_nw_row", [1, 128], F32)
        self.sb("Wonw_bc", [128, 128], F32)
        self.sb("ffn_sc", [128, 16], F32)
        self.sb("ffn_bi", [128, 16], F32)
        self.sb("gb16", [1, 16], F32)
        self.sb("bb16", [1, 16], F32)

        self.sb("identf", [128, 128], F32)
        self.sb("row_scr", [1, 4 * 512], F32)
        self.sb("ab_rows", [1, 2 * TPC], F32)
        self.sb("ab_rows_bf", [1, 2 * TPC], BF16)
        self.sb("sq_scr", [128, 4 * 512], BF16)
        self.sb("wslab", [128, 3 * 4096], BF16)
        self.abase = self.nc._sbuf_addr_for_side("left")
        self.asize = 229344 - self.abase - 64
        def at(name, shape, dtype, off):
            nbytes = int(np.prod(shape[1:])) * (2 if dtype == BF16 else 4)
            assert off + nbytes <= self.asize, (name, off, nbytes, self.asize)
            h = self.nc.alloc_sbuf_tensor_at(name, shape, dtype, offset=self.abase + off)
            self.t[name] = h
            return h
        self.at = at
        K = 1024
        at("x_res", [128, NCH * TPC], F32, 0)
        at("x_bf", [128, NCH * TPC], BF16, 65 * K)
        at("xn", [128, NCH * TPC], BF16, 97 * K)

        self.ps("ps_p0", [128, 512])
        self.ps("ps_p1", [128, 512])
        self.ps("ps_a", [128, 512])
        self.ps("ps_b", [128, 512])
        self.ps("ps_c", [128, 512])
        self.ps("ps_row", [128, 512])
        self.ps("ps_t0", [128, 512], BF16)
        self.ps("ps_t1", [128, 512], BF16)

        for s in ["s_ld", "s_w", "s_wc", "s_mm", "s_ev", "s_ev2", "s_pk", "s_cc",
                  "s_st", "s_x", "s_t0", "s_t1", "s_pa", "s_pb", "s_pc",
                  "s_row", "s_act", "s_ve", "s_g"]:
            self.sem(s)

        self.p0_loads()
        self.ln("xn1")
        self.attn_a()
        if self.stop is None:
            self.ln("xn2")
            self.ffn()
            self.ln("xn3")
            self.attn_b()
            self.ln("xn4")
            self.ffn()
        self.finish()
        self.ctx.close()
        return self.nc

    def p0_loads(self):
        nc, t, d = self.nc, self.t, self.d
        s_ld = t["s_ld"]

        with nc.Block() as blk:
            @blk.sync
            def _(s):
                loads = [
                    (t["x_res"][:], d["xs"].ap().rearrange("(c p) t -> p c t", p=128)),
                    (t["ident"][:].bitcast(mybir.dt.uint16), self.ident_dram.ap()),
                    (t["identf"][:], self.identf_dram.ap()),
                    (t["maskQ"][:], self.maskQ_dram.ap()),
                    (t["maskS"][:], self.maskS_dram.ap()),
                    (t["maskI"][:], self.maskI_dram.ap()),
                    (t["Ecols"][:].bitcast(mybir.dt.uint16), self.E_dram.ap()),
                    (t["cq_a"][:], d["conv_q_a"].ap()),
                    (t["ck_a"][:], d["conv_k_a"].ap()),
                    (t["cv_a"][:], d["conv_v_a"].ap()),
                    (t["A_log_c"][:], d["A_log"].ap()),
                    (t["dt_bias_c"][:], d["dt_bias"].ap()),
                    (t["o_nw_row"][:], d["o_norm_w"].ap()),
                    (t["ffn_sc"][:], d["ffn_scale"].ap()),
                    (t["ffn_bi"][:], d["ffn_bias"].ap()),
                    (t["gb16"][:], d["bg_b"].ap()),
                    (t["bb16"][:], d["bb_b"].ap()),
                    (t["bqk_conv_b"][:, 0:16], d["bq_conv_b"].ap()),
                    (t["bqk_conv_b"][:, 16:32], d["bk_conv_b"].ap()),
                    (t["row_scr"][0:1, 0:8], d["rankinfo"].ap()),
                ]
                for o_, i_ in loads:
                    s.dma_start(out=o_, in_=i_).then_inc(s_ld.h, 16)

            @blk.gpsimd
            def _(g):
                g.memset(t["ones_col"][:], 1.0)
                g.memset(t["ones_row"][:], 1.0)
                g.memset(t["ones_rowf"][:], 1.0)
                g.memset(t["cst"][:, 0:1], 1e-5)
                g.memset(t["cst"][:, 1:2], 1.0 / 128.0)
                g.memset(t["cst"][:, 2:3], 1.0)
            s_ld.n += 16 * 20

        with nc.Block() as blk:
            n_ld = s_ld.n
            s_mm, s_act = t["s_mm"], t["s_act"]

            @blk.scalar
            def _(sc):
                sc.wait_ge(s_ld.h, n_ld)
                for c in range(NCH):
                    sc.copy(t["x_bf"][:, c * TPC:(c + 1) * TPC],
                            t["x_res"][:, c * TPC:(c + 1) * TPC])
                sc.activation(t["negexpA"][:], t["A_log_c"][:], AF.Exp).then_inc(s_act.h, 1)

            @blk.vector
            def _(v):
                v.wait_ge(s_act.h, s_act.n + 1)
                v.tensor_scalar_mul(t["negexpA"][:], t["negexpA"][:], -1.0)
                v.wait_ge(s_ld.h, n_ld)
                v.tensor_copy(t["ab_rows_bf"][0:1, 0:8], t["row_scr"][0:1, 0:8])
                v.tensor_copy(t["ab_rows_bf"][0:1, 8:136], t["o_nw_row"][0:1, 0:128]).then_inc(t["s_ve"].h, 1)
            s_act.n += 1
            s_ve_n = t["s_ve"].n = t["s_ve"].n + 1

            @blk.tensor
            def _(te):
                te.wait_ge(t["s_ve"].h, s_ve_n)
                te.matmul(t["ps_row"][:, 0:8], t["ones_row"][:], t["ab_rows_bf"][0:1, 0:8],
                          start=True, stop=True)
                te.matmul(t["ps_row"][:, 8:136], t["ones_row"][:], t["ab_rows_bf"][0:1, 8:136],
                          start=True, stop=True).then_inc(s_mm.h, 1)
            s_mm.n += 1

        with nc.Block() as blk:
            n_mm = t["s_mm"].n

            @blk.scalar
            def _(sc):
                sc.wait_ge(t["s_mm"].h, n_mm)
                sc.copy(t["rmask"][:], t["ps_row"][:, 0:8])
                sc.copy(t["Wonw_bc"][:], t["ps_row"][:, 8:136])

    def ln(self, tapn):
        nc, t = self.nc, self.t
        s_mm, s_act, s_ve, s_row, s_g = t["s_mm"], t["s_act"], t["s_ve"], t["s_row"], t["s_g"]
        src, dst = "x_bf", "xn"

        with nc.Block() as blk:
            mm_base, act_base = s_mm.n, s_act.n

            @blk.scalar
            def _(sc):
                for i in range(2 * NCH):
                    c, tb = i // 2, i % 2
                    if i >= 4:
                        sc.wait_ge(s_mm.h, mm_base + (i - 4) + 1)
                    sc.activation(t["sq_scr"][:, (i % 4) * 512:(i % 4) * 512 + 512],
                                  t[src][:, c * TPC + tb * 512: c * TPC + tb * 512 + 512],
                                  AF.Square).then_inc(s_act.h, 1)
            s_act.n += 2 * NCH

            @blk.tensor
            def _(te):
                for i in range(2 * NCH):
                    c, tb = i // 2, i % 2
                    te.matmul(t["ps_a" if tb == 0 else "ps_b"][0:1, :],
                              t["ones_col"][:],
                              t[src][:, c * TPC + tb * 512: c * TPC + tb * 512 + 512],
                              start=(c == 0), stop=(c == NCH - 1))
                    te.wait_ge(s_act.h, act_base + i + 1)
                    te.matmul(t["ps_c" if tb == 0 else "ps_row"][0:1, :],
                              t["ones_col"][:],
                              t["sq_scr"][:, (i % 4) * 512:(i % 4) * 512 + 512],
                              start=(c == 0), stop=(c == NCH - 1)).then_inc(s_mm.h, 1)
            s_mm.n += 2 * NCH

        with nc.Block() as blk:
            @blk.vector
            def _(v):
                v.tensor_copy(t["row_scr"][0:1, 0:512], t["ps_a"][0:1, :])
                v.tensor_copy(t["row_scr"][0:1, 512:1024], t["ps_b"][0:1, :])
                v.tensor_copy(t["row_scr"][0:1, 1024:1536], t["ps_c"][0:1, :])
                v.tensor_copy(t["row_scr"][0:1, 1536:2048], t["ps_row"][0:1, :])
                v.tensor_scalar_mul(t["ab_rows"][0:1, 0:1024], t["row_scr"][0:1, 0:1024], 1.0 / D)
                v.tensor_scalar_mul(t["ab_rows"][0:1, 1024:2048], t["row_scr"][0:1, 1024:2048], 1.0 / D)
                v.tensor_mul(t["row_scr"][0:1, 0:1024], t["ab_rows"][0:1, 0:1024], t["ab_rows"][0:1, 0:1024])
                v.engine_nop()
                v.engine_nop()
                v.tensor_sub(t["row_scr"][0:1, 1024:2048], t["ab_rows"][0:1, 1024:2048],
                             t["row_scr"][0:1, 0:1024])
                v.engine_nop()
                v.engine_nop().then_inc(s_row.h, 1)
                v.wait_ge(s_act.h, s_act.n + 1)
                v.reciprocal(t["row_scr"][0:1, 1024:2048], t["row_scr"][0:1, 0:1024])
                v.engine_nop()
                v.engine_nop().then_inc(s_row.h, 1)
            s_row.n += 2

            @blk.scalar
            def _(sc):
                sc.wait_ge(s_row.h, s_row.n - 1)
                sc.activation(t["row_scr"][0:1, 0:1024], t["row_scr"][0:1, 1024:2048],
                              AF.Sqrt, bias=t["cst"][0:1, 0:1]).then_inc(s_act.h, 1)
            s_act.n += 1

            @blk.gpsimd
            def _(g):
                g.wait_ge(s_row.h, s_row.n)
                g.tensor_mul(t["ab_rows"][0:1, 1024:2048], t["ab_rows"][0:1, 0:1024],
                             t["row_scr"][0:1, 1024:2048])
                g.engine_nop()
                g.engine_nop()
                g.tensor_scalar_mul(t["ab_rows"][0:1, 1024:2048], t["ab_rows"][0:1, 1024:2048], -1.0)
                g.tensor_copy(t["ab_rows"][0:1, 0:1024], t["row_scr"][0:1, 1024:2048])
                g.engine_nop()
                g.engine_nop()
                g.tensor_copy(t["ab_rows_bf"][0:1, 0:2048], t["ab_rows"][0:1, 0:2048])
                g.engine_nop()
                g.engine_nop().then_inc(s_g.h, 1)
            s_g.n += 1

        with nc.Block() as blk:
            banks = ["ps_a", "ps_b", "ps_c", "ps_row"]

            @blk.tensor
            def _(te):
                te.wait_ge(s_g.h, s_g.n)
                for half in range(4):
                    te.matmul(t[banks[half]][:, 0:512], t["ones_row"][:],
                              t["ab_rows_bf"][0:1, half * 512:(half + 1) * 512],
                              start=True, stop=True).then_inc(s_mm.h, 1)
            s_mm.n += 4

        with nc.Block() as blk:
            @blk.vector
            def _(v):
                for c in range(NCH):
                    for tb in range(2):
                        sl = slice(c * TPC + tb * 512, c * TPC + tb * 512 + 512)
                        v.tensor_mul(t[dst][:, sl], t[src][:, sl], t[banks[tb]][:, 0:512])
                        v.tensor_add(t[dst][:, sl], t[dst][:, sl], t[banks[2 + tb]][:, 0:512])

        self.tap(tapn, lambda: t[dst][:], [128, NCH * TPC], BF16)

    def ffn(self):
        t = self.t

        def rhs(i, tb):
            return t["xn"][:, i * TPC + tb * 512: i * TPC + tb * 512 + 512]

        def evac(v, oc, tb, ps):
            xsl = t["x_res"][:, oc * TPC + tb * 512: oc * TPC + tb * 512 + 512]
            v.scalar_tensor_tensor(xsl, ps, t["ffn_sc"][:, oc:oc + 1], xsl,
                                   ALU.mult, ALU.add)
            v.tensor_scalar_add(xsl, xsl, t["ffn_bi"][:, oc:oc + 1])
            v.tensor_copy(t["x_bf"][:, oc * TPC + tb * 512: oc * TPC + tb * 512 + 512], xsl)

        self.proj_lhsT("ffn_w", NCH, 128, list(range(NCH)), rhs, evac)

    def attn_a(self):
        nc, t, d = self.nc, self.t, self.d
        K = 1024

        def at(name, shape, dtype, offk):
            off = int(offk * K)
            nbytes = int(np.prod(shape[1:])) * (2 if dtype == BF16 else 4)
            assert off + nbytes <= self.asize, (name, off, nbytes, self.asize)
            h = self.nc.alloc_sbuf_tensor_at(name, shape, dtype, offset=self.abase + off)
            self.t[name] = h
            return h
        self.aat = at

        at("aq", [128, 8 * 1024], BF16, 0)
        at("ak", [128, 8 * 2048], BF16, 16)
        at("bpad", [16, 1024], F32, 48)
        at("grow", [8, 1024], F32, 52)
        at("Lp", [8, 2080], F32, 56)
        at("rowA", [8, 2048], F32, 65)
        at("colA", [8, 2048], F32, 73)
        at("rowAq", [8, 1024], F32, 81)
        at("b2il", [8, 2048], BF16, 85)
        at("toks", [128, 6 * 128], F32, 89)
        at("lamCbc", [128, 128], F32, 92)
        at("rows2", [8, 64], F32, 92.5)
        at("rowAfl", [1, 2 * 1024], F32, 93)
        at("rowAqfl", [1, 2 * 512], F32, 101)
        at("pscr", [128, 2 * 1024], BF16, 105)
        at("cacc", [128, 2 * 1024], F32, 109)
        at("vscr", [128, 2048], BF16, 117)
        at("bpadB", [8, 1024], F32, 117)
        at("sqs", [128, 2 * 1024], BF16, 121)
        at("tailpk", [128, 128], F32, 142)
        at("halo", [128, 128], F32, 142.5)
        at("evsc", [128, 4 * 512], BF16, 0)
        at("c6", [128, 16], F32, 130)
        at("vin", [128, 2 * 128], BF16, 131)
        at("ktk", [128, 2 * 2 * 128], BF16, 131.5)
        at("bvt", [128, 2 * 128], BF16, 132.5)
        at("earg", [128, 2 * 2 * 128], F32, 133)
        at("abf", [128, 2 * 2 * 128], BF16, 135)
        at("sol", [128, 2 * 10 * 128], BF16, 136)
        at("vw", [128, 2 * 2 * 128], BF16, 141)
        at("FM", [128, 2 * 8 * 128], F32, 142)
        at("FMbf", [128, 2 * 8 * 128], BF16, 48)
        at("Sbf", [128, 8 * 128], BF16, 52)
        at("relst", [128, 2048], F32, 105)
        at("Sin", [128, 8 * 128], F32, 150)
        at("ofm_a", [128, 4 * 1024], BF16, 105)
        at("ofm_b", [128, 4 * 1024], BF16, 117)
        at("ggb", [128, 2 * 1024], BF16, 113)
        at("vwin", [128, 2 * 2 * 128], BF16, 126)
        at("vp2", [128, 2 * 128], BF16, 127)
        at("otf", [64, 2 * 128], F32, 127.5)
        at("gtf", [64, 2 * 128], F32, 128.5)
        at("obf", [64, 2 * 128], BF16, 129.5)
        at("rmsc", [64, 8], F32, 130)

        def ofm_tile(i):
            return (t["ofm_a"][:, i * 1024:(i + 1) * 1024] if i < 4
                    else t["ofm_b"][:, (i - 4) * 1024:(i - 3) * 1024])
        self.ofm_tile = ofm_tile

        self.asc = {
            "qp": nc.dram_tensor("qp_st", [8, 128, 1024], BF16),
            "kp": nc.dram_tensor("kp_st", [16, 128, 1024], BF16),
            "vp": nc.dram_tensor("vp_st", [16, 128, 1024], BF16),
            "gg": nc.dram_tensor("gg_st", [8, 128, 1024], BF16),
            "va": nc.dram_tensor("va_st", [8, 128, 2048], BF16),
            "vw": nc.dram_tensor("vw_st", [NAC * HA, 128, 256], BF16),
        }

        self.a_park_and_proj()
        if self.stop == "proj":
            return
        self.a_halo()
        self.a_conv()
        self.tap("aq", lambda: t["aq"][:], [128, 8 * 1024], BF16)
        self.tap("ak", lambda: t["ak"][:], [128, 8 * 2048], BF16)
        if self.stop == "conv":
            return
        self.a_pads()
        self.tap("Lp", lambda: t["Lp"][0:8, :], [8, 2080], F32)
        self.tap("rowA", lambda: t["rowA"][0:8, :], [8, 2048], F32)
        self.tap("rowAq", lambda: t["rowAq"][0:8, :], [8, 1024], F32)
        self.tap("colA", lambda: t["colA"][0:8, :], [8, 2048], F32)
        self.tap("toks", lambda: t["toks"][:], [128, 6 * 128], F32)
        if self.stop == "pads":
            return
        self.a_pass1()
        self.tap("FM", lambda: t["FM"][:], [128, 2048], F32)
        if "vwst" in self.debug_taps:
            out = nc.dram_tensor("tap_vwst", [NAC * HA, 128, 256], BF16, kind="ExternalOutput")
            sem = Sem(self.ctx.enter_context(nc.semaphore("tapsem_vwst")))
            with nc.Block() as blk:
                @blk.sync
                def _(s):
                    s.dma_start(out=out.ap(), in_=self.asc["vw"].ap()).then_inc(sem.h, 16)
                    s.wait_ge(sem.h, 16)
        if self.stop == "pass1":
            return
        self.a_relay()
        self.tap("Sin", lambda: t["Sin"][:], [128, 1024], F32)
        if self.stop == "relay":
            return
        self.a_pass2()
        self.tap("ofm_a", lambda: t["ofm_a"][:], [128, 4096], BF16)
        self.tap("ofm_b", lambda: t["ofm_b"][:], [128, 4096], BF16)
        if self.stop == "pass2":
            return
        self.a_wo()
        self.tap("x1", lambda: t["x_res"][:], [128, NCH * TPC], F32)

    def a_park_and_proj(self):
        nc, t = self.nc, self.t
        s_pk, s_ev, s_st, s_act = t["s_pk"], t["s_ev"], t["s_st"], t["s_act"]

        with nc.Block() as blk:
            @blk.scalar
            def _(se):
                for c in range(NCH):
                    se.dma_start(out=self.x_park.ap()[c * 128:(c + 1) * 128, :],
                                 in_=t["x_res"][:, c * TPC:(c + 1) * TPC]).then_inc(s_pk.h, 16)
                se.wait_ge(s_pk.h, s_pk.n + 16 * NCH)
            s_pk.n += 16 * NCH

        def rhs(i, tb):
            return t["xn"][:, i * TPC + tb * 512: i * TPC + tb * 512 + 512]

        def mk_evac(base_slot):
            def evac(v, oc, tb, ps):
                gi = oc * 2 + tb
                v.tensor_copy(t["evsc"][:, (gi % 4) * 512:(gi % 4) * 512 + 512], ps)
                if tb == 1:
                    v.tensor_copy(t["tailpk"][:, (base_slot + oc) * 3:(base_slot + oc) * 3 + 3],
                                  ps[:, 509:512])
            return evac

        def mk_store(stname):
            def store(se, oc, tb, ev_t):
                gi = oc * 2 + tb
                se.wait_ge(s_ev.h, ev_t)
                se.dma_start(out=self.asc[stname].ap()[oc][:, tb * 512:tb * 512 + 512],
                             in_=t["evsc"][:, (gi % 4) * 512:(gi % 4) * 512 + 512]).then_inc(s_st.h, 16)
            return store

        self.proj_lhsT("Wq_a", 8, 128, list(range(NCH)), rhs, mk_evac(0), store_fn=mk_store("qp"))
        self.proj_lhsT("Wk_a", 16, 128, list(range(NCH)), rhs, mk_evac(8), store_fn=mk_store("kp"))
        self.proj_lhsT("Wv_a", 16, 128, list(range(NCH)), rhs, mk_evac(24), store_fn=mk_store("vp"))

        def evac_ba(v, oc, tb, ps):
            pass

        def scalar_ba(sc, oc, tb, mm_t):
            sc.wait_ge(t["s_mm"].h, mm_t)
            ps = t["ps_p0" if tb % 2 == 0 else "ps_p1"]
            sl = slice(tb * 512, tb * 512 + 512)
            sc.activation(t["bpad"][0:16, sl], ps[0:16, 0:512], AF.Sigmoid)
            sc.activation(t["grow"][0:8, sl], ps[32:40, 0:512], AF.Relu,
                          bias=t["dt_bias_c"][:])
            sc.activation(t["rowAq"][0:8, sl], ps[32:40, 0:512], AF.Abs,
                          bias=t["dt_bias_c"][:])
            sc.activation(t["rowAq"][0:8, sl], t["rowAq"][0:8, sl], AF.Exp, scale=-1.0)
            sc.activation(t["rowAq"][0:8, sl], t["rowAq"][0:8, sl], AF.Ln,
                          bias=t["cst"][0:8, 2:3]).then_inc(s_act.h, 1)
        self.proj_lhsT("Wba_w", 1, 40, list(range(NCH)), rhs, evac_ba, scalar_fn=scalar_ba)
        s_act.n += 2

        with nc.Block() as blk:
            n_act = s_act.n

            @blk.vector
            def _(v):
                v.wait_ge(s_act.h, n_act)
                v.tensor_scalar_mul(t["bpad"][:], t["bpad"][:], 2.0)
                v.tensor_add(t["grow"][:], t["grow"][:], t["rowAq"][0:8, :])
                v.tensor_scalar(t["grow"][:], t["grow"][:], t["negexpA"][:], None, ALU.mult)

        self.proj_rhs("Wg_a", 2, "gg")

    def proj_rhs(self, wname, n_og, stname, slabname="ak", evname="evsc", xname="xn"):
        nc, t, d = self.nc, self.t, self.d
        s_w, s_mm, s_ev, s_st = t["s_w"], t["s_mm"], t["s_ev"], t["s_st"]
        w_base, mm_base, ev_base, st_base = s_w.n, s_mm.n, s_ev.n, s_st.n
        slab = t[slabname]

        with nc.Block() as blk:
            @blk.gpsimd
            def _(g):
                for og in range(n_og):
                    if og >= 2:
                        g.wait_ge(s_ev.h, ev_base + (og - 1) * NTC)
                    g.dma_start(out=slab[:, (og % 2) * 8192:(og % 2) * 8192 + 8192],
                                in_=d[wname].ap()[og]).then_inc(s_w.h, 16)
            s_w.n += 16 * n_og

            @blk.tensor
            def _(te):
                for og in range(n_og):
                    te.wait_ge(s_w.h, w_base + 16 * (og + 1))
                    for tc in range(NTC):
                        gi = og * NTC + tc
                        ps = t["ps_p0" if gi % 2 == 0 else "ps_p1"][:, 0:512]
                        if gi >= 2:
                            te.wait_ge(s_ev.h, ev_base + gi - 1)
                        for kc in range(NCH):
                            ins = te.matmul(
                                ps,
                                t[xname][:, kc * TPC + tc * 128: kc * TPC + (tc + 1) * 128],
                                slab[:, (og % 2) * 8192 + kc * 512:(og % 2) * 8192 + (kc + 1) * 512],
                                start=(kc == 0), stop=(kc == NCH - 1))
                        ins.then_inc(s_mm.h, 1)
            s_mm.n += n_og * NTC

            @blk.vector
            def _(v):
                for gi in range(n_og * NTC):
                    v.wait_ge(s_mm.h, mm_base + gi + 1)
                    if gi >= 4:
                        v.wait_ge(s_st.h, st_base + 16 * (gi - 3))
                    ps = t["ps_p0" if gi % 2 == 0 else "ps_p1"][:, 0:512]
                    v.tensor_copy(t[evname][:, (gi % 4) * 512:(gi % 4) * 512 + 512], ps).then_inc(s_ev.h, 1)
            s_ev.n += n_og * NTC

            @blk.sync
            def _(se):
                for og in range(n_og):
                    for tc in range(NTC):
                        gi = og * NTC + tc
                        se.wait_ge(s_ev.h, ev_base + gi + 1)
                        dst = self.asc.get(stname) if hasattr(self, "asc") and stname in getattr(self, "asc", {}) else self.bsc[stname[:-2] if stname.endswith("_b") else stname]
                        se.dma_start(out=dst.ap()[tc][:, og * 512:og * 512 + 512],
                                     in_=t[evname][:, (gi % 4) * 512:(gi % 4) * 512 + 512]).then_inc(s_st.h, 16)
            s_st.n += 16 * n_og * NTC

    def a_halo(self):
        nc, t = self.nc, self.t
        s_st, s_cc, s_ld = t["s_st"], t["s_cc"], t["s_ld"]
        with nc.Block() as blk:
            @blk.sync
            def _(s):
                s.dma_start(out=self.cc["halo_a_in"].ap(), in_=t["tailpk"][:, 0:120]).then_inc(s_st.h, 16)
            s_st.n += 16

            @blk.gpsimd
            def _(g):
                g.wait_ge(s_st.h, s_st.n)
                g.collective_compute(
                    "AllGather", ALU.bypass,
                    replica_groups=[[0, 1, 2, 3], [4, 5, 6, 7]],
                    ins=[self.cc["halo_a_in"].ap()],
                    outs=[self.cc["halo_a_out"].ap()],
                ).then_inc(s_cc.h, 1)
            s_cc.n += 1

        with nc.Block() as blk:
            @blk.sync
            def _(s):
                s.wait_ge(s_cc.h, s_cc.n)
                for j in range(3):
                    s.dma_start(out=t["relst"][:, j * 128:j * 128 + 120],
                                in_=self.cc["halo_a_out"].ap()[j * 128:(j + 1) * 128, :]).then_inc(s_ld.h, 16)
            s_ld.n += 48

            @blk.vector
            def _(v):
                v.memset(t["halo"][:], 0.0)
                v.wait_ge(s_ld.h, s_ld.n)
                for j in range(3):
                    v.engine_nop()
                    v.engine_nop()
                    v.scalar_tensor_tensor(t["halo"][:, 0:120], t["relst"][:, j * 128:j * 128 + 120],
                                           t["rmask"][:, 3 + j:4 + j], t["halo"][:, 0:120],
                                           ALU.mult, ALU.add)

    def a_conv(self):
        nc, t = self.nc, self.t

        def out_slice(kind, h):
            if kind == "q":
                return t["aq"][:, h * 1024:(h + 1) * 1024]
            if kind == "k0":
                return t["ak"][:, h * 2048:h * 2048 + 2048:2]
            if kind == "k1":
                return t["ak"][:, h * 2048 + 1:h * 2048 + 2048:2]
            if kind == "v0":
                return t["vscr"][:, 0:2048:2]
            return t["vscr"][:, 1:2048:2]

        items_a = ([("q", "qp", h, h, h, h) for h in range(8)]
                   + [("k0", "kp", h, h, h, 8 + h) for h in range(8)])
        items_b = ([("k1", "kp", 8 + h, 8 + h, h, 16 + h) for h in range(8)]
                   + sum([[("v0", "vp", h, h, h, 24 + h), ("v1", "vp", 8 + h, 8 + h, h, 32 + h)]
                          for h in range(8)], []))

        self._conv_block(items_a, {"q": ("ps_a", "ps_b"), "k0": ("ps_c", "ps_row")}, "cq_ck")
        with nc.Block() as blk:
            @blk.vector
            def _(v):
                v.tensor_copy(t["rowAq"][0:8, 0:512], t["ps_a"][0:8, 0:512])
                v.tensor_copy(t["rowAq"][0:8, 512:1024], t["ps_b"][0:8, 0:512])
                v.tensor_copy(t["colA"][0:8, 0:1024:2], t["ps_c"][0:8, 0:512])
                v.tensor_copy(t["colA"][0:8, 1024:2048:2], t["ps_row"][0:8, 0:512])
        self._conv_block(items_b, {"k1": ("ps_a", "ps_b")}, "k1_v")
        with nc.Block() as blk:
            @blk.vector
            def _(v):
                v.tensor_copy(t["colA"][0:8, 1:1024:2], t["ps_a"][0:8, 0:512])
                v.tensor_copy(t["colA"][0:8, 1025:2048:2], t["ps_b"][0:8, 0:512])

    def _conv_block(self, items, normbanks, tag):
        nc, t = self.nc, self.t
        s_ld, s_ve, s_act, s_st = t["s_ld"], t["s_ve"], t["s_act"], t["s_st"]
        n_it = len(items)
        ld_b, ve_b, act_b, st_b = s_ld.n, s_ve.n, s_act.n, s_st.n
        act_after = []
        a = act_b
        for (kind, stn, sti, cwc, h, slot) in items:
            a += 1
            if kind in ("q", "k0", "k1"):
                a += 1
            act_after.append(a)
        s_act.n = a
        cwname = {"q": "cq_a", "k0": "ck_a", "k1": "ck_a", "v0": "cv_a", "v1": "cv_a"}

        with nc.Block() as blk:
            @blk.sync
            def _(s):
                for i, (kind, stn, sti, cwc, h, slot) in enumerate(items):
                    if i >= 2:
                        s.wait_ge(s_ve.h, ve_b + (i - 1))
                    s.dma_start(out=t["pscr"][:, (i % 2) * 1024:(i % 2) * 1024 + 1024],
                                in_=self.asc[stn].ap()[sti]).then_inc(s_ld.h, 16)
            s_ld.n += 16 * n_it

            @blk.vector
            def _(v):
                for i, (kind, stn, sti, cwc, h, slot) in enumerate(items):
                    v.wait_ge(s_ld.h, ld_b + 16 * (i + 1))
                    if i >= 2:
                        v.wait_ge(s_act.h, act_after[i - 2])
                    if kind == "v1":
                        pass
                    src = t["pscr"][:, (i % 2) * 1024:(i % 2) * 1024 + 1024]
                    acc = t["cacc"][:, (i % 2) * 1024:(i % 2) * 1024 + 1024]
                    wv = t[cwname[kind]]
                    v.tensor_scalar(acc[:, 3:1024], src[:, 0:1021],
                                    wv[:, cwc * 4:cwc * 4 + 1], None, ALU.mult)
                    for tau in range(1, 4):
                        v.scalar_tensor_tensor(acc[:, 3:1024], src[:, tau:tau + 1021],
                                               wv[:, cwc * 4 + tau:cwc * 4 + tau + 1],
                                               acc[:, 3:1024], ALU.mult, ALU.add)
                    v.tensor_copy(t["c6"][:, 0:3], t["halo"][:, slot * 3:slot * 3 + 3])
                    v.tensor_copy(t["c6"][:, 3:6], src[:, 0:3])
                    v.tensor_scalar(acc[:, 0:3], t["c6"][:, 0:3],
                                    wv[:, cwc * 4:cwc * 4 + 1], None, ALU.mult)
                    for tau in range(1, 4):
                        v.scalar_tensor_tensor(acc[:, 0:3], t["c6"][:, tau:tau + 3],
                                               wv[:, cwc * 4 + tau:cwc * 4 + tau + 1],
                                               acc[:, 0:3], ALU.mult, ALU.add)
                    v.engine_nop()
                    v.engine_nop().then_inc(s_ve.h, 1)
            s_ve.n += n_it

            @blk.scalar
            def _(sc):
                nst = 0
                for i, (kind, stn, sti, cwc, h, slot) in enumerate(items):
                    sc.wait_ge(s_ve.h, ve_b + i + 1)
                    if kind == "v0" and h >= 1:
                        sc.wait_ge(s_st.h, st_b + 16 * h)
                    acc = t["cacc"][:, (i % 2) * 1024:(i % 2) * 1024 + 1024]
                    sc.activation(out_slice_ := out_slice_fn(kind, h, t), acc, AF.Silu).then_inc(s_act.h, 1)
                    if kind in ("q", "k0", "k1"):
                        sc.activation(t["sqs"][:, (i % 2) * 1024:(i % 2) * 1024 + 1024],
                                      out_slice_, AF.Square).then_inc(s_act.h, 1)
                    if kind == "v1":
                        sc.wait_ge(s_act.h, act_after[i])
                        sc.dma_start(out=self.asc["va"].ap()[h], in_=t["vscr"][:]).then_inc(s_st.h, 16)
                        nst += 1
            if any(k == "v1" for k, *_ in items):
                s_st.n += 16 * 8

            @blk.tensor
            def _(te):
                for i, (kind, stn, sti, cwc, h, slot) in enumerate(items):
                    if kind not in normbanks:
                        continue
                    b0, b1 = normbanks[kind]
                    first, last = (h == 0), (h == 7)
                    te.wait_ge(s_act.h, act_after[i])
                    for tb in range(2):
                        te.matmul(t[b0 if tb == 0 else b1][0:8, 0:512],
                                  t["Ecols"][:, 8 * h:8 * h + 8],
                                  t["sqs"][:, (i % 2) * 1024 + tb * 512:(i % 2) * 1024 + tb * 512 + 512],
                                  start=first, stop=last)

    def a_pads(self):
        nc, t = self.nc, self.t
        s_ld, s_ve, s_act, s_mm, s_g = t["s_ld"], t["s_ve"], t["s_act"], t["s_mm"], t["s_g"]
        LN128 = float(np.log(128.0))

        with nc.Block() as blk:
            @blk.sync
            def _(s):
                s.dma_start(out=t["bpadB"][:], in_=t["bpad"][8:16, :]).then_inc(s_ld.h, 16)
            s_ld.n += 16

            @blk.vector
            def _(v):
                v.tensor_copy(t["b2il"][0:8, 0:2048:2], t["bpad"][0:8, :])
                v.wait_ge(s_ld.h, s_ld.n)
                v.tensor_copy(t["b2il"][0:8, 1:2048:2], t["bpadB"][:])
                v.memset(t["rowA"][0:8, :], 0.0)
                v.tensor_copy(t["rowA"][0:8, 0:2048:2], t["grow"][0:8, :])
                v.memset(t["Lp"][0:8, :], 0.0)
                v.tensor_tensor_scan(t["Lp"][0:8, 1:2049], t["rowA"][0:8, 0:2048],
                                     t["rowA"][0:8, 0:2048], 0.0, ALU.add, ALU.bypass)
                v.engine_nop()
                v.engine_nop().then_inc(s_ve.h, 1)
            s_ve.n += 1

        with nc.Block() as blk:
            @blk.vector
            def _(v):
                v.tensor_copy(t["rows2"][0:8, 32:48], t["Lp"][0:8, 128:2049:128])
                v.tensor_scalar_mul(t["rows2"][0:8, 16:32], t["Lp"][0:8, 0:1921:128], -1.0)
                v.engine_nop()
                v.engine_nop()
                v.engine_nop()
                v.tensor_add(t["rows2"][0:8, 48:64], t["rows2"][0:8, 32:48], t["rows2"][0:8, 16:32])
                v.tensor_scalar_max(t["rowAq"][0:8, :], t["rowAq"][0:8, :], 1e-24)
                v.tensor_scalar_max(t["colA"][0:8, :], t["colA"][0:8, :], 1e-24)
                v.engine_nop()
                v.engine_nop().then_inc(s_ve.h, 1)
            s_ve.n += 1

            @blk.scalar
            def _(sc):
                sc.wait_ge(s_ve.h, s_ve.n)
                sc.activation(t["rows2"][0:8, 0:16], t["rows2"][0:8, 48:64], AF.Exp)
                sc.activation(t["grow"][0:8, :], t["rowAq"][0:8, :], AF.Ln)
                sc.activation(t["cacc"][0:8, 0:2048], t["colA"][0:8, :], AF.Ln).then_inc(s_act.h, 1)
            s_act.n += 1

            @blk.gpsimd
            def _(g):
                g.wait_ge(s_act.h, s_act.n)
                g.engine_nop()
                g.engine_nop().then_inc(s_g.h, 1)
            s_g.n += 1

        with nc.Block() as blk:
            @blk.scalar
            def _(sc):
                sc.activation(t["rowA"][0:8, :], t["b2il"][0:8, :], AF.Ln).then_inc(s_act.h, 1)
            s_act.n += 1

            @blk.vector
            def _(v):
                v.scalar_tensor_tensor(t["rowAq"][0:8, :], t["grow"][0:8, :], -0.5,
                                       t["Lp"][0:8, 2:2049:2], ALU.mult, ALU.add)
                v.tensor_scalar_add(t["rowAq"][0:8, :], t["rowAq"][0:8, :], -0.5 * LN128)
                v.scalar_tensor_tensor(t["colA"][0:8, :], t["cacc"][0:8, 0:2048], -0.5,
                                       t["Lp"][0:8, 1:2049], ALU.mult, ALU.subtract)
                v.wait_ge(s_act.h, s_act.n)
                v.tensor_add(t["rowA"][0:8, :], t["rowA"][0:8, :], t["colA"][0:8, :])
                v.engine_nop()
                v.engine_nop()
                v.scalar_tensor_tensor(t["rowA"][0:8, :], t["Lp"][0:8, 1:2049], 2.0,
                                       t["rowA"][0:8, :], ALU.mult, ALU.add)

        with nc.Block() as blk:
            @blk.sync
            def _(s):
                s.dma_start(out=t["rowAqfl"][0:1, 512:640], in_=t["rows2"][0:8, 0:16]).then_inc(s_ld.h, 16)
            s_ld.n += 16

            @blk.tensor
            def _(te):
                te.wait_ge(s_ld.h, s_ld.n)
                te.matmul(t["ps_row"][:, 0:128], t["ones_rowf"][:], t["rowAqfl"][0:1, 512:640],
                          start=True, stop=True).then_inc(s_mm.h, 1)
            s_mm.n += 1

            @blk.vector
            def _(v):
                v.wait_ge(s_mm.h, s_mm.n)
                v.tensor_copy(t["lamCbc"][:], t["ps_row"][:, 0:128])

        with nc.Block() as blk:
            act_b, mm_b, ve_b = s_act.n, s_mm.n, s_ve.n

            @blk.scalar
            def _(sc):
                for c in range(NAC):
                    if c >= 1:
                        sc.wait_ge(s_mm.h, mm_b + 6 * c)
                    sc.activation(t["grow"][0:8, 0:128], t["rowA"][0:8, c * 128:(c + 1) * 128],
                                  AF.Exp, bias=t["rows2"][0:8, 16 + c:17 + c])
                    sc.activation(t["grow"][0:8, 128:256], t["colA"][0:8, c * 128:(c + 1) * 128],
                                  AF.Exp, bias=t["rows2"][0:8, 32 + c:33 + c])
                    sc.activation(t["grow"][0:8, 256:320], t["rowAq"][0:8, c * 64:(c + 1) * 64],
                                  AF.Exp, bias=t["rows2"][0:8, 16 + c:17 + c]).then_inc(s_act.h, 1)
            s_act.n += NAC

            @blk.tensor
            def _(te):
                for c in range(NAC):
                    te.wait_ge(s_act.h, act_b + c + 1)
                    if c >= 1:
                        te.wait_ge(s_ve.h, ve_b + 6 * c)
                    srcs = [
                        (t["rowA"][0:8, c * 128:(c + 1) * 128], "ps_c", False),
                        (t["colA"][0:8, c * 128:(c + 1) * 128], "ps_row", False),
                        (t["b2il"][0:8, c * 128:(c + 1) * 128], "ps_t0", True),
                        (t["grow"][0:8, 0:128], "ps_c", False),
                        (t["grow"][0:8, 128:256], "ps_row", False),
                        (t["grow"][0:8, 256:320], "ps_c", False),
                    ]
                    for qi, (ap, bank, isbf) in enumerate(srcs):
                        ident = t["ident"][0:8, 0:8] if isbf else t["identf"][0:8, 0:8]
                        nrow = 64 if qi == 5 else 128
                        te.wait_ge(s_ve.h, ve_b + 6 * c + max(0, qi - 1))
                        te.transpose(t[bank][0:nrow, 0:8], ap, ident).then_inc(s_mm.h, 1)
            s_mm.n += 6 * NAC

            @blk.vector
            def _(v):
                for c in range(NAC):
                    for qi in range(6):
                        v.wait_ge(s_mm.h, mm_b + 6 * c + qi + 1)
                        bank = ["ps_c", "ps_row", "ps_t0", "ps_c", "ps_row", "ps_c"][qi]
                        nrow = 64 if qi == 5 else 128
                        v.tensor_copy(t["toks"][0:nrow, qi * 128 + c * 8: qi * 128 + c * 8 + 8],
                                      t[bank][0:nrow, 0:8])
                        v.engine_nop()
                        v.engine_nop().then_inc(s_ve.h, 1)
            s_ve.n += 6 * NAC

    def a_pass1(self):
        import os
        NACL = int(os.environ.get("ATTNA_NAC", NAC))
        nc, t = self.nc, self.t
        s_ld, s_ve, s_act, s_mm, s_st, s_g = (t["s_ld"], t["s_ve"], t["s_act"],
                                              t["s_mm"], t["s_st"], t["s_g"])
        NQ = NAC * HA

        with nc.Block() as blk:
            @blk.vector
            def _(v):
                for h in range(HA):
                    v.tensor_copy(t["FM"][:, h * 128:(h + 1) * 128], t["identf"][:])
                    v.tensor_copy(t["FMbf"][:, h * 128:(h + 1) * 128], t["identf"][:])
                v.memset(t["FM"][:, 1024:2048], 0.0)
                v.memset(t["FMbf"][:, 1024:2048], 0.0)

        ev = {}
        cmm = s_mm.n
        cve = s_ve.n
        cact = s_act.n
        cld = s_ld.n
        cst = t["s_x"].n
        cg = s_g.n
        for c in range(NACL):
            cld += 16; ev[(c, "rowfl")] = cld
            cmm += 2; ev[(c, "rbc")] = cmm
            for hp in range(HA // 2):
                qq = (c * HA + 2 * hp, c * HA + 2 * hp + 1)
                for q in qq:
                    cld += 16; ev[(q, "vload")] = cld
                for q in qq:
                    cmm += 1; ev[(q, "ktr")] = cmm
                    cmm += 1; ev[(q, "vtr")] = cmm
                for q in qq:
                    cmm += 1; ev[(q, "kk")] = cmm
                for q in qq:
                    cmm += 1; ev[(q, "r0tr")] = cmm
                for j in range(1, 7):
                    for q in qq:
                        cmm += 1; ev[(q, f"sq{j}")] = cmm
                for j in range(5, 0, -1):
                    for q in qq:
                        cmm += 1; ev[(q, f"g{j}")] = cmm
                for name in ("tfin", "v0", "w2t", "w2tok", "pct", "mc", "pm", "pf"):
                    for q in qq:
                        cmm += 1; ev[(q, name)] = cmm
                for q in qq:
                    cve += 1; ev[(q, "ktil")] = cve
                    cve += 1; ev[(q, "bv")] = cve
                for q in qq:
                    cve += 1; ev[(q, "arg")] = cve
                for q in qq:
                    cve += 1; ev[(q, "abf")] = cve
                for q in qq:
                    cve += 1; ev[(q, "r0")] = cve
                for j in range(1, 7):
                    for q in qq:
                        cve += 1; ev[(q, f"sev{j}")] = cve
                for q in qq:
                    cve += 1; ev[(q, "ginit")] = cve
                for j in range(5, 0, -1):
                    for q in qq:
                        cve += 1; ev[(q, f"gev{j}")] = cve
                for name in ("tev", "v0ev", "w2tokev", "mcomb", "fcomb"):
                    for q in qq:
                        cve += 1; ev[(q, name)] = cve
                for q in qq:
                    cact += 1; ev[(q, "w2k")] = cact
                for q in qq:
                    cact += 1; ev[(q, "exp")] = cact
                for j in range(1, 6):
                    for q in qq:
                        cact += 1; ev[(q, f"rev{j}")] = cact
                for q in qq:
                    cact += 1; ev[(q, "w2tev")] = cact
                for q in qq:
                    cact += 1; ev[(q, "pctev")] = cact
                for q in qq:
                    cg += 1; ev[(q, "argdone")] = cg
                for q in qq:
                    cst += 16; ev[(q, "vwst")] = cst
        self._p1ev = ev

        def tk(q, name):
            return ev[(q, name)]

        def bank(p):
            return t["ps_a" if p == 0 else "ps_b"]

        def cbank(p):
            return t["ps_c" if p == 0 else "ps_row"]

        def tbank(p, which):
            return t["ps_t0" if which == 0 else "ps_t1"]

        def sol(p, slot):
            return t["sol"][:, (p * 10 + slot) * 128:(p * 10 + slot) * 128 + 128]

        def abf(p, i):
            return t["abf"][:, (p * 2 + i) * 128:(p * 2 + i) * 128 + 128]

        def earg(p, i):
            return t["earg"][:, (p * 2 + i) * 128:(p * 2 + i) * 128 + 128]

        def ktk(p, i):
            return t["ktk"][:, (p * 2 + i) * 128:(p * 2 + i) * 128 + 128]

        def vw(p, i):
            return t["vw"][:, (p * 2 + i) * 128:(p * 2 + i) * 128 + 128]

        def tokcol(qty, c, h, nrow=128):
            return t["toks"][0:nrow, qty * 128 + c * 8 + h: qty * 128 + c * 8 + h + 1]

        def lamcol(c, h):
            return t["lamCbc"][:, h * 16 + c: h * 16 + c + 1]

        with nc.Block() as blk:
            @blk.sync
            def _(s):
                for c in range(NACL):
                    if c >= 2:
                        s.wait_ge(s_mm.h, tk(c - 1, "rbc"))
                    s.dma_start(out=t["rowAfl"][0:1, (c % 2) * 1024:(c % 2) * 1024 + 1024],
                                in_=t["rowA"][0:8, c * 128:(c + 1) * 128]).then_inc(s_ld.h, 16)
                    for h in range(HA):
                        q = c * HA + h
                        if q >= 2:
                            s.wait_ge(s_mm.h, tk(q - 2, "vtr"))
                        s.dma_start(out=t["vin"][:, (q % 2) * 128:(q % 2) * 128 + 128],
                                    in_=self.asc["va"].ap()[h][:, c * 128:(c + 1) * 128]).then_inc(s_ld.h, 16)
            s_ld.n = cld
            t["s_x"].n = cst

            @blk.tensor
            def _(te):
                for c in range(NACL):
                    te.wait_ge(s_ld.h, tk(c, "rowfl"))
                    if c >= 1:
                        te.wait_ge(s_g.h, tk(c * HA - 1, "argdone"))
                    for half in range(2):
                        te.matmul(t["ps_p0" if half == 0 else "ps_p1"][:, 0:512],
                                  t["ones_rowf"][:],
                                  t["rowAfl"][0:1, (c % 2) * 1024 + half * 512:(c % 2) * 1024 + half * 512 + 512],
                                  start=True, stop=True).then_inc(s_mm.h, 1)
                    for hp in range(HA // 2):
                        qq = (c * HA + 2 * hp, c * HA + 2 * hp + 1)
                        for q in qq:
                            p = q % 2
                            h = q - c * HA
                            if q >= 1:
                                te.wait_ge(s_act.h, tk(q - 1, "w2k"))
                            te.transpose(tbank(p, 0)[:, 0:128],
                                         t["ak"][:, h * 2048 + c * 128:h * 2048 + (c + 1) * 128],
                                         t["ident"][:]).then_inc(s_mm.h, 1)
                            te.wait_ge(s_ld.h, tk(q, "vload"))
                            if q >= 1:
                                te.wait_ge(s_ve.h, tk(q - 1, "r0" if p == 0 else "bv"))
                            te.transpose(tbank(p, 1)[:, 0:128],
                                         t["vin"][:, (q % 2) * 128:(q % 2) * 128 + 128],
                                         t["ident"][:]).then_inc(s_mm.h, 1)
                        for q in qq:
                            p = q % 2
                            h = q - c * HA
                            if q >= 2:
                                te.wait_ge(s_ve.h, tk(q - 2, "fcomb"))
                            te.matmul(bank(p)[:, 0:128],
                                      t["ak"][:, h * 2048 + c * 128:h * 2048 + (c + 1) * 128],
                                      t["ak"][:, h * 2048 + c * 128:h * 2048 + (c + 1) * 128],
                                      start=True, stop=True).then_inc(s_mm.h, 1)
                        for q in qq:
                            p = q % 2
                            te.wait_ge(s_ve.h, tk(q, "abf") if p == 0 else tk(q - 1, "r0"))
                            te.transpose(tbank(p, 1)[:, 0:128], abf(p, 0), t["ident"][:]).then_inc(s_mm.h, 1)
                        for j in range(1, 7):
                            for q in qq:
                                p = q % 2
                                te.wait_ge(s_ve.h, tk(q, "r0" if j == 1 else f"sev{j-1}"))
                                if j > 1:
                                    te.wait_ge(s_act.h, tk(q, f"rev{j-1}"))
                                Sprev = abf(p, 0) if j == 1 else sol(p, 8)
                                Rprev = abf(p, 1) if j == 1 else sol(p, j - 2)
                                ins = te.matmul(bank(p)[:, 0:128], Rprev, Sprev, start=True, stop=True)
                                if j < 6:
                                    ins = te.matmul(cbank(p)[:, 0:128], Sprev, Rprev, start=True, stop=True)
                                ins.then_inc(s_mm.h, 1)
                        for j in range(5, 0, -1):
                            for q in qq:
                                p = q % 2
                                te.wait_ge(s_ve.h, tk(q, "ginit" if j == 5 else f"gev{j+1}"))
                                te.wait_ge(s_act.h, tk(q, f"rev{j}"))
                                te.matmul(bank(p)[:, 0:128], sol(p, j - 1), sol(p, 6),
                                          start=True, stop=True).then_inc(s_mm.h, 1)
                        for q in qq:
                            p = q % 2
                            te.wait_ge(s_ve.h, tk(q, "gev1"))
                            te.matmul(bank(p)[:, 0:128], abf(p, 1), sol(p, 6),
                                      start=True, stop=True).then_inc(s_mm.h, 1)
                        for q in qq:
                            p = q % 2
                            te.wait_ge(s_ve.h, tk(q, "tev"))
                            te.wait_ge(s_act.h, tk(q, "rev5"))
                            if q >= 2:
                                te.wait_ge(t["s_x"].h, tk(q - 2, "vwst"))
                                te.wait_ge(s_ve.h, tk(q - 2, "mcomb"))
                            te.matmul(cbank(p)[:, 0:128], sol(p, 7), t["bvt"][:, p * 128:p * 128 + 128],
                                      start=True, stop=True).then_inc(s_mm.h, 1)
                        for q in qq:
                            p = q % 2
                            te.wait_ge(s_ve.h, tk(q, "v0ev"))
                            te.matmul(cbank(p)[:, 0:128], ktk(p, 1), sol(p, 7),
                                      start=True, stop=True).then_inc(s_mm.h, 1)
                        for q in qq:
                            p = q % 2
                            te.wait_ge(s_act.h, tk(q, "w2tev"))
                            te.matmul(cbank(p)[:, 0:128], sol(p, 7), ktk(p, 1),
                                      start=True, stop=True).then_inc(s_mm.h, 1)
                        for q in qq:
                            p = q % 2
                            te.wait_ge(s_ve.h, tk(q, "w2tokev"))
                            te.matmul(cbank(p)[:, 0:128], abf(p, 0), ktk(p, 0),
                                      start=True, stop=True).then_inc(s_mm.h, 1)
                        for q in qq:
                            p = q % 2
                            te.wait_ge(s_act.h, tk(q, "pctev"))
                            te.matmul(cbank(p)[:, 0:128], ktk(p, 0), vw(p, 0),
                                      start=True, stop=True).then_inc(s_mm.h, 1)
                        for q in qq:
                            p = q % 2
                            h = q - c * HA
                            te.matmul(bank(p)[:, 0:128], sol(p, 8), t["FMbf"][:, 1024 + h * 128:1024 + (h + 1) * 128],
                                      start=True, stop=True).then_inc(s_mm.h, 1)
                        for q in qq:
                            p = q % 2
                            h = q - c * HA
                            te.wait_ge(s_ve.h, tk(q, "mcomb"))
                            te.matmul(bank(p)[:, 0:128], sol(p, 8), t["FMbf"][:, h * 128:(h + 1) * 128],
                                      start=True, stop=True).then_inc(s_mm.h, 1)
            s_mm.n = cmm

            @blk.vector
            def _(v):
                for c in range(NACL):
                    for hp in range(HA // 2):
                        qq = (c * HA + 2 * hp, c * HA + 2 * hp + 1)
                        for q in qq:
                            p = q % 2
                            h = q - c * HA
                            v.wait_ge(s_mm.h, tk(q, "ktr"))
                            v.tensor_scalar(ktk(p, 0), tbank(p, 0)[:, 0:128], tokcol(4, c, h), None, ALU.mult)
                            v.engine_nop()
                            v.engine_nop().then_inc(s_ve.h, 1)
                            v.wait_ge(s_mm.h, tk(q, "vtr"))
                            v.tensor_scalar(t["bvt"][:, p * 128:p * 128 + 128], tbank(p, 1)[:, 0:128],
                                            tokcol(2, c, h), None, ALU.mult)
                            v.engine_nop()
                            v.engine_nop().then_inc(s_ve.h, 1)
                        for q in qq:
                            p = q % 2
                            h = q - c * HA
                            v.wait_ge(s_mm.h, tk(c, "rbc"))
                            rows = t["ps_p0" if h < 4 else "ps_p1"]
                            v.tensor_add(earg(p, 0), rows[:, (h % 4) * 128:(h % 4) * 128 + 128], t["maskS"][:])
                            v.engine_nop()
                            v.engine_nop().then_inc(s_ve.h, 1)
                        for q in qq:
                            p = q % 2
                            v.wait_ge(s_act.h, tk(q, "exp"))
                            v.wait_ge(s_mm.h, tk(q, "kk"))
                            v.tensor_mul(abf(p, 0), bank(p)[:, 0:128], earg(p, 1))
                            v.engine_nop()
                            v.engine_nop().then_inc(s_ve.h, 1)
                        for q in qq:
                            p = q % 2
                            v.wait_ge(s_mm.h, tk(q, "r0tr"))
                            v.tensor_copy(abf(p, 1), tbank(p, 1)[:, 0:128])
                            v.engine_nop()
                            v.engine_nop().then_inc(s_ve.h, 1)
                        for j in range(1, 7):
                            for q in qq:
                                p = q % 2
                                v.wait_ge(s_mm.h, tk(q, f"sq{j}"))
                                v.tensor_copy(sol(p, 8), bank(p)[:, 0:128])
                                v.engine_nop()
                                v.engine_nop().then_inc(s_ve.h, 1)
                        for q in qq:
                            p = q % 2
                            v.tensor_add(sol(p, 6), sol(p, 8), t["ident"][:])
                            v.engine_nop()
                            v.engine_nop().then_inc(s_ve.h, 1)
                        for j in range(5, 0, -1):
                            for q in qq:
                                p = q % 2
                                v.wait_ge(s_mm.h, tk(q, f"g{j}"))
                                v.tensor_add(sol(p, 6), sol(p, 6), bank(p)[:, 0:128])
                                v.engine_nop()
                                v.engine_nop().then_inc(s_ve.h, 1)
                        for q in qq:
                            p = q % 2
                            v.wait_ge(s_mm.h, tk(q, "tfin"))
                            v.tensor_sub(sol(p, 7), sol(p, 6), bank(p)[:, 0:128])
                            v.engine_nop()
                            v.engine_nop().then_inc(s_ve.h, 1)
                        for q in qq:
                            p = q % 2
                            v.wait_ge(s_mm.h, tk(q, "v0"))
                            v.tensor_copy(vw(p, 0), cbank(p)[:, 0:128])
                            v.engine_nop()
                            v.engine_nop().then_inc(s_ve.h, 1)
                        for q in qq:
                            p = q % 2
                            v.wait_ge(s_mm.h, tk(q, "w2tok"))
                            v.tensor_copy(abf(p, 0), cbank(p)[:, 0:128])
                            v.engine_nop()
                            v.engine_nop().then_inc(s_ve.h, 1)
                        for q in qq:
                            p = q % 2
                            h = q - c * HA
                            v.wait_ge(s_mm.h, tk(q, "pm"))
                            v.scalar_tensor_tensor(t["FM"][:, 1024 + h * 128:1024 + (h + 1) * 128],
                                                   t["FM"][:, 1024 + h * 128:1024 + (h + 1) * 128],
                                                   lamcol(c, h), bank(p)[:, 0:128], ALU.mult, ALU.subtract)
                            v.engine_nop()
                            v.engine_nop()
                            v.tensor_add(t["FM"][:, 1024 + h * 128:1024 + (h + 1) * 128],
                                         t["FM"][:, 1024 + h * 128:1024 + (h + 1) * 128],
                                         cbank(p)[:, 0:128])
                            v.engine_nop()
                            v.engine_nop()
                            v.tensor_copy(t["FMbf"][:, 1024 + h * 128:1024 + (h + 1) * 128],
                                          t["FM"][:, 1024 + h * 128:1024 + (h + 1) * 128])
                            v.engine_nop()
                            v.engine_nop().then_inc(s_ve.h, 1)
                        for q in qq:
                            p = q % 2
                            h = q - c * HA
                            v.wait_ge(s_mm.h, tk(q, "pf"))
                            v.scalar_tensor_tensor(t["FM"][:, h * 128:(h + 1) * 128],
                                                   t["FM"][:, h * 128:(h + 1) * 128],
                                                   lamcol(c, h), bank(p)[:, 0:128], ALU.mult, ALU.subtract)
                            v.engine_nop()
                            v.engine_nop()
                            v.tensor_copy(t["FMbf"][:, h * 128:(h + 1) * 128], t["FM"][:, h * 128:(h + 1) * 128])
                            v.engine_nop()
                            v.engine_nop().then_inc(s_ve.h, 1)
            s_ve.n = cve

            @blk.scalar
            def _(sc):
                for c in range(NACL):
                    for hp in range(HA // 2):
                        qq = (c * HA + 2 * hp, c * HA + 2 * hp + 1)
                        for q in qq:
                            p = q % 2
                            h = q - c * HA
                            sc.wait_ge(s_ve.h, tk(q, "ktil"))
                            sc.activation(ktk(p, 1), tbank(p, 0)[:, 0:128], AF.Copy,
                                          scale=tokcol(3, c, h)).then_inc(s_act.h, 1)
                        for q in qq:
                            p = q % 2
                            h = q - c * HA
                            sc.wait_ge(s_ve.h, tk(q, "arg"))
                            sc.activation(earg(p, 1), earg(p, 0), AF.Exp,
                                          bias=tokcol(1, c, h)).then_inc(s_act.h, 1)
                        for j in range(1, 6):
                            for q in qq:
                                p = q % 2
                                sc.wait_ge(s_mm.h, tk(q, f"sq{j}"))
                                sc.activation(sol(p, j - 1), cbank(p)[:, 0:128], AF.Copy).then_inc(s_act.h, 1)
                        for q in qq:
                            p = q % 2
                            sc.wait_ge(s_mm.h, tk(q, "w2t"))
                            sc.activation(vw(p, 1), cbank(p)[:, 0:128], AF.Copy).then_inc(s_act.h, 1)
                        for q in qq:
                            p = q % 2
                            sc.wait_ge(s_mm.h, tk(q, "pct"))
                            sc.activation(sol(p, 8), cbank(p)[:, 0:128], AF.Copy).then_inc(s_act.h, 1)
            s_act.n = cact

            @blk.gpsimd
            def _(g):
                for c in range(NACL):
                    for hp in range(HA // 2):
                        qq = (c * HA + 2 * hp, c * HA + 2 * hp + 1)
                        for q in qq:
                            g.wait_ge(s_ve.h, tk(q, "arg"))
                            g.engine_nop()
                            g.engine_nop().then_inc(s_g.h, 1)
                        for q in qq:
                            g.wait_ge(s_act.h, tk(q, "w2tev"))
                            g.dma_start(out=self.asc["vw"].ap()[q],
                                        in_=t["vw"][:, (q % 2) * 256:(q % 2) * 256 + 256]).then_inc(t["s_x"].h, 16)
            s_g.n = cg

    def a_relay(self):
        nc, t = self.nc, self.t
        s_mm, s_ve, s_st, s_cc, s_ld = t["s_mm"], t["s_ve"], t["s_st"], t["s_cc"], t["s_ld"]
        with nc.Block() as blk:
            mm_b, ve_b = s_mm.n, s_ve.n

            @blk.tensor
            def _(te):
                for h in range(HA):
                    if h >= 2:
                        te.wait_ge(s_ve.h, ve_b + h - 1)
                    te.transpose(t["ps_c" if h % 2 == 0 else "ps_row"][:, 0:128],
                                 t["FM"][:, h * 128:(h + 1) * 128], t["identf"][:]).then_inc(s_mm.h, 1)
            s_mm.n += HA

            @blk.vector
            def _(v):
                for h in range(HA):
                    v.wait_ge(s_mm.h, mm_b + h + 1)
                    v.tensor_copy(t["relst"][:, h * 128:(h + 1) * 128],
                                  t["ps_c" if h % 2 == 0 else "ps_row"][:, 0:128])
                    v.engine_nop()
                    v.engine_nop().then_inc(s_ve.h, 1)
                    v.tensor_copy(t["relst"][:, 1024 + h * 128:1024 + (h + 1) * 128],
                                  t["FM"][:, 1024 + h * 128:1024 + (h + 1) * 128])
            s_ve.n += HA

        with nc.Block() as blk:
            @blk.sync
            def _(s):
                s.dma_start(out=self.cc["rel_a_in"].ap(), in_=t["relst"][:]).then_inc(s_st.h, 16)
            s_st.n += 16

            @blk.gpsimd
            def _(g):
                g.wait_ge(s_st.h, s_st.n)
                g.collective_compute(
                    "AllGather", ALU.bypass,
                    replica_groups=[[0, 1, 2, 3], [4, 5, 6, 7]],
                    ins=[self.cc["rel_a_in"].ap()],
                    outs=[self.cc["rel_a_out"].ap()],
                ).then_inc(s_cc.h, 1)
            s_cc.n += 1

        with nc.Block() as blk:
            @blk.vector
            def _(v):
                v.memset(t["Sin"][:], 0.0)
                v.memset(t["Sbf"][:], 0.0)
        for j in range(3):
            with nc.Block() as blk:
                ld_b, mm_b, ve_b = s_ld.n, s_mm.n, s_ve.n

                @blk.sync
                def _(s):
                    s.wait_ge(s_cc.h, s_cc.n)
                    s.dma_start(out=t["relst"][:],
                                in_=self.cc["rel_a_out"].ap()[j * 128:(j + 1) * 128, :]).then_inc(s_ld.h, 16)
                s_ld.n += 16

                @blk.tensor
                def _(te):
                    te.wait_ge(s_ld.h, ld_b + 16)
                    for h in range(HA):
                        if h >= 2:
                            te.wait_ge(s_ve.h, ve_b + h - 1)
                        te.matmul(t["ps_c" if h % 2 == 0 else "ps_row"][:, 0:128],
                                  t["relst"][:, h * 128:(h + 1) * 128],
                                  t["Sin"][:, h * 128:(h + 1) * 128],
                                  start=True, stop=True).then_inc(s_mm.h, 1)
                s_mm.n += HA

                @blk.vector
                def _(v):
                    for h in range(HA):
                        v.wait_ge(s_mm.h, mm_b + h + 1)
                        ps = t["ps_c" if h % 2 == 0 else "ps_row"][:, 0:128]
                        dcol = t["earg"][:, 0:128]
                        v.tensor_add(dcol, ps, t["relst"][:, 1024 + h * 128:1024 + (h + 1) * 128])
                        v.engine_nop()
                        v.engine_nop()
                        v.tensor_sub(dcol, dcol, t["Sin"][:, h * 128:(h + 1) * 128])
                        v.engine_nop()
                        v.engine_nop()
                        v.scalar_tensor_tensor(t["Sin"][:, h * 128:(h + 1) * 128], dcol,
                                               t["rmask"][:, j:j + 1],
                                               t["Sin"][:, h * 128:(h + 1) * 128],
                                               ALU.mult, ALU.add)
                        v.engine_nop()
                        v.engine_nop().then_inc(s_ve.h, 1)
                s_ve.n += HA
        with nc.Block() as blk:
            @blk.vector
            def _(v):
                v.tensor_copy(t["Sbf"][:], t["Sin"][:])

    def a_pass2(self):
        nc, t = self.nc, self.t
        s_ld, s_ve, s_act, s_mm, s_g = t["s_ld"], t["s_ve"], t["s_act"], t["s_mm"], t["s_g"]
        NQ = NAC * HA
        ev = {}
        cmm, cve, cact, cld, cg = s_mm.n, s_ve.n, s_act.n, s_ld.n, s_g.n
        for c in range(NAC):
            cld += 16; ev[(c, "qfl")] = cld
            cmm += 1; ev[(c, "qbc")] = cmm
            cld += 16; ev[(c, "gg")] = cld
            for h in range(HA):
                q = c * HA + h
                cld += 16; ev[(q, "vwin")] = cld
                cmm += 1; ev[(q, "w2s")] = cmm
                cve += 1; ev[(q, "vp")] = cve
                cmm += 1; ev[(q, "kq")] = cmm
                cve += 1; ev[(q, "qarg")] = cve
                cg += 1; ev[(q, "qargdone")] = cg
                cact += 1; ev[(q, "qexp")] = cact
                cve += 1; ev[(q, "aqbf")] = cve
                cmm += 1; ev[(q, "o12")] = cmm
                cve += 1; ev[(q, "ocmb")] = cve
                cact += 1; ev[(q, "osq")] = cact
                cve += 1; ev[(q, "ossum")] = cve
                cact += 1; ev[(q, "oexp")] = cact
                cact += 1; ev[(q, "gexp")] = cact
                cve += 1; ev[(q, "obf")] = cve
                cmm += 1; ev[(q, "otr")] = cmm
                cve += 1; ev[(q, "oev")] = cve
                cmm += 1; ev[(q, "ktr")] = cmm
                cve += 1; ev[(q, "ktil")] = cve
                cmm += 1; ev[(q, "smm")] = cmm
                cve += 1; ev[(q, "supd")] = cve
        self._p2ev = ev

        def tk(a, name):
            return ev[(a, name)]

        def bank(p):
            return t["ps_a" if p == 0 else "ps_b"]

        def obank(p):
            return t["ps_c" if p == 0 else "ps_row"]

        def tbank(p):
            return t["ps_t0" if p == 0 else "ps_t1"]

        def tokcol(qty, c, h, nrow=128):
            return t["toks"][0:nrow, qty * 128 + c * 8 + h: qty * 128 + c * 8 + h + 1]

        def lamcol(c, h):
            return t["lamCbc"][:, h * 16 + c: h * 16 + c + 1]

        def earg(p, i):
            return t["earg"][:, (p * 2 + i) * 128:(p * 2 + i) * 128 + 128]

        def abf(p, i):
            return t["abf"][:, (p * 2 + i) * 128:(p * 2 + i) * 128 + 128]

        def ktk(p, i):
            return t["ktk"][:, (p * 2 + i) * 128:(p * 2 + i) * 128 + 128]

        with nc.Block() as blk:
            @blk.sync
            def _(s):
                for c in range(NAC):
                    if c >= 2:
                        s.wait_ge(s_mm.h, tk(c - 1, "qbc"))
                    s.dma_start(out=t["rowAqfl"][0:1, (c % 2) * 512:(c % 2) * 512 + 512],
                                in_=t["rowAq"][0:8, c * 64:(c + 1) * 64]).then_inc(s_ld.h, 16)
                    if c >= 2:
                        s.wait_ge(s_ve.h, tk((c - 1) * HA - 1, "obf"))
                    s.dma_start(out=t["ggb"][0:64, (c % 2) * 1024:(c % 2) * 1024 + 1024],
                                in_=self.asc["gg"].ap()[c // 2][(c % 2) * 64:(c % 2) * 64 + 64, :]
                                ).then_inc(s_ld.h, 16)
                    for h in range(HA):
                        q = c * HA + h
                        if q >= 2:
                            s.wait_ge(s_ve.h, tk(q - 2, "vp"))
                        s.dma_start(out=t["vwin"][:, (q % 2) * 256:(q % 2) * 256 + 256],
                                    in_=self.asc["vw"].ap()[q]).then_inc(s_ld.h, 16)
            s_ld.n = cld

            @blk.tensor
            def _(te):
                for c in range(NAC):
                    te.wait_ge(s_ld.h, tk(c, "qfl"))
                    if c >= 1:
                        te.wait_ge(s_g.h, tk(c * HA - 1, "qargdone"))
                    te.matmul(t["ps_p0"][:, 0:512], t["ones_rowf"][:],
                              t["rowAqfl"][0:1, (c % 2) * 512:(c % 2) * 512 + 512],
                              start=True, stop=True).then_inc(s_mm.h, 1)
                    for h in range(HA):
                        q = c * HA + h
                        p = q % 2
                        ksl = t["ak"][:, h * 2048 + c * 128:h * 2048 + (c + 1) * 128]
                        qsl = t["aq"][:, h * 1024 + c * 64:h * 1024 + (c + 1) * 64]
                        te.wait_ge(s_ld.h, tk(q, "vwin"))
                        if q >= 2:
                            te.wait_ge(s_ve.h, tk(q - 2, "supd"))
                        te.matmul(bank(p)[:, 0:128],
                                  t["vwin"][:, (q % 2) * 256 + 128:(q % 2) * 256 + 256],
                                  t["Sbf"][:, h * 128:(h + 1) * 128],
                                  start=True, stop=True).then_inc(s_mm.h, 1)
                        te.wait_ge(s_ve.h, tk(q, "vp"))
                        if q >= 2:
                            te.wait_ge(s_ve.h, tk(q - 2, "aqbf"))
                        te.matmul(bank(p)[:, 128:192], ksl, qsl, start=True, stop=True).then_inc(s_mm.h, 1)
                        te.wait_ge(s_ve.h, tk(q, "aqbf"))
                        if q >= 2:
                            te.wait_ge(s_ve.h, tk(q - 2, "ocmb"))
                        te.matmul(obank(p)[0:64, 0:128], abf(p, 0)[:, 0:64],
                                  t["vp2"][:, p * 128:p * 128 + 128], start=True, stop=True)
                        te.matmul(obank(p)[0:64, 128:256], qsl, t["Sbf"][:, h * 128:(h + 1) * 128],
                                  start=True, stop=True).then_inc(s_mm.h, 1)
                        te.wait_ge(s_ve.h, tk(q, "obf"))
                        if q >= 2:
                            te.wait_ge(s_ve.h, tk(q - 2, "oev"))
                        te.transpose(tbank(p)[0:128, 0:64], t["obf"][0:64, p * 128:p * 128 + 128],
                                     t["ident"][0:64, 0:64]).then_inc(s_mm.h, 1)
                        if q >= 2:
                            te.wait_ge(s_ve.h, tk(q - 2, "ktil"))
                        te.wait_ge(s_ve.h, tk(q, "oev"))
                        te.transpose(tbank(p)[0:128, 64:192], ksl, t["ident"][:]).then_inc(s_mm.h, 1)
                        te.wait_ge(s_ve.h, tk(q, "ktil"))
                        te.matmul(bank(p)[:, 256:384], ktk(p, 0), t["vp2"][:, p * 128:p * 128 + 128],
                                  start=True, stop=True).then_inc(s_mm.h, 1)
            s_mm.n = cmm

            @blk.vector
            def _(v):
                for c in range(NAC):
                    for h in range(HA):
                        q = c * HA + h
                        p = q % 2
                        v.wait_ge(s_mm.h, tk(q, "w2s"))
                        v.tensor_sub(t["vp2"][:, p * 128:p * 128 + 128],
                                     t["vwin"][:, (q % 2) * 256:(q % 2) * 256 + 128],
                                     bank(p)[:, 0:128])
                        v.engine_nop()
                        v.engine_nop().then_inc(s_ve.h, 1)
                        v.wait_ge(s_mm.h, tk(c, "qbc"))
                        v.tensor_add(earg(p, 0)[:, 0:64], t["ps_p0"][:, h * 64:(h + 1) * 64], t["maskQ"][:, 0:64])
                        v.engine_nop()
                        v.engine_nop().then_inc(s_ve.h, 1)
                        v.wait_ge(s_act.h, tk(q, "qexp"))
                        v.wait_ge(s_mm.h, tk(q, "kq"))
                        v.tensor_mul(abf(p, 0)[:, 0:64], bank(p)[:, 128:192], earg(p, 1)[:, 0:64])
                        v.engine_nop()
                        v.engine_nop().then_inc(s_ve.h, 1)
                        v.wait_ge(s_mm.h, tk(q, "o12"))
                        v.tensor_copy(t["otf"][0:64, p * 128:p * 128 + 128], obank(p)[0:64, 0:128])
                        v.engine_nop()
                        v.engine_nop()
                        v.scalar_tensor_tensor(t["otf"][0:64, p * 128:p * 128 + 128],
                                               obank(p)[0:64, 128:256], tokcol(5, c, h, 64),
                                               t["otf"][0:64, p * 128:p * 128 + 128], ALU.mult, ALU.add)
                        v.engine_nop()
                        v.engine_nop().then_inc(s_ve.h, 1)
                        v.wait_ge(s_act.h, tk(q, "osq"))
                        v.reduce_sum(t["rmsc"][0:64, p * 4:p * 4 + 1], t["gtf"][0:64, p * 128:p * 128 + 128],
                                     axis=mybir.AxisListType.X)
                        v.engine_nop()
                        v.engine_nop().then_inc(s_ve.h, 1)
                        v.wait_ge(s_act.h, tk(q, "gexp"))
                        ggsl = t["ggb"][0:64, (c % 2) * 1024 + h * 128:(c % 2) * 1024 + (h + 1) * 128]
                        gsl = t["gtf"][0:64, p * 128:p * 128 + 128]
                        osl = t["otf"][0:64, p * 128:p * 128 + 128]
                        v.tensor_scalar_add(gsl, gsl, 1.0)
                        v.engine_nop()
                        v.engine_nop()
                        v.reciprocal(gsl, gsl)
                        v.engine_nop()
                        v.engine_nop()
                        v.tensor_mul(gsl, gsl, ggsl)
                        v.engine_nop()
                        v.engine_nop()
                        v.tensor_mul(gsl, gsl, t["Wonw_bc"][0:64, :])
                        v.tensor_scalar(osl, osl, t["rmsc"][0:64, p * 4 + 1:p * 4 + 2], None, ALU.mult)
                        v.engine_nop()
                        v.engine_nop()
                        v.tensor_mul(t["obf"][0:64, p * 128:p * 128 + 128], osl, gsl)
                        v.engine_nop()
                        v.engine_nop().then_inc(s_ve.h, 1)
                        v.wait_ge(s_mm.h, tk(q, "otr"))
                        v.tensor_copy(self.ofm_tile(h)[:, c * 64:(c + 1) * 64], tbank(p)[0:128, 0:64])
                        v.engine_nop()
                        v.engine_nop().then_inc(s_ve.h, 1)
                        v.wait_ge(s_mm.h, tk(q, "ktr"))
                        v.tensor_scalar(ktk(p, 0), tbank(p)[0:128, 64:192], tokcol(4, c, h), None, ALU.mult)
                        v.engine_nop()
                        v.engine_nop().then_inc(s_ve.h, 1)
                        v.wait_ge(s_mm.h, tk(q, "smm"))
                        v.scalar_tensor_tensor(t["Sin"][:, h * 128:(h + 1) * 128],
                                               t["Sin"][:, h * 128:(h + 1) * 128], lamcol(c, h),
                                               bank(p)[:, 256:384], ALU.mult, ALU.add)
                        v.engine_nop()
                        v.engine_nop()
                        v.tensor_copy(t["Sbf"][:, h * 128:(h + 1) * 128], t["Sin"][:, h * 128:(h + 1) * 128])
                        v.engine_nop()
                        v.engine_nop().then_inc(s_ve.h, 1)
            s_ve.n = cve

            @blk.scalar
            def _(sc):
                for c in range(NAC):
                    for h in range(HA):
                        q = c * HA + h
                        p = q % 2
                        sc.wait_ge(s_ve.h, tk(q, "qarg"))
                        sc.activation(earg(p, 1)[:, 0:64], earg(p, 0)[:, 0:64], AF.Exp,
                                      bias=tokcol(1, c, h)).then_inc(s_act.h, 1)
                        sc.wait_ge(s_ve.h, tk(q, "ocmb"))
                        sc.activation(t["gtf"][0:64, p * 128:p * 128 + 128],
                                      t["otf"][0:64, p * 128:p * 128 + 128],
                                      AF.Square).then_inc(s_act.h, 1)
                        sc.wait_ge(s_ve.h, tk(q, "ossum"))
                        sc.activation(t["rmsc"][0:64, p * 4 + 2:p * 4 + 3],
                                      t["rmsc"][0:64, p * 4:p * 4 + 1], AF.Ln,
                                      bias=t["cst"][0:64, 0:1], scale=t["cst"][0:64, 1:2])
                        sc.activation(t["rmsc"][0:64, p * 4 + 1:p * 4 + 2],
                                      t["rmsc"][0:64, p * 4 + 2:p * 4 + 3], AF.Exp,
                                      scale=-0.5).then_inc(s_act.h, 1)
                        sc.wait_ge(s_ld.h, tk(c, "gg"))
                        sc.activation(t["gtf"][0:64, p * 128:p * 128 + 128],
                                      t["ggb"][0:64, (c % 2) * 1024 + h * 128:(c % 2) * 1024 + (h + 1) * 128],
                                      AF.Exp, scale=-1.0).then_inc(s_act.h, 1)
            s_act.n = cact

            @blk.gpsimd
            def _(g):
                for c in range(NAC):
                    for h in range(HA):
                        q = c * HA + h
                        g.wait_ge(s_ve.h, tk(q, "qarg"))
                        g.engine_nop()
                        g.engine_nop().then_inc(s_g.h, 1)
            s_g.n = cg

    def a_wo(self):
        nc, t = self.nc, self.t
        s_ld = t["s_ld"]
        with nc.Block() as blk:
            @blk.sync
            def _(s):
                for c in range(NCH):
                    s.dma_start(out=t["x_res"][:, c * TPC:(c + 1) * TPC],
                                in_=self.x_park.ap()[c * 128:(c + 1) * 128, :]).then_inc(s_ld.h, 16)
                s.wait_ge(s_ld.h, s_ld.n + 16 * NCH)
            s_ld.n += 16 * NCH

        def rhs(i, tb):
            return self.ofm_tile(i)[:, tb * 512:tb * 512 + 512]

        def evac(v, oc, tb, ps):
            xsl = t["x_res"][:, oc * TPC + tb * 512: oc * TPC + tb * 512 + 512]
            v.tensor_add(xsl, xsl, ps)
            v.tensor_copy(t["x_bf"][:, oc * TPC + tb * 512: oc * TPC + tb * 512 + 512], xsl)

        self.proj_lhsT("Wo_a", 16, 128, list(range(8)), rhs, evac)

    def attn_b(self):
        nc, t, d = self.nc, self.t, self.d
        K = 1024

        def at(name, shape, dtype, offk):
            off = int(offk * K)
            nbytes = int(np.prod(shape[1:])) * (2 if dtype == BF16 else 4)
            assert off + nbytes <= self.asize, (name, off, nbytes, self.asize)
            h = self.nc.alloc_sbuf_tensor_at(name, shape, dtype, offset=self.abase + off)
            self.t[name] = h
            return h

        at("qb", [128, 16 * 1024], BF16, 0)
        at("kb", [128, 16 * 1024], BF16, 32)
        at("evscb", [128, 4 * 512], BF16, 32)
        at("Mb", [128, 16 * 256], F32, 64)
        at("Sbb", [128, 16 * 256], BF16, 80)
        at("vbin", [128, 2 * 4096], BF16, 88)
        at("xpin", [128, 2 * 512], F32, 88)
        at("xout", [128, 2 * 512], F32, 92)
        at("ofmb", [128, 32 * 512], BF16, 104)
        at("convwb", [128, 2 * 4 * 2048], BF16, 104)
        at("relbuf", [128, 4096], F32, 104)
        at("eargb", [128, 2 * 128], F32, 136)
        at("abfb", [128, 2 * 128], BF16, 137)
        at("ktkb", [128, 2 * 128], BF16, 137.5)
        at("otfb", [128, 2 * 256], F32, 138)
        at("obfb", [128, 2 * 256], BF16, 140)
        at("Lbp", [16, 1056], F32, 141)
        at("toksb", [128, 4 * 128], F32, 145.25)
        at("lamCbb", [128, 128], F32, 147.25)
        at("rows2b", [16, 32], F32, 147.75)
        at("bfl2", [1, 1024], F32, 148)
        at("lbfl", [1, 128], F32, 152)
        at("pscrb", [128, 2 * 1056], BF16, 97)
        at("gateb", [16, 1024], F32, 64)
        at("halob", [128, 128], F32, 101.25)
        at("tailb", [128, 96], F32, 36)

        self.bsc = {
            "vb": nc.dram_tensor("vb_st", [NBC, 128, 4096], BF16),
            "qpb": nc.dram_tensor("qpb_st", [16, 128, 1024], BF16),
            "kpb": nc.dram_tensor("kpb_st", [16, 128, 1024], BF16),
            "xp2": nc.dram_tensor("xp2_st", [D, TPC], F32),
        }

        self.b_park_and_proj()
        self.b_halo()
        self.b_conv()
        self.tap("qbt", lambda: t["qb"][:], [128, 16 * 1024], BF16)
        self.tap("kbt", lambda: t["kb"][:], [128, 16 * 1024], BF16)
        self.b_pads()
        self.tap("Lbp", lambda: t["Lbp"][0:16, :], [16, 1056], F32)
        self.tap("toksb", lambda: t["toksb"][:], [128, 512], F32)
        self.tap("lamCbb", lambda: t["lamCbb"][:], [128, 128], F32)
        self.tap("rows2b", lambda: t["rows2b"][0:16, :], [16, 32], F32)
        self.b_pass1()
        self.tap("Mb", lambda: t["Mb"][:], [128, 4096], F32)
        self.b_relay()
        self.tap("Sinb", lambda: t["Mb"][:], [128, 4096], F32)
        self.b_pass2()
        self.b_restore()

    def b_park_and_proj(self):
        nc, t, d = self.nc, self.t, self.d
        s_pk, s_ev, s_st, s_act, s_mm = t["s_pk"], t["s_ev"], t["s_st"], t["s_act"], t["s_mm"]

        with nc.Block() as blk:
            @blk.scalar
            def _(se):
                for c in range(NCH):
                    se.dma_start(out=self.x_park.ap()[c * 128:(c + 1) * 128, :],
                                 in_=t["x_res"][:, c * TPC:(c + 1) * TPC]).then_inc(s_pk.h, 16)
                se.wait_ge(s_pk.h, s_pk.n + 16 * NCH)
            s_pk.n += 16 * NCH

        self.proj_rhs("Wv_b", 8, "vb_b", slabname="qb", evname="evscb")

        def rhs(i, tb):
            return t["xn"][:, i * TPC + tb * 512: i * TPC + tb * 512 + 512]

        def mk_evac(base_slot):
            def evac(v, oc, tb, ps):
                gi = oc * 2 + tb
                v.tensor_copy(t["evscb"][:, (gi % 4) * 512:(gi % 4) * 512 + 512], ps)
                if tb == 1:
                    v.tensor_copy(t["tailb"][:, (base_slot + oc) * 3:(base_slot + oc) * 3 + 3],
                                  ps[:, 509:512])
            return evac

        def mk_store(stname):
            def store(se, oc, tb, ev_t):
                gi = oc * 2 + tb
                se.wait_ge(s_ev.h, ev_t)
                se.dma_start(out=self.bsc[stname].ap()[oc][:, tb * 512:tb * 512 + 512],
                             in_=t["evscb"][:, (gi % 4) * 512:(gi % 4) * 512 + 512]).then_inc(s_st.h, 16)
            return store

        self.proj_lhsT("Wq_b", 16, 128, list(range(NCH)), rhs, mk_evac(0), store_fn=mk_store("qpb"))
        self.proj_lhsT("Wk_b", 16, 128, list(range(NCH)), rhs, mk_evac(16), store_fn=mk_store("kpb"))

        def evac_g(v, oc, tb, ps):
            pass

        def scalar_g(sc, oc, tb, mm_t):
            sc.wait_ge(t["s_mm"].h, mm_t)
            ps = t["ps_p0" if tb % 2 == 0 else "ps_p1"]
            sl = slice(tb * 512, tb * 512 + 512)
            sc.activation(t["gateb"][0:16, sl], ps[0:16, 0:512], AF.Sigmoid,
                          bias=t["gbcol"][:])
            sc.activation(t["Lbp"][0:16, 1 + tb * 512:1 + tb * 512 + 512], ps[32:48, 0:512],
                          AF.Relu, bias=t["bbncol"][:], scale=-1.0)
            sc.activation(t["lnscr"][0:16, sl], ps[32:48, 0:512], AF.Abs, bias=t["bbcol"][:])
            sc.activation(t["lnscr"][0:16, sl], t["lnscr"][0:16, sl], AF.Exp, scale=-1.0)
            sc.activation(t["lnscr"][0:16, sl], t["lnscr"][0:16, sl], AF.Ln,
                          bias=t["cst"][0:16, 2:3]).then_inc(s_act.h, 1)

        self.t["gbcol"] = self.nc.alloc_sbuf_tensor_at("gbcol", [16, 1], F32, offset=self.abase + int(147.875 * 1024))
        self.t["bbcol"] = self.nc.alloc_sbuf_tensor_at("bbcol", [16, 1], F32, offset=self.abase + int(147.90625 * 1024))
        self.t["bbncol"] = self.nc.alloc_sbuf_tensor_at("bbncol", [16, 1], F32, offset=self.abase + int(147.9375 * 1024))
        self.t["lnscr"] = self.nc.alloc_sbuf_tensor_at("lnscr", [16, 1024], BF16 if False else F32,
                                                       offset=self.abase + int(88 * 1024))
        with nc.Block() as blk:
            s_ld = t["s_ld"]

            @blk.sync
            def _(s):
                s.dma_start(out=t["gbcol"][:], in_=d["bg_b"].ap().rearrange("o h -> h o")).then_inc(s_ld.h, 16)
                s.dma_start(out=t["bbcol"][:], in_=d["bb_b"].ap().rearrange("o h -> h o")).then_inc(s_ld.h, 16)
                s.wait_ge(s_ld.h, s_ld.n + 32)
            s_ld.n += 32

            @blk.vector
            def _(v):
                v.wait_ge(s_ld.h, s_ld.n)
                v.tensor_scalar_mul(t["bbncol"][:], t["bbcol"][:], -1.0)

        self.proj_lhsT("Wgb_w", 1, 48, list(range(NCH)), rhs, evac_g, scalar_fn=scalar_g)
        s_act.n += 2

        with nc.Block() as blk:
            s_ve = t["s_ve"]
            n_act = s_act.n

            @blk.vector
            def _(v):
                v.wait_ge(s_act.h, n_act)
                v.tensor_add(t["Lbp"][0:16, 1:1025], t["Lbp"][0:16, 1:1025], t["lnscr"][0:16, :])
                v.engine_nop()
                v.engine_nop()
                v.tensor_scalar_mul(t["Lbp"][0:16, 1:1025], t["Lbp"][0:16, 1:1025], -1.0)
                v.memset(t["Lbp"][0:16, 0:1], 0.0)
                v.memset(t["Lbp"][0:16, 1025:1056], 0.0)
                v.engine_nop()
                v.engine_nop()
                v.tensor_tensor_scan(t["Lbp"][0:16, 1:1025], t["Lbp"][0:16, 1:1025],
                                     t["Lbp"][0:16, 1:1025], 0.0, ALU.add, ALU.bypass)

    def b_halo(self):
        nc, t = self.nc, self.t
        s_st, s_cc, s_ld = t["s_st"], t["s_cc"], t["s_ld"]
        with nc.Block() as blk:
            @blk.sync
            def _(s):
                s.dma_start(out=self.cc["halo_b_in"].ap(), in_=t["tailb"][:, 0:96]).then_inc(s_st.h, 16)
            s_st.n += 16

            @blk.gpsimd
            def _(g):
                g.wait_ge(s_st.h, s_st.n)
                g.collective_compute(
                    "AllGather", ALU.bypass,
                    replica_groups=[[0, 1, 2, 3], [4, 5, 6, 7]],
                    ins=[self.cc["halo_b_in"].ap()],
                    outs=[self.cc["halo_b_out"].ap()],
                ).then_inc(s_cc.h, 1)
            s_cc.n += 1

        with nc.Block() as blk:
            @blk.sync
            def _(s):
                s.wait_ge(s_cc.h, s_cc.n)
                for j in range(3):
                    s.dma_start(out=t["otfb"][:, j * 128:j * 128 + 96],
                                in_=self.cc["halo_b_out"].ap()[j * 128:(j + 1) * 128, :]).then_inc(s_ld.h, 16)
            s_ld.n += 48

            @blk.vector
            def _(v):
                v.memset(t["halob"][:], 0.0)
                v.wait_ge(s_ld.h, s_ld.n)
                for j in range(3):
                    v.engine_nop()
                    v.engine_nop()
                    v.scalar_tensor_tensor(t["halob"][:, 0:96],
                                           t["otfb"][:, j * 128:j * 128 + 96],
                                           t["rmask"][:, 3 + j:4 + j], t["halob"][:, 0:96],
                                           ALU.mult, ALU.add)

    def b_conv(self):
        nc, t, d = self.nc, self.t, self.d
        s_ld, s_ve, s_act, s_mm, s_w = t["s_ld"], t["s_ve"], t["s_act"], t["s_mm"], t["s_w"]
        w_base = s_w.n
        with nc.Block() as blk:
            @blk.gpsimd
            def _(g):
                g.dma_start(out=t["convwb"][:, 0:8192],
                            in_=d["convW_q_b"].ap().rearrange("t i o -> i t o")).then_inc(s_w.h, 16)
                g.dma_start(out=t["convwb"][:, 8192:16384],
                            in_=d["convW_k_b"].ap().rearrange("t i o -> i t o")).then_inc(s_w.h, 16)
            s_w.n += 32

        items = [("q", g) for g in range(16)] + [("k", g) for g in range(16)]
        ld_b, ve_b, act_b, mm_b = s_ld.n, s_ve.n, s_act.n, s_mm.n
        with nc.Block() as blk:
            @blk.sync
            def _(s):
                for i, (qk, g_) in enumerate(items):
                    if i >= 2:
                        s.wait_ge(s_mm.h, mm_b + (i - 1) * 2)
                    st = self.bsc["qpb" if qk == "q" else "kpb"]
                    s.dma_start(out=t["pscrb"][:, (i % 2) * 1056 + 3:(i % 2) * 1056 + 3 + 1024],
                                in_=st.ap()[g_]).then_inc(s_ld.h, 16)
            s_ld.n += 16 * len(items)

            @blk.vector
            def _(v):
                for i, (qk, g_) in enumerate(items):
                    v.wait_ge(s_ld.h, ld_b + 16 * (i + 1))
                    if i >= 2:
                        v.wait_ge(s_mm.h, mm_b + (i - 1) * 2)
                    slot = (i % 2) * 1056
                    hs = (g_ if qk == "q" else 16 + g_) * 3
                    v.tensor_copy(t["pscrb"][:, slot:slot + 3], t["halob"][:, hs:hs + 3])
                    v.engine_nop()
                    v.engine_nop().then_inc(s_ve.h, 1)
            s_ve.n += len(items)

            @blk.tensor
            def _(te):
                te.wait_ge(s_w.h, w_base + 32)
                for i, (qk, g_) in enumerate(items):
                    te.wait_ge(s_ve.h, ve_b + i + 1)
                    if i >= 1:
                        te.wait_ge(s_act.h, act_b + (i - 1) * 2 + 2)
                    slot = (i % 2) * 1056
                    woff = (0 if qk == "q" else 8192) + g_ * 128
                    for tb in range(2):
                        ps = t["ps_p0" if tb == 0 else "ps_p1"][:, 0:512]
                        for tau in range(4):
                            ins = te.matmul(ps,
                                            t["convwb"][:, woff + tau * 2048:woff + tau * 2048 + 128],
                                            t["pscrb"][:, slot + tb * 512 + tau:slot + tb * 512 + tau + 512],
                                            start=(tau == 0), stop=(tau == 3))
                        ins.then_inc(s_mm.h, 1)
            s_mm.n += 2 * len(items)

            @blk.scalar
            def _(sc):
                for i, (qk, g_) in enumerate(items):
                    dst = t["qb" if qk == "q" else "kb"]
                    bcol = t["bqk_conv_b"][:, (0 if qk == "q" else 16) + g_:(0 if qk == "q" else 16) + g_ + 1]
                    for tb in range(2):
                        sc.wait_ge(s_mm.h, mm_b + i * 2 + tb + 1)
                        ps = t["ps_p0" if tb == 0 else "ps_p1"][:, 0:512]
                        sc.activation(dst[:, g_ * 1024 + tb * 512:g_ * 1024 + tb * 512 + 512],
                                      ps, AF.Silu, bias=bcol).then_inc(s_act.h, 1)
            s_act.n += 2 * len(items)


    def b_pads(self):
        nc, t = self.nc, self.t
        s_ld, s_ve, s_act, s_mm = t["s_ld"], t["s_ve"], t["s_act"], t["s_mm"]
        with nc.Block() as blk:
            @blk.vector
            def _(v):
                v.tensor_scalar_mul(t["rows2b"][0:16, 0:8], t["Lbp"][0:16, 0:1024:128], -1.0)
                v.tensor_copy(t["rows2b"][0:16, 8:16], t["Lbp"][0:16, 128:1025:128])
                v.engine_nop()
                v.engine_nop()
                v.engine_nop()
                v.tensor_add(t["rows2b"][0:16, 16:24], t["rows2b"][0:16, 8:16], t["rows2b"][0:16, 0:8])
                v.engine_nop()
                v.engine_nop().then_inc(t["s_ve"].h, 1)
            s_ve.n += 1

            @blk.scalar
            def _(sc):
                sc.wait_ge(s_ve.h, s_ve.n)
                sc.activation(t["rows2b"][0:16, 16:24], t["rows2b"][0:16, 16:24], AF.Exp).then_inc(s_act.h, 1)
            s_act.n += 1

        with nc.Block() as blk:
            @blk.sync
            def _(s):
                s.wait_ge(s_act.h, s_act.n)
                s.dma_start(out=t["lbfl"][0:1, :], in_=t["rows2b"][0:16, 16:24]).then_inc(s_ld.h, 16)
            s_ld.n += 16

            @blk.tensor
            def _(te):
                te.wait_ge(s_ld.h, s_ld.n)
                te.matmul(t["ps_row"][:, 0:128], t["ones_rowf"][:], t["lbfl"][0:1, :],
                          start=True, stop=True).then_inc(s_mm.h, 1)
            s_mm.n += 1

            @blk.vector
            def _(v):
                v.wait_ge(s_mm.h, s_mm.n)
                v.tensor_copy(t["lamCbb"][:], t["ps_row"][:, 0:128])

        with nc.Block() as blk:
            act_b, mm_b, ve_b = s_act.n, s_mm.n, s_ve.n

            @blk.scalar
            def _(sc):
                for c in range(NBC):
                    if c >= 1:
                        sc.wait_ge(s_mm.h, mm_b + 4 * c)
                    sc.activation(t["lnscr"][0:16, 0:128], t["Lbp"][0:16, 1 + c * 128:1 + (c + 1) * 128],
                                  AF.Exp, bias=t["rows2b"][0:16, c:c + 1])
                    sc.activation(t["lnscr"][0:16, 128:256], t["Lbp"][0:16, 1 + c * 128:1 + (c + 1) * 128],
                                  AF.Exp, scale=-1.0, bias=t["rows2b"][0:16, 8 + c:9 + c]).then_inc(s_act.h, 1)
            s_act.n += NBC

            @blk.vector
            def _(v0):
                for c in range(NBC):
                    v0.wait_ge(s_act.h, act_b + c + 1)
                    v0.tensor_scalar_mul(t["lnscr"][0:16, 256:384],
                                         t["Lbp"][0:16, 1 + c * 128:1 + (c + 1) * 128], -1.0)
                    v0.engine_nop()
                    v0.engine_nop().then_inc(s_ve.h, 1)
                    for qi in range(4):
                        v0.wait_ge(s_mm.h, mm_b + 4 * c + qi + 1)
                        bank = ["ps_c", "ps_row", "ps_c", "ps_row"][qi]
                        v0.tensor_copy(t["toksb"][:, qi * 128 + c * 16: qi * 128 + c * 16 + 16],
                                       t[bank][0:128, 0:16])
                        v0.engine_nop()
                        v0.engine_nop().then_inc(s_ve.h, 1)
            s_ve.n += 5 * NBC

            @blk.tensor
            def _(te):
                for c in range(NBC):
                    te.wait_ge(s_ve.h, ve_b + 5 * c + 1)
                    srcs = [
                        (t["gateb"][0:16, c * 128:(c + 1) * 128], "ps_c"),
                        (t["lnscr"][0:16, 0:128], "ps_row"),
                        (t["lnscr"][0:16, 128:256], "ps_c"),
                        (t["lnscr"][0:16, 256:384], "ps_row"),
                    ]
                    te.wait_ge(s_act.h, act_b + c + 1)
                    for qi, (ap, bank) in enumerate(srcs):
                        te.wait_ge(s_ve.h, ve_b + 5 * c + 1 + max(0, qi - 1))
                        te.transpose(t[bank][0:128, 0:16], ap, t["identf"][0:16, 0:16]).then_inc(s_mm.h, 1)
            s_mm.n += 4 * NBC

    def b_pass1(self):
        nc, t = self.nc, self.t
        s_ld, s_ve, s_mm = t["s_ld"], t["s_ve"], t["s_mm"]
        NQ = NBC * HB
        ev = {}
        cmm, cve, cld = s_mm.n, s_ve.n, s_ld.n
        for c in range(NBC):
            cld += 16; ev[(c, "vb")] = cld
            for h in range(HB):
                q = c * HB + h
                cmm += 1; ev[(q, "ktr")] = cmm
                cve += 1; ev[(q, "ktil")] = cve
                cmm += 1; ev[(q, "smm")] = cmm
                cve += 1; ev[(q, "mupd")] = cve
        self._bp1ev = ev

        def tk(a, name):
            return ev[(a, name)]

        def tokcol(qty, c, h):
            return t["toksb"][:, qty * 128 + c * 16 + h: qty * 128 + c * 16 + h + 1]

        with nc.Block() as blk:
            @blk.vector
            def _(v):
                v.memset(t["Mb"][:], 0.0)

        with nc.Block() as blk:
            @blk.sync
            def _(s):
                for c in range(NBC):
                    if c >= 2:
                        s.wait_ge(s_mm.h, tk((c - 1) * HB - 1, "smm"))
                    s.dma_start(out=t["vbin"][:, (c % 2) * 4096:(c % 2) * 4096 + 4096],
                                in_=self.bsc["vb"].ap()[c]).then_inc(s_ld.h, 16)
            s_ld.n = cld

            @blk.tensor
            def _(te):
                for c in range(NBC):
                    te.wait_ge(s_ld.h, tk(c, "vb"))
                    for h in range(HB):
                        q = c * HB + h
                        p = q % 2
                        ksl = t["kb"][:, h * 1024 + c * 128:h * 1024 + (c + 1) * 128]
                        if q >= 1:
                            te.wait_ge(s_ve.h, tk(q - 1, "ktil"))
                        te.transpose(t["ps_t0" if p == 0 else "ps_t1"][:, 0:128], ksl,
                                     t["ident"][:]).then_inc(s_mm.h, 1)
                        te.wait_ge(s_ve.h, tk(q, "ktil"))
                        if q >= 2:
                            te.wait_ge(s_ve.h, tk(q - 2, "mupd"))
                        te.matmul((t["ps_a"] if p == 0 else t["ps_b"])[:, 0:256],
                                  t["ktkb"][:, p * 128:p * 128 + 128],
                                  t["vbin"][:, (c % 2) * 4096 + h * 256:(c % 2) * 4096 + (h + 1) * 256],
                                  start=True, stop=True).then_inc(s_mm.h, 1)
            s_mm.n = cmm

            @blk.vector
            def _(v):
                for c in range(NBC):
                    for h in range(HB):
                        q = c * HB + h
                        p = q % 2
                        v.wait_ge(s_mm.h, tk(q, "ktr"))
                        v.tensor_scalar(t["ktkb"][:, p * 128:p * 128 + 128],
                                        t["ps_t0" if p == 0 else "ps_t1"][:, 0:128],
                                        tokcol(2, c, h), None, ALU.mult)
                        v.engine_nop()
                        v.engine_nop().then_inc(s_ve.h, 1)
                        v.wait_ge(s_mm.h, tk(q, "smm"))
                        v.scalar_tensor_tensor(t["Mb"][:, h * 256:(h + 1) * 256],
                                               t["Mb"][:, h * 256:(h + 1) * 256],
                                               t["lamCbb"][:, h * 8 + c:h * 8 + c + 1],
                                               (t["ps_a"] if p == 0 else t["ps_b"])[:, 0:256],
                                               ALU.mult, ALU.add)
                        v.engine_nop()
                        v.engine_nop().then_inc(s_ve.h, 1)
            s_ve.n = cve

    def b_relay(self):
        nc, t = self.nc, self.t
        s_st, s_cc, s_ld, s_ve, s_mm = t["s_st"], t["s_cc"], t["s_ld"], t["s_ve"], t["s_mm"]
        with nc.Block() as blk:
            @blk.sync
            def _(s):
                s.dma_start(out=t["lbfl"][0:1, 0:16],
                            in_=t["Lbp"][0:16, 1024:1025]).then_inc(s_ld.h, 16)
            s_ld.n += 16

            @blk.scalar
            def _(sc):
                sc.wait_ge(s_ld.h, s_ld.n)
                sc.activation(t["lbfl"][0:1, 16:32], t["lbfl"][0:1, 0:16], AF.Exp).then_inc(t["s_act"].h, 1)
            t["s_act"].n += 1

            @blk.tensor
            def _(te):
                te.wait_ge(t["s_act"].h, t["s_act"].n)
                te.matmul(t["ps_row"][:, 0:16], t["ones_rowf"][:], t["lbfl"][0:1, 16:32],
                          start=True, stop=True).then_inc(s_mm.h, 1)
            s_mm.n += 1

            @blk.vector
            def _(v):
                v.wait_ge(s_mm.h, s_mm.n)
                v.tensor_copy(t["eargb"][:, 0:16], t["ps_row"][:, 0:16])
        with nc.Block() as blk:
            @blk.sync
            def _(s):
                s.dma_start(out=self.cc["rel_b_in"].ap()[:, 0:4096], in_=t["Mb"][:]).then_inc(s_st.h, 16)
                s.dma_start(out=self.cc["rel_b_in"].ap()[:, 4096:4112], in_=t["eargb"][:, 0:16]).then_inc(s_st.h, 16)
            s_st.n += 32

            @blk.gpsimd
            def _(g):
                g.wait_ge(s_st.h, s_st.n)
                g.collective_compute(
                    "AllGather", ALU.bypass,
                    replica_groups=[[0, 1, 2, 3], [4, 5, 6, 7]],
                    ins=[self.cc["rel_b_in"].ap()],
                    outs=[self.cc["rel_b_out"].ap()],
                ).then_inc(s_cc.h, 1)
            s_cc.n += 1

        with nc.Block() as blk:
            @blk.vector
            def _(v):
                v.memset(t["Mb"][:], 0.0)
                v.memset(t["Sbb"][:], 0.0)
        for j in range(3):
            with nc.Block() as blk:
                ld_b = s_ld.n

                @blk.sync
                def _(s):
                    s.wait_ge(s_cc.h, s_cc.n)
                    s.dma_start(out=t["relbuf"][:],
                                in_=self.cc["rel_b_out"].ap()[j * 128:(j + 1) * 128, 0:4096]).then_inc(s_ld.h, 16)
                    s.dma_start(out=t["eargb"][:, 0:16],
                                in_=self.cc["rel_b_out"].ap()[j * 128:(j + 1) * 128, 4096:4112]).then_inc(s_ld.h, 16)
                s_ld.n += 32

                @blk.vector
                def _(v):
                    v.wait_ge(s_ld.h, s_ld.n)
                    for h in range(HB):
                        Ssl = t["Mb"][:, h * 256:(h + 1) * 256]
                        d1 = t["otfb"][:, 0:256]
                        v.scalar_tensor_tensor(d1, Ssl, t["eargb"][:, h:h + 1],
                                               t["relbuf"][:, h * 256:(h + 1) * 256], ALU.mult, ALU.add)
                        v.engine_nop()
                        v.engine_nop()
                        v.tensor_sub(d1, d1, Ssl)
                        v.engine_nop()
                        v.engine_nop()
                        v.scalar_tensor_tensor(Ssl, d1, t["rmask"][:, j:j + 1], Ssl, ALU.mult, ALU.add)
        with nc.Block() as blk:
            @blk.vector
            def _(v):
                v.tensor_copy(t["Sbb"][:], t["Mb"][:])

    def b_pass2(self):
        nc, t = self.nc, self.t
        for blkc in range(2):
            self._b_pass2_block(blkc)
            self.b_wo(blkc)

    def _b_pass2_block(self, blkc):
        nc, t = self.nc, self.t
        s_ld, s_ve, s_act, s_mm, s_g = t["s_ld"], t["s_ve"], t["s_act"], t["s_mm"], t["s_g"]
        cs = list(range(blkc * 4, blkc * 4 + 4))
        ev = {}
        cmm, cve, cact, cld, cg = s_mm.n, s_ve.n, s_act.n, s_ld.n, s_g.n
        for c in cs:
            cld += 16; ev[(c, "vb")] = cld
            cld += 16; ev[(c, "lb0")] = cld
            ev[(c, "lb1")] = None
            for grp in range(4):
                if grp == 2:
                    cld += 16; ev[(c, "lb1")] = cld
                cmm += 1; ev[(c, f"rbc{grp}")] = cmm
                for h in range(grp * 4, grp * 4 + 4):
                    q = c * HB + h
                    cmm += 1; ev[(q, "amm")] = cmm
                    cve += 1; ev[(q, "aarg")] = cve
                    cg += 1; ev[(q, "aargdone")] = cg
                    cact += 1; ev[(q, "aexp")] = cact
                    cve += 1; ev[(q, "abf")] = cve
                    cmm += 1; ev[(q, "o12")] = cmm
                    cve += 1; ev[(q, "ocmb")] = cve
                    cmm += 1; ev[(q, "otr")] = cmm
                    cve += 1; ev[(q, "oev")] = cve
                    cmm += 1; ev[(q, "ktr")] = cmm
                    cve += 1; ev[(q, "ktil")] = cve
                    cmm += 1; ev[(q, "smm")] = cmm
                    cve += 1; ev[(q, "supd")] = cve

        def tk(a, name):
            return ev[(a, name)]

        def tokcol(qty, c, h):
            return t["toksb"][:, qty * 128 + c * 16 + h: qty * 128 + c * 16 + h + 1]

        def bank(p):
            return t["ps_a" if p == 0 else "ps_b"]

        def obank(p):
            return t["ps_c" if p == 0 else "ps_row"]

        def tb_(p):
            return t["ps_t0" if p == 0 else "ps_t1"]

        def heads(c):
            for grp in range(4):
                for h in range(grp * 4, grp * 4 + 4):
                    yield grp, h

        with nc.Block() as blk:
            @blk.sync
            def _(s):
                for c in cs:
                    if c - blkc * 4 >= 2:
                        s.wait_ge(s_ve.h, tk((c - 1) * HB - 1, "supd"))
                    s.dma_start(out=t["vbin"][:, (c % 2) * 4096:(c % 2) * 4096 + 4096],
                                in_=self.bsc["vb"].ap()[c]).then_inc(s_ld.h, 16)
                    if c - blkc * 4 >= 1:
                        s.wait_ge(s_mm.h, tk(c - 1, "rbc3"))
                    s.dma_start(out=t["bfl2"][0:1, 0:1024],
                                in_=t["Lbp"][0:8, 1 + c * 128:1 + (c + 1) * 128]).then_inc(s_ld.h, 16)
                    s.wait_ge(s_mm.h, tk(c, "rbc1"))
                    s.dma_start(out=t["bfl2"][0:1, 0:1024],
                                in_=t["Lbp"][8:16, 1 + c * 128:1 + (c + 1) * 128]).then_inc(s_ld.h, 16)
            s_ld.n = cld

            @blk.tensor
            def _(te):
                for c in cs:
                    te.wait_ge(s_ld.h, tk(c, "vb"))
                    for grp, h in heads(c):
                        q = c * HB + h
                        p = q % 2
                        if h % 4 == 0:
                            te.wait_ge(s_ld.h, tk(c, "lb0" if grp < 2 else "lb1"))
                            if (c, grp) != (cs[0], 0):
                                prevq = c * HB + grp * 4 - 1 if grp > 0 else c * HB - 1
                                if (prevq, "aargdone") in ev:
                                    te.wait_ge(s_g.h, tk(prevq, "aargdone"))
                            te.matmul(t["ps_p0"][:, 0:512], t["ones_rowf"][:],
                                      t["bfl2"][0:1, (grp % 2) * 512:(grp % 2) * 512 + 512],
                                      start=True, stop=True).then_inc(s_mm.h, 1)
                        ksl = t["kb"][:, h * 1024 + c * 128:h * 1024 + (c + 1) * 128]
                        qsl = t["qb"][:, h * 1024 + c * 128:h * 1024 + (c + 1) * 128]
                        vsl = t["vbin"][:, (c % 2) * 4096 + h * 256:(c % 2) * 4096 + (h + 1) * 256]
                        if q - cs[0] * HB >= 2:
                            te.wait_ge(s_ve.h, tk(q - 2, "supd"))
                        te.matmul(bank(p)[:, 0:128], ksl, qsl, start=True, stop=True).then_inc(s_mm.h, 1)
                        te.wait_ge(s_ve.h, tk(q, "abf"))
                        if q - cs[0] * HB >= 2:
                            te.wait_ge(s_ve.h, tk(q - 2, "ocmb"))
                        te.matmul(obank(p)[:, 0:256], t["abfb"][:, p * 128:p * 128 + 128],
                                  vsl, start=True, stop=True)
                        te.matmul(obank(p)[:, 256:512], qsl, t["Sbb"][:, h * 256:(h + 1) * 256],
                                  start=True, stop=True).then_inc(s_mm.h, 1)
                        te.wait_ge(s_ve.h, tk(q, "ocmb"))
                        if q - cs[0] * HB >= 2:
                            te.wait_ge(s_ve.h, tk(q - 2, "oev"))
                        if q - cs[0] * HB >= 1:
                            te.wait_ge(s_ve.h, tk(q - 1, "ktil"))
                        for half in range(2):
                            ins = te.transpose(tb_(p)[:, half * 128:half * 128 + 128],
                                               t["obfb"][:, p * 256 + half * 128:p * 256 + (half + 1) * 128],
                                               t["ident"][:])
                        ins.then_inc(s_mm.h, 1)
                        te.wait_ge(s_ve.h, tk(q, "oev"))
                        te.transpose(tb_(p)[:, 256:384], ksl, t["ident"][:]).then_inc(s_mm.h, 1)
                        te.wait_ge(s_ve.h, tk(q, "ktil"))
                        te.matmul(bank(p)[:, 128:384], t["ktkb"][:, p * 128:p * 128 + 128], vsl,
                                  start=True, stop=True).then_inc(s_mm.h, 1)
            s_mm.n = cmm

            @blk.vector
            def _(v):
                for c in cs:
                    for grp, h in heads(c):
                        q = c * HB + h
                        p = q % 2
                        v.wait_ge(s_mm.h, tk(c, f"rbc{grp}"))
                        v.tensor_add(t["eargb"][:, p * 128:p * 128 + 128],
                                     t["ps_p0"][:, (h % 4) * 128:(h % 4) * 128 + 128], t["maskI"][:])
                        v.engine_nop()
                        v.engine_nop().then_inc(s_ve.h, 1)
                        v.wait_ge(s_act.h, tk(q, "aexp"))
                        v.wait_ge(s_mm.h, tk(q, "amm"))
                        v.tensor_mul(t["abfb"][:, p * 128:p * 128 + 128], bank(p)[:, 0:128],
                                     t["eargb"][:, p * 128:p * 128 + 128])
                        v.engine_nop()
                        v.engine_nop().then_inc(s_ve.h, 1)
                        v.wait_ge(s_mm.h, tk(q, "o12"))
                        osl = t["otfb"][:, p * 256:p * 256 + 256]
                        v.tensor_copy(osl, obank(p)[:, 0:256])
                        v.engine_nop()
                        v.engine_nop()
                        v.scalar_tensor_tensor(osl, obank(p)[:, 256:512], tokcol(1, c, h),
                                               osl, ALU.mult, ALU.add)
                        v.engine_nop()
                        v.engine_nop()
                        v.tensor_scalar(t["obfb"][:, p * 256:p * 256 + 256], osl,
                                        tokcol(0, c, h), None, ALU.mult)
                        v.engine_nop()
                        v.engine_nop().then_inc(s_ve.h, 1)
                        v.wait_ge(s_mm.h, tk(q, "otr"))
                        base = (2 * h) * 512 + (c % 4) * 128
                        v.tensor_copy(t["ofmb"][:, base:base + 128], tb_(p)[:, 0:128])
                        v.tensor_copy(t["ofmb"][:, base + 512:base + 512 + 128], tb_(p)[:, 128:256])
                        v.engine_nop()
                        v.engine_nop().then_inc(s_ve.h, 1)
                        v.wait_ge(s_mm.h, tk(q, "ktr"))
                        v.tensor_scalar(t["ktkb"][:, p * 128:p * 128 + 128], tb_(p)[:, 256:384],
                                        tokcol(2, c, h), None, ALU.mult)
                        v.engine_nop()
                        v.engine_nop().then_inc(s_ve.h, 1)
                        v.wait_ge(s_mm.h, tk(q, "smm"))
                        Ssl = t["Mb"][:, h * 256:(h + 1) * 256]
                        v.scalar_tensor_tensor(Ssl, Ssl, t["lamCbb"][:, h * 8 + c:h * 8 + c + 1],
                                               bank(p)[:, 128:384], ALU.mult, ALU.add)
                        v.engine_nop()
                        v.engine_nop()
                        v.tensor_copy(t["Sbb"][:, h * 256:(h + 1) * 256], Ssl)
                        v.engine_nop()
                        v.engine_nop().then_inc(s_ve.h, 1)
            s_ve.n = cve

            @blk.scalar
            def _(sc):
                for c in cs:
                    for grp, h in heads(c):
                        q = c * HB + h
                        p = q % 2
                        sc.wait_ge(s_ve.h, tk(q, "aarg"))
                        sc.activation(t["eargb"][:, p * 128:p * 128 + 128],
                                      t["eargb"][:, p * 128:p * 128 + 128], AF.Exp,
                                      bias=tokcol(3, c, h)).then_inc(s_act.h, 1)
            s_act.n = cact

            @blk.gpsimd
            def _(g):
                for c in cs:
                    for grp, h in heads(c):
                        q = c * HB + h
                        g.wait_ge(s_ve.h, tk(q, "aarg"))
                        g.engine_nop()
                        g.engine_nop().then_inc(s_g.h, 1)
            s_g.n = cg

    def b_wo(self, blkc):
        nc, t = self.nc, self.t
        s_pk, s_ev, s_st = t["s_pk"], t["s_ev"], t["s_st"]

        def rhs(i, tb):
            return t["ofmb"][:, i * 512:(i + 1) * 512]

        ev_base = s_ev.n
        pk_base = s_pk.n
        st_base = s_st.n

        def scalar_xp(sc, oc, tb, mm_t):
            if oc >= 2:
                sc.wait_ge(s_ev.h, ev_base + oc - 1)
            sc.dma_start(out=t["xpin"][:, (oc % 2) * 512:(oc % 2) * 512 + 512],
                         in_=self.x_park.ap()[oc * 128:(oc + 1) * 128,
                                              blkc * 512:blkc * 512 + 512]).then_inc(s_pk.h, 16)

        def evac2(v, oc, tb, ps):
            v.wait_ge(s_pk.h, pk_base + 16 * (oc + 1))
            if oc >= 2:
                v.wait_ge(s_st.h, st_base + 16 * (oc - 1))
            v.tensor_add(t["xout"][:, (oc % 2) * 512:(oc % 2) * 512 + 512],
                         t["xpin"][:, (oc % 2) * 512:(oc % 2) * 512 + 512], ps)

        def store(se, oc, tb, ev_t):
            se.wait_ge(s_ev.h, ev_t)
            se.dma_start(out=self.bsc["xp2"].ap()[oc * 128:(oc + 1) * 128,
                                                  blkc * 512:blkc * 512 + 512],
                         in_=t["xout"][:, (oc % 2) * 512:(oc % 2) * 512 + 512]).then_inc(s_st.h, 16)

        self.proj_lhsT("Wo_b", 16, 128, list(range(32)), rhs, evac2, nt=1, tw=512,
                       scalar_fn=scalar_xp, store_fn=store)
        s_pk.n += 16 * 16

    def b_restore(self):
        nc, t = self.nc, self.t
        s_ld, s_act = t["s_ld"], t["s_act"]
        with nc.Block() as blk:
            ld_b = s_ld.n

            @blk.sync
            def _(s):
                for c in range(NCH):
                    s.dma_start(out=t["x_res"][:, c * TPC:(c + 1) * TPC],
                                in_=self.bsc["xp2"].ap()[c * 128:(c + 1) * 128, :]).then_inc(s_ld.h, 16)
            s_ld.n += 16 * NCH

            @blk.scalar
            def _(sc):
                for c in range(NCH):
                    sc.wait_ge(s_ld.h, ld_b + 16 * (c + 1))
                    sc.copy(t["x_bf"][:, c * TPC:(c + 1) * TPC], t["x_res"][:, c * TPC:(c + 1) * TPC])

    def finish(self):
        nc, t = self.nc, self.t
        s_st = t["s_st"]
        with nc.Block() as blk:
            @blk.sync
            def _(s):
                for c in range(NCH):
                    s.dma_start(out=self.y.ap()[c * 128:(c + 1) * 128, :],
                                in_=t["x_res"][:, c * TPC:(c + 1) * TPC]).then_inc(s_st.h, 16)
                s.wait_ge(s_st.h, s_st.n + 16 * NCH)
            s_st.n += 16 * NCH


def _lhsT_swizzle(w, ocols=128):
    O, Din = w.shape
    n_oc = max(1, O // ocols)
    wt = np.ascontiguousarray(w.T)
    h = wt.reshape(Din // 128, 128, n_oc, ocols).transpose(2, 1, 0, 3)
    return np.ascontiguousarray(h.reshape(n_oc, 128, (Din // 128) * ocols))


def _colchunks(a):
    C, k = a.shape
    return np.ascontiguousarray(a.reshape(C // 128, 128, k).transpose(1, 0, 2).reshape(128, -1))


def _colvec(a):
    return np.ascontiguousarray(a.reshape(-1, 128).T)


def _rhs_swizzle(w, og=512):
    O, Din = w.shape
    wt = np.ascontiguousarray(w.T)
    h = wt.reshape(Din // 128, 128, O // og, og).transpose(2, 1, 0, 3)
    return np.ascontiguousarray(h.reshape(O // og, 128, (Din // 128) * og))


def host_prep(inputs):
    f = np.ascontiguousarray
    x = inputs["L0_attn_a_x"]
    Wba = np.concatenate([inputs["L0_attn_a_Wb"],
                          np.zeros((16, 2048), np.float32),
                          inputs["L0_attn_a_Wa"]], 0)
    Wgb = np.concatenate([inputs["L0_attn_b_Wg"],
                          np.zeros((16, 2048), np.float32),
                          inputs["L0_attn_b_Wb"]], 0)
    shared = {
        "Wq_a": _lhsT_swizzle(inputs["L0_attn_a_Wq"]),
        "Wk_a": _lhsT_swizzle(inputs["L0_attn_a_Wk"]),
        "Wv_a": _lhsT_swizzle(inputs["L0_attn_a_Wv"]),
        "Wba_w": _lhsT_swizzle(Wba, 40),
        "Wo_a": _lhsT_swizzle(inputs["L0_attn_a_Wo"]),
        "ffn_w": _lhsT_swizzle(inputs["L0_ffn_weight"]),
        "Wq_b": _lhsT_swizzle(inputs["L0_attn_b_Wq"]),
        "Wk_b": _lhsT_swizzle(inputs["L0_attn_b_Wk"]),
        "Wgb_w": _lhsT_swizzle(Wgb, 48),
        "Wo_b": _lhsT_swizzle(inputs["L0_attn_b_Wo"]),
        "Wg_a": _rhs_swizzle(inputs["L0_attn_a_Wg"]),
        "Wv_b": _rhs_swizzle(inputs["L0_attn_b_Wv"]),
        "conv_q_a": _colchunks(inputs["L0_attn_a_conv_q"]),
        "conv_k_a": _colchunks(inputs["L0_attn_a_conv_k"]),
        "conv_v_a": _colchunks(inputs["L0_attn_a_conv_v"]),
        "convW_q_b": f(inputs["L0_attn_b_Wq_conv"].transpose(2, 1, 0)),
        "convW_k_b": f(inputs["L0_attn_b_Wk_conv"].transpose(2, 1, 0)),
        "bq_conv_b": _colvec(inputs["L0_attn_b_bq_conv"]),
        "bk_conv_b": _colvec(inputs["L0_attn_b_bk_conv"]),
        "bg_b": f(inputs["L0_attn_b_bg"][None]),
        "bb_b": f(inputs["L0_attn_b_bb"][None]),
        "A_log": f(inputs["L0_attn_a_A_log"][:, None]),
        "dt_bias": f(inputs["L0_attn_a_dt_bias"][:, None]),
        "o_norm_w": f(inputs["L0_attn_a_o_norm_w"][None]),
        "ffn_scale": _colvec(inputs["L0_ffn_scale_weight"]),
        "ffn_bias": _colvec(inputs["L0_ffn_bias"]),
    }
    in_maps = []
    for core in range(8):
        b, r = core // 4, core % 4
        m = dict(shared)
        m["xs"] = f(x[b, r * TPC:(r + 1) * TPC, :].T)
        ri = np.zeros(8, np.float32)
        for j in range(3):
            ri[j] = 1.0 if j < r else 0.0
            ri[3 + j] = 1.0 if j == r - 1 else 0.0
        ri[6], ri[7] = r, b
        m["rankinfo"] = ri[None]
        in_maps.append(m)
    return in_maps


def kernel(**inputs):
    in_maps = host_prep(inputs)
    bld = Builder()
    nc = bld.build()
    res = run_bass_kernel_spmd(nc, in_maps, core_ids=list(range(8)))
    out = np.empty((B, L, D), np.float32)
    for core in range(8):
        b, r = core // 4, core % 4
        out[b, r * TPC:(r + 1) * TPC, :] = res.results[core]["y"].T
    return out


# version 05 (final)
# speedup vs baseline: 1.2989x; 1.1226x over previous
import numpy as np
from contextlib import ExitStack

import concourse.bass as bass
import concourse.mybir as mybir
from concourse.bass_utils import run_bass_kernel_spmd

F32 = mybir.dt.float32
BF16 = mybir.dt.bfloat16
AF = mybir.ActivationFunctionType
ALU = mybir.AluOpType

B, L, D = 2, 4096, 2048
TPC = 1024
NCH = D // 128
NTC = TPC // 128
HA, DH, NH = 8, 128, 2
SS = TPC * NH
NAC = SS // 128
HB, KD, VEFF = 16, 128, 256
NBC = TPC // 128

NEG = -1.0e30



def out_slice_fn(kind, h, t):
    if kind == "q":
        return t["aq"][:, h * 1024:(h + 1) * 1024]
    if kind == "k0":
        return t["ak"][:, h * 2048:h * 2048 + 2048:2]
    if kind == "k1":
        return t["ak"][:, h * 2048 + 1:h * 2048 + 2048:2]
    if kind == "v0":
        return t["vscr"][:, 0:2048:2]
    return t["vscr"][:, 1:2048:2]


class Sem:
    def __init__(self, h):
        self.h = h
        self.n = 0

    def inc(self, inst, k=1):
        inst.then_inc(self.h, k)
        self.n += k
        return self.n


class Builder:
    def __init__(self, debug_taps=(), stop=None):
        self.stop = stop
        self.debug_taps = list(debug_taps)
        nc = bass.Bass("TRN2", target_bir_lowering=False, detect_race_conditions=False)
        self.nc = nc
        self.ctx = ExitStack()
        self.t = {}

    def sb(self, name, shape, dtype, stack=None):
        h = (stack or self.ctx).enter_context(self.nc.sbuf_tensor(name, shape, dtype))
        self.t[name] = h
        return h

    def ps(self, name, shape, dtype=F32):
        h = self.ctx.enter_context(self.nc.psum_tensor(name, shape, dtype))
        self.t[name] = h
        return h

    def sem(self, name):
        s = Sem(self.ctx.enter_context(self.nc.semaphore(name)))
        self.t[name] = s
        return s

    def tap(self, name, ap_fn, shape, dtype=F32):
        if name not in self.debug_taps:
            return
        nc = self.nc
        out = nc.dram_tensor("tap_" + name, shape, dtype, kind="ExternalOutput")
        sem = Sem(self.ctx.enter_context(nc.semaphore("tapsem_" + name)))
        with nc.Block() as blk:
            @blk.sync
            def _(s):
                s.dma_start(out=out.ap(), in_=ap_fn()).then_inc(sem.h, 16)
                s.wait_ge(sem.h, 16)

    def proj_lhsT(self, wname, n_oc, ocols, kcs, rhs_fn, evac_fn, nt=2, tw=512,
                  scalar_fn=None, store_fn=None):
        nc, t = self.nc, self.t
        s_w, s_wc, s_mm, s_ev = t["s_w"], t["s_wc"], t["s_mm"], t["s_ev"]
        nk = len(kcs)
        wslab = t["wslab"]
        slabw = nk * ocols
        assert slabw <= 4096
        d = self.d
        w_base, wc_base, mm_base, ev_base = s_w.n, s_wc.n, s_mm.n, s_ev.n
        ngr = n_oc * nt

        with nc.Block() as blk:
            @blk.gpsimd
            def _(g):
                for oc in range(n_oc):
                    if oc >= 3:
                        g.wait_ge(s_ev.h, ev_base + (oc - 2) * nt)
                    g.dma_start(
                        out=wslab[:, (oc % 3) * 4096:(oc % 3) * 4096 + slabw],
                        in_=d[wname].ap()[oc],
                    ).then_inc(s_w.h, 16)
            s_w.n += 16 * n_oc

            @blk.tensor
            def _(te):
                for oc in range(n_oc):
                    te.wait_ge(s_w.h, w_base + 16 * (oc + 1))
                    for tb in range(nt):
                        gi = oc * nt + tb
                        ps = t["ps_p0" if gi % 2 == 0 else "ps_p1"][0:ocols, 0:tw]
                        if gi >= 2:
                            te.wait_ge(s_ev.h, ev_base + gi - 1)
                        for i, kc in enumerate(kcs):
                            ins = te.matmul(
                                ps,
                                wslab[:, (oc % 3) * 4096 + i * ocols:(oc % 3) * 4096 + (i + 1) * ocols],
                                rhs_fn(i, tb),
                                start=(i == 0), stop=(i == nk - 1))
                        ins.then_inc(s_mm.h, 1)
            s_mm.n += ngr

            s_st = t["s_st"]
            st_base = s_st.n

            @blk.vector
            def _(v):
                for oc in range(n_oc):
                    for tb in range(nt):
                        gi = oc * nt + tb
                        v.wait_ge(s_mm.h, mm_base + gi + 1)
                        if store_fn is not None and gi >= 4:
                            v.wait_ge(s_st.h, st_base + 16 * (gi - 3))
                        ps = t["ps_p0" if gi % 2 == 0 else "ps_p1"][0:ocols, 0:tw]
                        evac_fn(v, oc, tb, ps)
                        v.engine_nop()
                        v.engine_nop().then_inc(s_ev.h, 1)
            s_ev.n += ngr

            if scalar_fn is not None:
                @blk.scalar
                def _(sc):
                    for oc in range(n_oc):
                        for tb in range(nt):
                            gi = oc * nt + tb
                            scalar_fn(sc, oc, tb, mm_base + gi + 1)

            if store_fn is not None:
                @blk.sync
                def _(se):
                    for oc in range(n_oc):
                        for tb in range(nt):
                            gi = oc * nt + tb
                            store_fn(se, oc, tb, ev_base + gi + 1)
                s_st.n += 16 * ngr

    def build(self):
        nc = self.nc
        d = {}

        def din(name, shape):
            d[name] = nc.dram_tensor(name, shape, F32, kind="ExternalInput")
            return d[name]

        din("xs", [D, TPC])
        din("rankinfo", [1, 8])
        din("Wq_a", [8, 128, NCH * 128])
        din("Wk_a", [16, 128, NCH * 128])
        din("Wv_a", [16, 128, NCH * 128])
        din("Wba_w", [1, 128, NCH * 40])
        din("Wo_a", [16, 128, 8 * 128])
        din("ffn_w", [16, 128, NCH * 128])
        din("Wq_b", [16, 128, NCH * 128])
        din("Wk_b", [16, 128, NCH * 128])
        din("Wgb_w", [1, 128, NCH * 48])
        din("Wo_b", [16, 128, 32 * 128])
        din("Wg_a", [2, 128, NCH * 512])
        din("Wv_b", [8, 128, NCH * 512])
        din("conv_q_a", [128, 8 * 4])
        din("conv_k_a", [128, 16 * 4])
        din("conv_v_a", [128, 16 * 4])
        din("convW_q_b", [4, 128, 2048])
        din("convW_k_b", [4, 128, 2048])
        din("bq_conv_b", [128, 16])
        din("bk_conv_b", [128, 16])
        din("bg_b", [1, 16])
        din("bb_b", [1, 16])
        din("A_log", [8, 1])
        din("dt_bias", [8, 1])
        din("o_norm_w", [1, 128])
        din("ffn_scale", [128, 16])
        din("ffn_bias", [128, 16])
        self.d = d

        self.y = nc.dram_tensor("y", [D, TPC], F32, kind="ExternalOutput")
        self.x_park = nc.dram_tensor("x_park", [D, TPC], F32)
        self.cc = {}
        for nm, shape in [
            ("halo_a_in", [128, 120]), ("halo_a_out", [512, 120]),
            ("rel_a_in", [128, HA * 256]), ("rel_a_out", [512, HA * 256]),
            ("halo_b_in", [128, 96]), ("halo_b_out", [512, 96]),
            ("rel_b_in", [128, HB * VEFF + 16]), ("rel_b_out", [512, HB * VEFF + 16]),
        ]:
            self.cc[nm] = nc.dram_tensor("cc_" + nm, shape, F32)

        import ml_dtypes
        ident_bf = np.eye(128, dtype=ml_dtypes.bfloat16)
        self.ident_dram = nc.inline_tensor(ident_bf.view(np.uint16), name="c_ident")
        ii, jj = np.arange(128)[:, None], np.arange(128)[None, :]
        self.maskS_dram = nc.inline_tensor(np.where(jj > ii, 0.0, NEG).astype(np.float32), name="c_maskS")
        self.maskI_dram = nc.inline_tensor(np.where(jj >= ii, 0.0, NEG).astype(np.float32), name="c_maskI")
        E = np.zeros((128, 64), np.float32)
        for h in range(8):
            E[:, 8 * h + h] = 1.0
        self.E_dram = nc.inline_tensor(E.astype(ml_dtypes.bfloat16).view(np.uint16), name="c_E")
        self.identf_dram = nc.inline_tensor(np.eye(128, dtype=np.float32), name="c_identf")
        mq = np.where(np.arange(128)[:, None] <= 2 * np.arange(64)[None, :] + 1, 0.0, NEG)
        self.maskQ_dram = nc.inline_tensor(mq.astype(np.float32), name="c_maskQ")

        self.sb("ident", [128, 128], BF16)
        self.sb("maskS", [128, 128], F32)
        self.sb("maskI", [128, 128], F32)
        self.sb("Ecols", [128, 64], BF16)
        self.sb("ones_col", [128, 1], BF16)
        self.sb("ones_row", [1, 128], BF16)
        self.sb("ones_rowf", [1, 128], F32)
        self.sb("maskQ", [128, 64], F32)
        self.sb("rmask", [128, 8], F32)
        self.sb("cst", [128, 4], F32)
        self.sb("cq_a", [128, 8 * 4], F32)
        self.sb("ck_a", [128, 16 * 4], F32)
        self.sb("cv_a", [128, 16 * 4], F32)
        self.sb("bqk_conv_b", [128, 2 * 16], F32)
        self.sb("A_log_c", [8, 1], F32)
        self.sb("dt_bias_c", [8, 1], F32)
        self.sb("negexpA", [8, 1], F32)
        self.sb("o_nw_row", [1, 128], F32)
        self.sb("Wonw_bc", [128, 128], F32)
        self.sb("ffn_sc", [128, 16], F32)
        self.sb("ffn_bi", [128, 16], F32)
        self.sb("gb16", [1, 16], F32)
        self.sb("bb16", [1, 16], F32)

        self.sb("identf", [128, 128], F32)
        self.sb("row_scr", [1, 4 * 512], F32)
        self.sb("ab_rows", [1, 2 * TPC], F32)
        self.sb("ab_rows_bf", [1, 2 * TPC], BF16)
        self.sb("sq_scr", [128, 4 * 512], BF16)
        self.sb("wslab", [128, 3 * 4096], BF16)
        self.abase = self.nc._sbuf_addr_for_side("left")
        self.asize = 229344 - self.abase - 64
        def at(name, shape, dtype, off):
            nbytes = int(np.prod(shape[1:])) * (2 if dtype == BF16 else 4)
            assert off + nbytes <= self.asize, (name, off, nbytes, self.asize)
            h = self.nc.alloc_sbuf_tensor_at(name, shape, dtype, offset=self.abase + off)
            self.t[name] = h
            return h
        self.at = at
        K = 1024
        at("x_res", [128, NCH * TPC], F32, 0)
        at("x_bf", [128, NCH * TPC], BF16, 65 * K)
        at("xn", [128, NCH * TPC], BF16, 97 * K)

        self.ps("ps_p0", [128, 512])
        self.ps("ps_p1", [128, 512])
        self.ps("ps_a", [128, 512])
        self.ps("ps_b", [128, 512])
        self.ps("ps_c", [128, 512])
        self.ps("ps_row", [128, 512])
        self.ps("ps_t0", [128, 512], BF16)
        self.ps("ps_t1", [128, 512], BF16)

        for s in ["s_ld", "s_w", "s_wc", "s_mm", "s_ev", "s_ev2", "s_pk", "s_cc",
                  "s_st", "s_x", "s_t0", "s_t1", "s_pa", "s_pb", "s_pc",
                  "s_row", "s_act", "s_ve", "s_g"]:
            self.sem(s)

        self.p0_loads()
        self.ln("xn1")
        self.attn_a()
        if self.stop is None:
            self.ln("xn2")
            self.ffn()
            self.ln("xn3")
            self.attn_b()
            self.ln("xn4")
            self.ffn()
        self.finish()
        self.ctx.close()
        return self.nc

    def p0_loads(self):
        nc, t, d = self.nc, self.t, self.d
        s_ld = t["s_ld"]

        with nc.Block() as blk:
            @blk.sync
            def _(s):
                loads = [
                    (t["x_res"][:], d["xs"].ap().rearrange("(c p) t -> p c t", p=128)),
                    (t["ident"][:].bitcast(mybir.dt.uint16), self.ident_dram.ap()),
                    (t["identf"][:], self.identf_dram.ap()),
                    (t["maskQ"][:], self.maskQ_dram.ap()),
                    (t["maskS"][:], self.maskS_dram.ap()),
                    (t["maskI"][:], self.maskI_dram.ap()),
                    (t["Ecols"][:].bitcast(mybir.dt.uint16), self.E_dram.ap()),
                    (t["cq_a"][:], d["conv_q_a"].ap()),
                    (t["ck_a"][:], d["conv_k_a"].ap()),
                    (t["cv_a"][:], d["conv_v_a"].ap()),
                    (t["A_log_c"][:], d["A_log"].ap()),
                    (t["dt_bias_c"][:], d["dt_bias"].ap()),
                    (t["o_nw_row"][:], d["o_norm_w"].ap()),
                    (t["ffn_sc"][:], d["ffn_scale"].ap()),
                    (t["ffn_bi"][:], d["ffn_bias"].ap()),
                    (t["gb16"][:], d["bg_b"].ap()),
                    (t["bb16"][:], d["bb_b"].ap()),
                    (t["bqk_conv_b"][:, 0:16], d["bq_conv_b"].ap()),
                    (t["bqk_conv_b"][:, 16:32], d["bk_conv_b"].ap()),
                    (t["row_scr"][0:1, 0:8], d["rankinfo"].ap()),
                ]
                for o_, i_ in loads:
                    s.dma_start(out=o_, in_=i_).then_inc(s_ld.h, 16)

            @blk.gpsimd
            def _(g):
                g.memset(t["ones_col"][:], 1.0)
                g.memset(t["ones_row"][:], 1.0)
                g.memset(t["ones_rowf"][:], 1.0)
                g.memset(t["cst"][:, 0:1], 1e-5)
                g.memset(t["cst"][:, 1:2], 1.0 / 128.0)
                g.memset(t["cst"][:, 2:3], 1.0)
            s_ld.n += 16 * 20

        with nc.Block() as blk:
            n_ld = s_ld.n
            s_mm, s_act = t["s_mm"], t["s_act"]

            @blk.scalar
            def _(sc):
                sc.wait_ge(s_ld.h, n_ld)
                for c in range(NCH):
                    sc.copy(t["x_bf"][:, c * TPC:(c + 1) * TPC],
                            t["x_res"][:, c * TPC:(c + 1) * TPC])
                sc.activation(t["negexpA"][:], t["A_log_c"][:], AF.Exp).then_inc(s_act.h, 1)

            @blk.vector
            def _(v):
                v.wait_ge(s_act.h, s_act.n + 1)
                v.tensor_scalar_mul(t["negexpA"][:], t["negexpA"][:], -1.0)
                v.wait_ge(s_ld.h, n_ld)
                v.tensor_copy(t["ab_rows_bf"][0:1, 0:8], t["row_scr"][0:1, 0:8])
                v.tensor_copy(t["ab_rows_bf"][0:1, 8:136], t["o_nw_row"][0:1, 0:128]).then_inc(t["s_ve"].h, 1)
            s_act.n += 1
            s_ve_n = t["s_ve"].n = t["s_ve"].n + 1

            @blk.tensor
            def _(te):
                te.wait_ge(t["s_ve"].h, s_ve_n)
                te.matmul(t["ps_row"][:, 0:8], t["ones_row"][:], t["ab_rows_bf"][0:1, 0:8],
                          start=True, stop=True)
                te.matmul(t["ps_row"][:, 8:136], t["ones_row"][:], t["ab_rows_bf"][0:1, 8:136],
                          start=True, stop=True).then_inc(s_mm.h, 1)
            s_mm.n += 1

        with nc.Block() as blk:
            n_mm = t["s_mm"].n

            @blk.scalar
            def _(sc):
                sc.wait_ge(t["s_mm"].h, n_mm)
                sc.copy(t["rmask"][:], t["ps_row"][:, 0:8])
                sc.copy(t["Wonw_bc"][:], t["ps_row"][:, 8:136])

    def ln(self, tapn):
        nc, t = self.nc, self.t
        s_mm, s_act, s_ve, s_row, s_g = t["s_mm"], t["s_act"], t["s_ve"], t["s_row"], t["s_g"]
        src, dst = "x_bf", "xn"

        with nc.Block() as blk:
            mm_base, act_base = s_mm.n, s_act.n

            @blk.scalar
            def _(sc):
                for i in range(2 * NCH):
                    c, tb = i // 2, i % 2
                    if i >= 4:
                        sc.wait_ge(s_mm.h, mm_base + (i - 4) + 1)
                    sc.activation(t["sq_scr"][:, (i % 4) * 512:(i % 4) * 512 + 512],
                                  t[src][:, c * TPC + tb * 512: c * TPC + tb * 512 + 512],
                                  AF.Square).then_inc(s_act.h, 1)
            s_act.n += 2 * NCH

            @blk.tensor
            def _(te):
                for i in range(2 * NCH):
                    c, tb = i // 2, i % 2
                    te.matmul(t["ps_a" if tb == 0 else "ps_b"][0:1, :],
                              t["ones_col"][:],
                              t[src][:, c * TPC + tb * 512: c * TPC + tb * 512 + 512],
                              start=(c == 0), stop=(c == NCH - 1))
                    te.wait_ge(s_act.h, act_base + i + 1)
                    te.matmul(t["ps_c" if tb == 0 else "ps_row"][0:1, :],
                              t["ones_col"][:],
                              t["sq_scr"][:, (i % 4) * 512:(i % 4) * 512 + 512],
                              start=(c == 0), stop=(c == NCH - 1)).then_inc(s_mm.h, 1)
            s_mm.n += 2 * NCH

        with nc.Block() as blk:
            @blk.vector
            def _(v):
                v.tensor_copy(t["row_scr"][0:1, 0:512], t["ps_a"][0:1, :])
                v.tensor_copy(t["row_scr"][0:1, 512:1024], t["ps_b"][0:1, :])
                v.tensor_copy(t["row_scr"][0:1, 1024:1536], t["ps_c"][0:1, :])
                v.tensor_copy(t["row_scr"][0:1, 1536:2048], t["ps_row"][0:1, :])
                v.tensor_scalar_mul(t["ab_rows"][0:1, 0:1024], t["row_scr"][0:1, 0:1024], 1.0 / D)
                v.tensor_scalar_mul(t["ab_rows"][0:1, 1024:2048], t["row_scr"][0:1, 1024:2048], 1.0 / D)
                v.tensor_mul(t["row_scr"][0:1, 0:1024], t["ab_rows"][0:1, 0:1024], t["ab_rows"][0:1, 0:1024])
                v.engine_nop()
                v.engine_nop()
                v.tensor_sub(t["row_scr"][0:1, 1024:2048], t["ab_rows"][0:1, 1024:2048],
                             t["row_scr"][0:1, 0:1024])
                v.engine_nop()
                v.engine_nop().then_inc(s_row.h, 1)
                v.wait_ge(s_act.h, s_act.n + 1)
                v.reciprocal(t["row_scr"][0:1, 1024:2048], t["row_scr"][0:1, 0:1024])
                v.engine_nop()
                v.engine_nop().then_inc(s_row.h, 1)
            s_row.n += 2

            @blk.scalar
            def _(sc):
                sc.wait_ge(s_row.h, s_row.n - 1)
                sc.activation(t["row_scr"][0:1, 0:1024], t["row_scr"][0:1, 1024:2048],
                              AF.Sqrt, bias=t["cst"][0:1, 0:1]).then_inc(s_act.h, 1)
            s_act.n += 1

            @blk.gpsimd
            def _(g):
                g.wait_ge(s_row.h, s_row.n)
                g.tensor_mul(t["ab_rows"][0:1, 1024:2048], t["ab_rows"][0:1, 0:1024],
                             t["row_scr"][0:1, 1024:2048])
                g.engine_nop()
                g.engine_nop()
                g.tensor_scalar_mul(t["ab_rows"][0:1, 1024:2048], t["ab_rows"][0:1, 1024:2048], -1.0)
                g.tensor_copy(t["ab_rows"][0:1, 0:1024], t["row_scr"][0:1, 1024:2048])
                g.engine_nop()
                g.engine_nop()
                g.tensor_copy(t["ab_rows_bf"][0:1, 0:2048], t["ab_rows"][0:1, 0:2048])
                g.engine_nop()
                g.engine_nop().then_inc(s_g.h, 1)
            s_g.n += 1

        with nc.Block() as blk:
            banks = ["ps_a", "ps_b", "ps_c", "ps_row"]

            @blk.tensor
            def _(te):
                te.wait_ge(s_g.h, s_g.n)
                for half in range(4):
                    te.matmul(t[banks[half]][:, 0:512], t["ones_row"][:],
                              t["ab_rows_bf"][0:1, half * 512:(half + 1) * 512],
                              start=True, stop=True).then_inc(s_mm.h, 1)
            s_mm.n += 4

        with nc.Block() as blk:
            @blk.vector
            def _(v):
                for c in range(NCH):
                    for tb in range(2):
                        sl = slice(c * TPC + tb * 512, c * TPC + tb * 512 + 512)
                        v.tensor_mul(t[dst][:, sl], t[src][:, sl], t[banks[tb]][:, 0:512])
                        v.tensor_add(t[dst][:, sl], t[dst][:, sl], t[banks[2 + tb]][:, 0:512])

        self.tap(tapn, lambda: t[dst][:], [128, NCH * TPC], BF16)

    def ffn(self):
        t = self.t

        def rhs(i, tb):
            return t["xn"][:, i * TPC + tb * 512: i * TPC + tb * 512 + 512]

        def evac(v, oc, tb, ps):
            xsl = t["x_res"][:, oc * TPC + tb * 512: oc * TPC + tb * 512 + 512]
            v.scalar_tensor_tensor(xsl, ps, t["ffn_sc"][:, oc:oc + 1], xsl,
                                   ALU.mult, ALU.add)
            v.tensor_scalar_add(xsl, xsl, t["ffn_bi"][:, oc:oc + 1])
            v.tensor_copy(t["x_bf"][:, oc * TPC + tb * 512: oc * TPC + tb * 512 + 512], xsl)

        self.proj_lhsT("ffn_w", NCH, 128, list(range(NCH)), rhs, evac)

    def attn_a(self):
        nc, t, d = self.nc, self.t, self.d
        K = 1024

        def at(name, shape, dtype, offk):
            off = int(offk * K)
            nbytes = int(np.prod(shape[1:])) * (2 if dtype == BF16 else 4)
            assert off + nbytes <= self.asize, (name, off, nbytes, self.asize)
            h = self.nc.alloc_sbuf_tensor_at(name, shape, dtype, offset=self.abase + off)
            self.t[name] = h
            return h
        self.aat = at

        at("aq", [128, 8 * 1024], BF16, 0)
        at("ak", [128, 8 * 2048], BF16, 16)
        at("bpad", [16, 1024], F32, 48)
        at("grow", [8, 1024], F32, 52)
        at("Lp", [8, 2080], F32, 56)
        at("rowA", [8, 2048], F32, 65)
        at("colA", [8, 2048], F32, 73)
        at("rowAq", [8, 1024], F32, 81)
        at("b2il", [8, 2048], BF16, 85)
        at("toks", [128, 6 * 128], F32, 89)
        at("lamCbc", [128, 128], F32, 92)
        at("rows2", [8, 64], F32, 92.5)
        at("rowAfl", [1, 2 * 1024], F32, 93)
        at("rowAqfl", [1, 2 * 512], F32, 101)
        at("pscr", [128, 2 * 1024], BF16, 105)
        at("cacc", [128, 2 * 1024], F32, 109)
        at("vscr", [128, 2048], BF16, 117)
        at("bpadB", [8, 1024], F32, 117)
        at("sqs", [128, 2 * 1024], BF16, 121)
        at("tailpk", [128, 128], F32, 142)
        at("halo", [128, 128], F32, 142.5)
        at("evsc", [128, 4 * 512], BF16, 0)
        at("c6", [128, 16], F32, 130)
        at("vin", [128, 2 * 128], BF16, 131)
        at("ktk", [128, 2 * 2 * 128], BF16, 131.5)
        at("bvt", [128, 2 * 128], BF16, 132.5)
        at("earg", [128, 2 * 2 * 128], F32, 133)
        at("abf", [128, 2 * 2 * 128], BF16, 135)
        at("sol", [128, 2 * 10 * 128], BF16, 136)
        at("vw", [128, 2 * 2 * 128], BF16, 141)
        at("FM", [128, 2 * 8 * 128], F32, 142)
        at("FMbf", [128, 2 * 8 * 128], BF16, 48)
        at("Sbf", [128, 8 * 128], BF16, 52)
        at("relst", [128, 2048], F32, 105)
        at("Sin", [128, 8 * 128], F32, 150)
        at("ofm_a", [128, 4 * 1024], BF16, 105)
        at("ofm_b", [128, 4 * 1024], BF16, 117)
        at("ggb", [128, 2 * 1024], BF16, 113)
        at("vwin", [128, 2 * 2 * 128], BF16, 126)
        at("vp2", [128, 2 * 128], BF16, 127)
        at("otf", [64, 2 * 128], F32, 127.5)
        at("gtf", [64, 2 * 128], F32, 128.5)
        at("obf", [64, 2 * 128], BF16, 129.5)
        at("rmsc", [64, 8], F32, 130)

        def ofm_tile(i):
            return (t["ofm_a"][:, i * 1024:(i + 1) * 1024] if i < 4
                    else t["ofm_b"][:, (i - 4) * 1024:(i - 3) * 1024])
        self.ofm_tile = ofm_tile

        self.asc = {
            "qp": nc.dram_tensor("qp_st", [8, 128, 1024], BF16),
            "kp": nc.dram_tensor("kp_st", [16, 128, 1024], BF16),
            "vp": nc.dram_tensor("vp_st", [16, 128, 1024], BF16),
            "gg": nc.dram_tensor("gg_st", [8, 128, 1024], BF16),
            "va": nc.dram_tensor("va_st", [8, 128, 2048], BF16),
            "vw": nc.dram_tensor("vw_st", [NAC * HA, 128, 256], BF16),
        }

        self.a_park_and_proj()
        if self.stop == "proj":
            return
        self.a_halo()
        self.a_conv()
        self.tap("aq", lambda: t["aq"][:], [128, 8 * 1024], BF16)
        self.tap("ak", lambda: t["ak"][:], [128, 8 * 2048], BF16)
        if self.stop == "conv":
            return
        self.a_pads()
        self.tap("Lp", lambda: t["Lp"][0:8, :], [8, 2080], F32)
        self.tap("rowA", lambda: t["rowA"][0:8, :], [8, 2048], F32)
        self.tap("rowAq", lambda: t["rowAq"][0:8, :], [8, 1024], F32)
        self.tap("colA", lambda: t["colA"][0:8, :], [8, 2048], F32)
        self.tap("toks", lambda: t["toks"][:], [128, 6 * 128], F32)
        if self.stop == "pads":
            return
        self.a_pass1()
        self.tap("FM", lambda: t["FM"][:], [128, 2048], F32)
        if "vwst" in self.debug_taps:
            out = nc.dram_tensor("tap_vwst", [NAC * HA, 128, 256], BF16, kind="ExternalOutput")
            sem = Sem(self.ctx.enter_context(nc.semaphore("tapsem_vwst")))
            with nc.Block() as blk:
                @blk.sync
                def _(s):
                    s.dma_start(out=out.ap(), in_=self.asc["vw"].ap()).then_inc(sem.h, 16)
                    s.wait_ge(sem.h, 16)
        if self.stop == "pass1":
            return
        self.a_relay()
        self.tap("Sin", lambda: t["Sin"][:], [128, 1024], F32)
        if self.stop == "relay":
            return
        self.a_pass2()
        self.tap("ofm_a", lambda: t["ofm_a"][:], [128, 4096], BF16)
        self.tap("ofm_b", lambda: t["ofm_b"][:], [128, 4096], BF16)
        if self.stop == "pass2":
            return
        self.a_wo()
        self.tap("x1", lambda: t["x_res"][:], [128, NCH * TPC], F32)

    def a_park_and_proj(self):
        nc, t = self.nc, self.t
        s_pk, s_ev, s_st, s_act = t["s_pk"], t["s_ev"], t["s_st"], t["s_act"]

        with nc.Block() as blk:
            @blk.scalar
            def _(se):
                for c in range(NCH):
                    se.dma_start(out=self.x_park.ap()[c * 128:(c + 1) * 128, :],
                                 in_=t["x_res"][:, c * TPC:(c + 1) * TPC]).then_inc(s_pk.h, 16)
                se.wait_ge(s_pk.h, s_pk.n + 16 * NCH)
            s_pk.n += 16 * NCH

        def rhs(i, tb):
            return t["xn"][:, i * TPC + tb * 512: i * TPC + tb * 512 + 512]

        def mk_evac(base_slot):
            def evac(v, oc, tb, ps):
                gi = oc * 2 + tb
                v.tensor_copy(t["evsc"][:, (gi % 4) * 512:(gi % 4) * 512 + 512], ps)
                if tb == 1:
                    v.tensor_copy(t["tailpk"][:, (base_slot + oc) * 3:(base_slot + oc) * 3 + 3],
                                  ps[:, 509:512])
            return evac

        def mk_store(stname):
            def store(se, oc, tb, ev_t):
                gi = oc * 2 + tb
                se.wait_ge(s_ev.h, ev_t)
                se.dma_start(out=self.asc[stname].ap()[oc][:, tb * 512:tb * 512 + 512],
                             in_=t["evsc"][:, (gi % 4) * 512:(gi % 4) * 512 + 512]).then_inc(s_st.h, 16)
            return store

        self.proj_lhsT("Wq_a", 8, 128, list(range(NCH)), rhs, mk_evac(0), store_fn=mk_store("qp"))
        self.proj_lhsT("Wk_a", 16, 128, list(range(NCH)), rhs, mk_evac(8), store_fn=mk_store("kp"))
        self.proj_lhsT("Wv_a", 16, 128, list(range(NCH)), rhs, mk_evac(24), store_fn=mk_store("vp"))

        def evac_ba(v, oc, tb, ps):
            pass

        def scalar_ba(sc, oc, tb, mm_t):
            sc.wait_ge(t["s_mm"].h, mm_t)
            ps = t["ps_p0" if tb % 2 == 0 else "ps_p1"]
            sl = slice(tb * 512, tb * 512 + 512)
            sc.activation(t["bpad"][0:16, sl], ps[0:16, 0:512], AF.Sigmoid)
            sc.activation(t["grow"][0:8, sl], ps[32:40, 0:512], AF.Relu,
                          bias=t["dt_bias_c"][:])
            sc.activation(t["rowAq"][0:8, sl], ps[32:40, 0:512], AF.Abs,
                          bias=t["dt_bias_c"][:])
            sc.activation(t["rowAq"][0:8, sl], t["rowAq"][0:8, sl], AF.Exp, scale=-1.0)
            sc.activation(t["rowAq"][0:8, sl], t["rowAq"][0:8, sl], AF.Ln,
                          bias=t["cst"][0:8, 2:3]).then_inc(s_act.h, 1)
        self.proj_lhsT("Wba_w", 1, 40, list(range(NCH)), rhs, evac_ba, scalar_fn=scalar_ba)
        s_act.n += 2

        with nc.Block() as blk:
            n_act = s_act.n

            @blk.vector
            def _(v):
                v.wait_ge(s_act.h, n_act)
                v.tensor_scalar_mul(t["bpad"][:], t["bpad"][:], 2.0)
                v.tensor_add(t["grow"][:], t["grow"][:], t["rowAq"][0:8, :])
                v.tensor_scalar(t["grow"][:], t["grow"][:], t["negexpA"][:], None, ALU.mult)

        self.proj_rhs("Wg_a", 2, "gg")

    def proj_rhs(self, wname, n_og, stname, slabname="ak", evname="evsc", xname="xn"):
        nc, t, d = self.nc, self.t, self.d
        s_w, s_mm, s_ev, s_st = t["s_w"], t["s_mm"], t["s_ev"], t["s_st"]
        w_base, mm_base, ev_base, st_base = s_w.n, s_mm.n, s_ev.n, s_st.n
        slab = t[slabname]

        with nc.Block() as blk:
            @blk.gpsimd
            def _(g):
                for og in range(n_og):
                    if og >= 2:
                        g.wait_ge(s_ev.h, ev_base + (og - 1) * NTC)
                    g.dma_start(out=slab[:, (og % 2) * 8192:(og % 2) * 8192 + 8192],
                                in_=d[wname].ap()[og]).then_inc(s_w.h, 16)
            s_w.n += 16 * n_og

            @blk.tensor
            def _(te):
                for og in range(n_og):
                    te.wait_ge(s_w.h, w_base + 16 * (og + 1))
                    for tc in range(NTC):
                        gi = og * NTC + tc
                        ps = t["ps_p0" if gi % 2 == 0 else "ps_p1"][:, 0:512]
                        if gi >= 2:
                            te.wait_ge(s_ev.h, ev_base + gi - 1)
                        for kc in range(NCH):
                            ins = te.matmul(
                                ps,
                                t[xname][:, kc * TPC + tc * 128: kc * TPC + (tc + 1) * 128],
                                slab[:, (og % 2) * 8192 + kc * 512:(og % 2) * 8192 + (kc + 1) * 512],
                                start=(kc == 0), stop=(kc == NCH - 1))
                        ins.then_inc(s_mm.h, 1)
            s_mm.n += n_og * NTC

            @blk.vector
            def _(v):
                for gi in range(n_og * NTC):
                    v.wait_ge(s_mm.h, mm_base + gi + 1)
                    if gi >= 4:
                        v.wait_ge(s_st.h, st_base + 16 * (gi - 3))
                    ps = t["ps_p0" if gi % 2 == 0 else "ps_p1"][:, 0:512]
                    v.tensor_copy(t[evname][:, (gi % 4) * 512:(gi % 4) * 512 + 512], ps).then_inc(s_ev.h, 1)
            s_ev.n += n_og * NTC

            @blk.sync
            def _(se):
                for og in range(n_og):
                    for tc in range(NTC):
                        gi = og * NTC + tc
                        se.wait_ge(s_ev.h, ev_base + gi + 1)
                        dst = self.asc.get(stname) if hasattr(self, "asc") and stname in getattr(self, "asc", {}) else self.bsc[stname[:-2] if stname.endswith("_b") else stname]
                        se.dma_start(out=dst.ap()[tc][:, og * 512:og * 512 + 512],
                                     in_=t[evname][:, (gi % 4) * 512:(gi % 4) * 512 + 512]).then_inc(s_st.h, 16)
            s_st.n += 16 * n_og * NTC

    def a_halo(self):
        nc, t = self.nc, self.t
        s_st, s_cc, s_ld = t["s_st"], t["s_cc"], t["s_ld"]
        with nc.Block() as blk:
            @blk.sync
            def _(s):
                s.dma_start(out=self.cc["halo_a_in"].ap(), in_=t["tailpk"][:, 0:120]).then_inc(s_st.h, 16)
            s_st.n += 16

            @blk.gpsimd
            def _(g):
                g.wait_ge(s_st.h, s_st.n)
                g.collective_compute(
                    "AllGather", ALU.bypass,
                    replica_groups=[[0, 1, 2, 3], [4, 5, 6, 7]],
                    ins=[self.cc["halo_a_in"].ap()],
                    outs=[self.cc["halo_a_out"].ap()],
                ).then_inc(s_cc.h, 1)
            s_cc.n += 1

        with nc.Block() as blk:
            @blk.sync
            def _(s):
                s.wait_ge(s_cc.h, s_cc.n)
                for j in range(3):
                    s.dma_start(out=t["relst"][:, j * 128:j * 128 + 120],
                                in_=self.cc["halo_a_out"].ap()[j * 128:(j + 1) * 128, :]).then_inc(s_ld.h, 16)
            s_ld.n += 48

            @blk.vector
            def _(v):
                v.memset(t["halo"][:], 0.0)
                v.wait_ge(s_ld.h, s_ld.n)
                for j in range(3):
                    v.engine_nop()
                    v.engine_nop()
                    v.scalar_tensor_tensor(t["halo"][:, 0:120], t["relst"][:, j * 128:j * 128 + 120],
                                           t["rmask"][:, 3 + j:4 + j], t["halo"][:, 0:120],
                                           ALU.mult, ALU.add)

    def a_conv(self):
        nc, t = self.nc, self.t

        def out_slice(kind, h):
            if kind == "q":
                return t["aq"][:, h * 1024:(h + 1) * 1024]
            if kind == "k0":
                return t["ak"][:, h * 2048:h * 2048 + 2048:2]
            if kind == "k1":
                return t["ak"][:, h * 2048 + 1:h * 2048 + 2048:2]
            if kind == "v0":
                return t["vscr"][:, 0:2048:2]
            return t["vscr"][:, 1:2048:2]

        items_a = ([("q", "qp", h, h, h, h) for h in range(8)]
                   + [("k0", "kp", h, h, h, 8 + h) for h in range(8)])
        items_b = ([("k1", "kp", 8 + h, 8 + h, h, 16 + h) for h in range(8)]
                   + sum([[("v0", "vp", h, h, h, 24 + h), ("v1", "vp", 8 + h, 8 + h, h, 32 + h)]
                          for h in range(8)], []))

        self._conv_block(items_a, {"q": ("ps_a", "ps_b"), "k0": ("ps_c", "ps_row")}, "cq_ck")
        with nc.Block() as blk:
            @blk.vector
            def _(v):
                v.tensor_copy(t["rowAq"][0:8, 0:512], t["ps_a"][0:8, 0:512])
                v.tensor_copy(t["rowAq"][0:8, 512:1024], t["ps_b"][0:8, 0:512])
                v.tensor_copy(t["colA"][0:8, 0:1024:2], t["ps_c"][0:8, 0:512])
                v.tensor_copy(t["colA"][0:8, 1024:2048:2], t["ps_row"][0:8, 0:512])
        self._conv_block(items_b, {"k1": ("ps_a", "ps_b")}, "k1_v")
        with nc.Block() as blk:
            @blk.vector
            def _(v):
                v.tensor_copy(t["colA"][0:8, 1:1024:2], t["ps_a"][0:8, 0:512])
                v.tensor_copy(t["colA"][0:8, 1025:2048:2], t["ps_b"][0:8, 0:512])

    def _conv_block(self, items, normbanks, tag):
        nc, t = self.nc, self.t
        s_ld, s_ve, s_act, s_st = t["s_ld"], t["s_ve"], t["s_act"], t["s_st"]
        n_it = len(items)
        ld_b, ve_b, act_b, st_b = s_ld.n, s_ve.n, s_act.n, s_st.n
        act_after = []
        a = act_b
        for (kind, stn, sti, cwc, h, slot) in items:
            a += 1
            if kind in ("q", "k0", "k1"):
                a += 1
            act_after.append(a)
        s_act.n = a
        cwname = {"q": "cq_a", "k0": "ck_a", "k1": "ck_a", "v0": "cv_a", "v1": "cv_a"}

        with nc.Block() as blk:
            @blk.sync
            def _(s):
                for i, (kind, stn, sti, cwc, h, slot) in enumerate(items):
                    if i >= 2:
                        s.wait_ge(s_ve.h, ve_b + (i - 1))
                    s.dma_start(out=t["pscr"][:, (i % 2) * 1024:(i % 2) * 1024 + 1024],
                                in_=self.asc[stn].ap()[sti]).then_inc(s_ld.h, 16)
            s_ld.n += 16 * n_it

            @blk.vector
            def _(v):
                for i, (kind, stn, sti, cwc, h, slot) in enumerate(items):
                    v.wait_ge(s_ld.h, ld_b + 16 * (i + 1))
                    if i >= 2:
                        v.wait_ge(s_act.h, act_after[i - 2])
                    if kind == "v1":
                        pass
                    src = t["pscr"][:, (i % 2) * 1024:(i % 2) * 1024 + 1024]
                    acc = t["cacc"][:, (i % 2) * 1024:(i % 2) * 1024 + 1024]
                    wv = t[cwname[kind]]
                    v.tensor_scalar(acc[:, 3:1024], src[:, 0:1021],
                                    wv[:, cwc * 4:cwc * 4 + 1], None, ALU.mult)
                    for tau in range(1, 4):
                        v.scalar_tensor_tensor(acc[:, 3:1024], src[:, tau:tau + 1021],
                                               wv[:, cwc * 4 + tau:cwc * 4 + tau + 1],
                                               acc[:, 3:1024], ALU.mult, ALU.add)
                    v.tensor_copy(t["c6"][:, 0:3], t["halo"][:, slot * 3:slot * 3 + 3])
                    v.tensor_copy(t["c6"][:, 3:6], src[:, 0:3])
                    v.tensor_scalar(acc[:, 0:3], t["c6"][:, 0:3],
                                    wv[:, cwc * 4:cwc * 4 + 1], None, ALU.mult)
                    for tau in range(1, 4):
                        v.scalar_tensor_tensor(acc[:, 0:3], t["c6"][:, tau:tau + 3],
                                               wv[:, cwc * 4 + tau:cwc * 4 + tau + 1],
                                               acc[:, 0:3], ALU.mult, ALU.add)
                    v.engine_nop()
                    v.engine_nop().then_inc(s_ve.h, 1)
            s_ve.n += n_it

            @blk.scalar
            def _(sc):
                nst = 0
                for i, (kind, stn, sti, cwc, h, slot) in enumerate(items):
                    sc.wait_ge(s_ve.h, ve_b + i + 1)
                    if kind == "v0" and h >= 1:
                        sc.wait_ge(s_st.h, st_b + 16 * h)
                    acc = t["cacc"][:, (i % 2) * 1024:(i % 2) * 1024 + 1024]
                    sc.activation(out_slice_ := out_slice_fn(kind, h, t), acc, AF.Silu).then_inc(s_act.h, 1)
                    if kind in ("q", "k0", "k1"):
                        sc.activation(t["sqs"][:, (i % 2) * 1024:(i % 2) * 1024 + 1024],
                                      out_slice_, AF.Square).then_inc(s_act.h, 1)
                    if kind == "v1":
                        sc.wait_ge(s_act.h, act_after[i])
                        sc.dma_start(out=self.asc["va"].ap()[h], in_=t["vscr"][:]).then_inc(s_st.h, 16)
                        nst += 1
            if any(k == "v1" for k, *_ in items):
                s_st.n += 16 * 8

            @blk.tensor
            def _(te):
                for i, (kind, stn, sti, cwc, h, slot) in enumerate(items):
                    if kind not in normbanks:
                        continue
                    b0, b1 = normbanks[kind]
                    first, last = (h == 0), (h == 7)
                    te.wait_ge(s_act.h, act_after[i])
                    for tb in range(2):
                        te.matmul(t[b0 if tb == 0 else b1][0:8, 0:512],
                                  t["Ecols"][:, 8 * h:8 * h + 8],
                                  t["sqs"][:, (i % 2) * 1024 + tb * 512:(i % 2) * 1024 + tb * 512 + 512],
                                  start=first, stop=last)

    def a_pads(self):
        nc, t = self.nc, self.t
        s_ld, s_ve, s_act, s_mm, s_g = t["s_ld"], t["s_ve"], t["s_act"], t["s_mm"], t["s_g"]
        LN128 = float(np.log(128.0))

        with nc.Block() as blk:
            @blk.sync
            def _(s):
                s.dma_start(out=t["bpadB"][:], in_=t["bpad"][8:16, :]).then_inc(s_ld.h, 16)
            s_ld.n += 16

            @blk.vector
            def _(v):
                v.tensor_copy(t["b2il"][0:8, 0:2048:2], t["bpad"][0:8, :])
                v.wait_ge(s_ld.h, s_ld.n)
                v.tensor_copy(t["b2il"][0:8, 1:2048:2], t["bpadB"][:])
                v.memset(t["rowA"][0:8, :], 0.0)
                v.tensor_copy(t["rowA"][0:8, 0:2048:2], t["grow"][0:8, :])
                v.memset(t["Lp"][0:8, :], 0.0)
                v.tensor_tensor_scan(t["Lp"][0:8, 1:2049], t["rowA"][0:8, 0:2048],
                                     t["rowA"][0:8, 0:2048], 0.0, ALU.add, ALU.bypass)
                v.engine_nop()
                v.engine_nop().then_inc(s_ve.h, 1)
            s_ve.n += 1

        with nc.Block() as blk:
            @blk.vector
            def _(v):
                v.tensor_copy(t["rows2"][0:8, 32:48], t["Lp"][0:8, 128:2049:128])
                v.tensor_scalar_mul(t["rows2"][0:8, 16:32], t["Lp"][0:8, 0:1921:128], -1.0)
                v.engine_nop()
                v.engine_nop()
                v.engine_nop()
                v.tensor_add(t["rows2"][0:8, 48:64], t["rows2"][0:8, 32:48], t["rows2"][0:8, 16:32])
                v.tensor_scalar_max(t["rowAq"][0:8, :], t["rowAq"][0:8, :], 1e-24)
                v.tensor_scalar_max(t["colA"][0:8, :], t["colA"][0:8, :], 1e-24)
                v.engine_nop()
                v.engine_nop().then_inc(s_ve.h, 1)
            s_ve.n += 1

            @blk.scalar
            def _(sc):
                sc.wait_ge(s_ve.h, s_ve.n)
                sc.activation(t["rows2"][0:8, 0:16], t["rows2"][0:8, 48:64], AF.Exp)
                sc.activation(t["grow"][0:8, :], t["rowAq"][0:8, :], AF.Ln)
                sc.activation(t["cacc"][0:8, 0:2048], t["colA"][0:8, :], AF.Ln).then_inc(s_act.h, 1)
            s_act.n += 1

            @blk.gpsimd
            def _(g):
                g.wait_ge(s_act.h, s_act.n)
                g.engine_nop()
                g.engine_nop().then_inc(s_g.h, 1)
            s_g.n += 1

        with nc.Block() as blk:
            @blk.scalar
            def _(sc):
                sc.activation(t["rowA"][0:8, :], t["b2il"][0:8, :], AF.Ln).then_inc(s_act.h, 1)
            s_act.n += 1

            @blk.vector
            def _(v):
                v.scalar_tensor_tensor(t["rowAq"][0:8, :], t["grow"][0:8, :], -0.5,
                                       t["Lp"][0:8, 2:2049:2], ALU.mult, ALU.add)
                v.tensor_scalar_add(t["rowAq"][0:8, :], t["rowAq"][0:8, :], -0.5 * LN128)
                v.scalar_tensor_tensor(t["colA"][0:8, :], t["cacc"][0:8, 0:2048], -0.5,
                                       t["Lp"][0:8, 1:2049], ALU.mult, ALU.subtract)
                v.wait_ge(s_act.h, s_act.n)
                v.tensor_add(t["rowA"][0:8, :], t["rowA"][0:8, :], t["colA"][0:8, :])
                v.engine_nop()
                v.engine_nop()
                v.scalar_tensor_tensor(t["rowA"][0:8, :], t["Lp"][0:8, 1:2049], 2.0,
                                       t["rowA"][0:8, :], ALU.mult, ALU.add)

        with nc.Block() as blk:
            @blk.sync
            def _(s):
                s.dma_start(out=t["rowAqfl"][0:1, 512:640], in_=t["rows2"][0:8, 0:16]).then_inc(s_ld.h, 16)
            s_ld.n += 16

            @blk.tensor
            def _(te):
                te.wait_ge(s_ld.h, s_ld.n)
                te.matmul(t["ps_row"][:, 0:128], t["ones_rowf"][:], t["rowAqfl"][0:1, 512:640],
                          start=True, stop=True).then_inc(s_mm.h, 1)
            s_mm.n += 1

            @blk.vector
            def _(v):
                v.wait_ge(s_mm.h, s_mm.n)
                v.tensor_copy(t["lamCbc"][:], t["ps_row"][:, 0:128])

        with nc.Block() as blk:
            act_b, mm_b, ve_b = s_act.n, s_mm.n, s_ve.n

            @blk.scalar
            def _(sc):
                for c in range(NAC):
                    if c >= 1:
                        sc.wait_ge(s_mm.h, mm_b + 6 * c)
                    sc.activation(t["grow"][0:8, 0:128], t["rowA"][0:8, c * 128:(c + 1) * 128],
                                  AF.Exp, bias=t["rows2"][0:8, 16 + c:17 + c])
                    sc.activation(t["grow"][0:8, 128:256], t["colA"][0:8, c * 128:(c + 1) * 128],
                                  AF.Exp, bias=t["rows2"][0:8, 32 + c:33 + c])
                    sc.activation(t["grow"][0:8, 256:320], t["rowAq"][0:8, c * 64:(c + 1) * 64],
                                  AF.Exp, bias=t["rows2"][0:8, 16 + c:17 + c]).then_inc(s_act.h, 1)
            s_act.n += NAC

            @blk.tensor
            def _(te):
                for c in range(NAC):
                    te.wait_ge(s_act.h, act_b + c + 1)
                    if c >= 1:
                        te.wait_ge(s_ve.h, ve_b + 6 * c)
                    srcs = [
                        (t["rowA"][0:8, c * 128:(c + 1) * 128], "ps_c", False),
                        (t["colA"][0:8, c * 128:(c + 1) * 128], "ps_row", False),
                        (t["b2il"][0:8, c * 128:(c + 1) * 128], "ps_t0", True),
                        (t["grow"][0:8, 0:128], "ps_c", False),
                        (t["grow"][0:8, 128:256], "ps_row", False),
                        (t["grow"][0:8, 256:320], "ps_c", False),
                    ]
                    for qi, (ap, bank, isbf) in enumerate(srcs):
                        ident = t["ident"][0:8, 0:8] if isbf else t["identf"][0:8, 0:8]
                        nrow = 64 if qi == 5 else 128
                        te.wait_ge(s_ve.h, ve_b + 6 * c + max(0, qi - 1))
                        te.transpose(t[bank][0:nrow, 0:8], ap, ident).then_inc(s_mm.h, 1)
            s_mm.n += 6 * NAC

            @blk.vector
            def _(v):
                for c in range(NAC):
                    for qi in range(6):
                        v.wait_ge(s_mm.h, mm_b + 6 * c + qi + 1)
                        bank = ["ps_c", "ps_row", "ps_t0", "ps_c", "ps_row", "ps_c"][qi]
                        nrow = 64 if qi == 5 else 128
                        v.tensor_copy(t["toks"][0:nrow, qi * 128 + c * 8: qi * 128 + c * 8 + 8],
                                      t[bank][0:nrow, 0:8])
                        v.engine_nop()
                        v.engine_nop().then_inc(s_ve.h, 1)
            s_ve.n += 6 * NAC

    def a_pass1(self):
        import os
        NACL = int(os.environ.get("ATTNA_NAC", NAC))
        nc, t = self.nc, self.t
        s_ld, s_ve, s_act, s_mm, s_st, s_g = (t["s_ld"], t["s_ve"], t["s_act"],
                                              t["s_mm"], t["s_st"], t["s_g"])
        NQ = NAC * HA

        with nc.Block() as blk:
            @blk.vector
            def _(v):
                for h in range(HA):
                    v.tensor_copy(t["FM"][:, h * 128:(h + 1) * 128], t["identf"][:])
                    v.tensor_copy(t["FMbf"][:, h * 128:(h + 1) * 128], t["identf"][:])
                v.memset(t["FM"][:, 1024:2048], 0.0)
                v.memset(t["FMbf"][:, 1024:2048], 0.0)

        ev = {}
        cmm = s_mm.n
        cve = s_ve.n
        cact = s_act.n
        cld = s_ld.n
        cst = t["s_x"].n
        cg = s_g.n
        for c in range(NACL):
            cld += 16; ev[(c, "rowfl")] = cld
            cmm += 2; ev[(c, "rbc")] = cmm
            for hp in range(HA // 2):
                qq = (c * HA + 2 * hp, c * HA + 2 * hp + 1)
                for q in qq:
                    cld += 16; ev[(q, "vload")] = cld
                for q in qq:
                    cmm += 1; ev[(q, "ktr")] = cmm
                    cmm += 1; ev[(q, "vtr")] = cmm
                for q in qq:
                    cmm += 1; ev[(q, "kk")] = cmm
                for q in qq:
                    cmm += 1; ev[(q, "r0tr")] = cmm
                for j in range(1, 7):
                    for q in qq:
                        cmm += 1; ev[(q, f"sq{j}")] = cmm
                for j in range(5, 0, -1):
                    for q in qq:
                        cmm += 1; ev[(q, f"g{j}")] = cmm
                for name in ("tfin", "v0", "w2t", "w2tok", "pct", "mc", "pm", "pf"):
                    for q in qq:
                        cmm += 1; ev[(q, name)] = cmm
                for q in qq:
                    cve += 1; ev[(q, "ktil")] = cve
                    cve += 1; ev[(q, "bv")] = cve
                for q in qq:
                    cve += 1; ev[(q, "arg")] = cve
                for q in qq:
                    cve += 1; ev[(q, "abf")] = cve
                for q in qq:
                    cve += 1; ev[(q, "r0")] = cve
                for j in range(1, 7):
                    for q in qq:
                        cve += 1; ev[(q, f"sev{j}")] = cve
                for q in qq:
                    cve += 1; ev[(q, "ginit")] = cve
                for j in range(5, 0, -1):
                    for q in qq:
                        cve += 1; ev[(q, f"gev{j}")] = cve
                for name in ("tev", "v0ev", "w2tokev", "mcomb", "fcomb"):
                    for q in qq:
                        cve += 1; ev[(q, name)] = cve
                for q in qq:
                    cact += 1; ev[(q, "w2k")] = cact
                for q in qq:
                    cact += 1; ev[(q, "exp")] = cact
                for j in range(1, 6):
                    for q in qq:
                        cact += 1; ev[(q, f"rev{j}")] = cact
                for q in qq:
                    cact += 1; ev[(q, "w2tev")] = cact
                for q in qq:
                    cact += 1; ev[(q, "pctev")] = cact
                for q in qq:
                    cg += 1; ev[(q, "argdone")] = cg
                for q in qq:
                    cst += 16; ev[(q, "vwst")] = cst
        self._p1ev = ev

        def tk(q, name):
            return ev[(q, name)]

        def bank(p):
            return t["ps_a" if p == 0 else "ps_b"]

        def cbank(p):
            return t["ps_c" if p == 0 else "ps_row"]

        def tbank(p, which):
            return t["ps_t0" if which == 0 else "ps_t1"]

        def sol(p, slot):
            return t["sol"][:, (p * 10 + slot) * 128:(p * 10 + slot) * 128 + 128]

        def abf(p, i):
            return t["abf"][:, (p * 2 + i) * 128:(p * 2 + i) * 128 + 128]

        def earg(p, i):
            return t["earg"][:, (p * 2 + i) * 128:(p * 2 + i) * 128 + 128]

        def ktk(p, i):
            return t["ktk"][:, (p * 2 + i) * 128:(p * 2 + i) * 128 + 128]

        def vw(p, i):
            return t["vw"][:, (p * 2 + i) * 128:(p * 2 + i) * 128 + 128]

        def tokcol(qty, c, h, nrow=128):
            return t["toks"][0:nrow, qty * 128 + c * 8 + h: qty * 128 + c * 8 + h + 1]

        def lamcol(c, h):
            return t["lamCbc"][:, h * 16 + c: h * 16 + c + 1]

        with nc.Block() as blk:
            @blk.sync
            def _(s):
                for c in range(NACL):
                    if c >= 2:
                        s.wait_ge(s_mm.h, tk(c - 1, "rbc"))
                    s.dma_start(out=t["rowAfl"][0:1, (c % 2) * 1024:(c % 2) * 1024 + 1024],
                                in_=t["rowA"][0:8, c * 128:(c + 1) * 128]).then_inc(s_ld.h, 16)
                    for h in range(HA):
                        q = c * HA + h
                        if q >= 2:
                            s.wait_ge(s_mm.h, tk(q - 2, "vtr"))
                        s.dma_start(out=t["vin"][:, (q % 2) * 128:(q % 2) * 128 + 128],
                                    in_=self.asc["va"].ap()[h][:, c * 128:(c + 1) * 128]).then_inc(s_ld.h, 16)
            s_ld.n = cld
            t["s_x"].n = cst

            @blk.tensor
            def _(te):
                for c in range(NACL):
                    te.wait_ge(s_ld.h, tk(c, "rowfl"))
                    if c >= 1:
                        te.wait_ge(s_g.h, tk(c * HA - 1, "argdone"))
                    for half in range(2):
                        te.matmul(t["ps_p0" if half == 0 else "ps_p1"][:, 0:512],
                                  t["ones_rowf"][:],
                                  t["rowAfl"][0:1, (c % 2) * 1024 + half * 512:(c % 2) * 1024 + half * 512 + 512],
                                  start=True, stop=True).then_inc(s_mm.h, 1)
                    for hp in range(HA // 2):
                        qq = (c * HA + 2 * hp, c * HA + 2 * hp + 1)
                        for q in qq:
                            p = q % 2
                            h = q - c * HA
                            if q >= 1:
                                te.wait_ge(s_act.h, tk(q - 1, "w2k"))
                            te.transpose(tbank(p, 0)[:, 0:128],
                                         t["ak"][:, h * 2048 + c * 128:h * 2048 + (c + 1) * 128],
                                         t["ident"][:]).then_inc(s_mm.h, 1)
                            te.wait_ge(s_ld.h, tk(q, "vload"))
                            if q >= 1:
                                te.wait_ge(s_ve.h, tk(q - 1, "r0" if p == 0 else "bv"))
                            te.transpose(tbank(p, 1)[:, 0:128],
                                         t["vin"][:, (q % 2) * 128:(q % 2) * 128 + 128],
                                         t["ident"][:]).then_inc(s_mm.h, 1)
                        for q in qq:
                            p = q % 2
                            h = q - c * HA
                            if q >= 2:
                                te.wait_ge(s_ve.h, tk(q - 2, "fcomb"))
                            te.matmul(bank(p)[:, 0:128],
                                      t["ak"][:, h * 2048 + c * 128:h * 2048 + (c + 1) * 128],
                                      t["ak"][:, h * 2048 + c * 128:h * 2048 + (c + 1) * 128],
                                      start=True, stop=True).then_inc(s_mm.h, 1)
                        for q in qq:
                            p = q % 2
                            te.wait_ge(s_ve.h, tk(q, "abf") if p == 0 else tk(q - 1, "r0"))
                            te.transpose(tbank(p, 1)[:, 0:128], abf(p, 0), t["ident"][:]).then_inc(s_mm.h, 1)
                        for j in range(1, 7):
                            for q in qq:
                                p = q % 2
                                te.wait_ge(s_ve.h, tk(q, "r0" if j == 1 else f"sev{j-1}"))
                                if j > 1:
                                    te.wait_ge(s_act.h, tk(q, f"rev{j-1}"))
                                Sprev = abf(p, 0) if j == 1 else sol(p, 8)
                                Rprev = abf(p, 1) if j == 1 else sol(p, j - 2)
                                ins = te.matmul(bank(p)[:, 0:128], Rprev, Sprev, start=True, stop=True)
                                if j < 6:
                                    ins = te.matmul(cbank(p)[:, 0:128], Sprev, Rprev, start=True, stop=True)
                                ins.then_inc(s_mm.h, 1)
                        for j in range(5, 0, -1):
                            for q in qq:
                                p = q % 2
                                te.wait_ge(s_ve.h, tk(q, "ginit" if j == 5 else f"gev{j+1}"))
                                te.wait_ge(s_act.h, tk(q, f"rev{j}"))
                                te.matmul(bank(p)[:, 0:128], sol(p, j - 1), sol(p, 6),
                                          start=True, stop=True).then_inc(s_mm.h, 1)
                        for q in qq:
                            p = q % 2
                            te.wait_ge(s_ve.h, tk(q, "gev1"))
                            te.matmul(bank(p)[:, 0:128], abf(p, 1), sol(p, 6),
                                      start=True, stop=True).then_inc(s_mm.h, 1)
                        for q in qq:
                            p = q % 2
                            te.wait_ge(s_ve.h, tk(q, "tev"))
                            te.wait_ge(s_act.h, tk(q, "rev5"))
                            if q >= 2:
                                te.wait_ge(t["s_x"].h, tk(q - 2, "vwst"))
                                te.wait_ge(s_ve.h, tk(q - 2, "mcomb"))
                            te.matmul(cbank(p)[:, 0:128], sol(p, 7), t["bvt"][:, p * 128:p * 128 + 128],
                                      start=True, stop=True).then_inc(s_mm.h, 1)
                        for q in qq:
                            p = q % 2
                            te.wait_ge(s_ve.h, tk(q, "v0ev"))
                            te.matmul(cbank(p)[:, 0:128], ktk(p, 1), sol(p, 7),
                                      start=True, stop=True).then_inc(s_mm.h, 1)
                        for q in qq:
                            p = q % 2
                            te.wait_ge(s_act.h, tk(q, "w2tev"))
                            te.matmul(cbank(p)[:, 0:128], sol(p, 7), ktk(p, 1),
                                      start=True, stop=True).then_inc(s_mm.h, 1)
                        for q in qq:
                            p = q % 2
                            te.wait_ge(s_ve.h, tk(q, "w2tokev"))
                            te.matmul(cbank(p)[:, 0:128], abf(p, 0), ktk(p, 0),
                                      start=True, stop=True).then_inc(s_mm.h, 1)
                        for q in qq:
                            p = q % 2
                            te.wait_ge(s_act.h, tk(q, "pctev"))
                            te.matmul(cbank(p)[:, 0:128], ktk(p, 0), vw(p, 0),
                                      start=True, stop=True).then_inc(s_mm.h, 1)
                        for q in qq:
                            p = q % 2
                            h = q - c * HA
                            te.matmul(bank(p)[:, 0:128], sol(p, 8), t["FMbf"][:, 1024 + h * 128:1024 + (h + 1) * 128],
                                      start=True, stop=True).then_inc(s_mm.h, 1)
                        for q in qq:
                            p = q % 2
                            h = q - c * HA
                            te.wait_ge(s_ve.h, tk(q, "mcomb"))
                            te.matmul(bank(p)[:, 0:128], sol(p, 8), t["FMbf"][:, h * 128:(h + 1) * 128],
                                      start=True, stop=True).then_inc(s_mm.h, 1)
            s_mm.n = cmm

            @blk.vector
            def _(v):
                for c in range(NACL):
                    for hp in range(HA // 2):
                        qq = (c * HA + 2 * hp, c * HA + 2 * hp + 1)
                        for q in qq:
                            p = q % 2
                            h = q - c * HA
                            v.wait_ge(s_mm.h, tk(q, "ktr"))
                            v.tensor_scalar(ktk(p, 0), tbank(p, 0)[:, 0:128], tokcol(4, c, h), None, ALU.mult)
                            v.engine_nop()
                            v.engine_nop().then_inc(s_ve.h, 1)
                            v.wait_ge(s_mm.h, tk(q, "vtr"))
                            v.tensor_scalar(t["bvt"][:, p * 128:p * 128 + 128], tbank(p, 1)[:, 0:128],
                                            tokcol(2, c, h), None, ALU.mult)
                            v.engine_nop()
                            v.engine_nop().then_inc(s_ve.h, 1)
                        for q in qq:
                            p = q % 2
                            h = q - c * HA
                            v.wait_ge(s_mm.h, tk(c, "rbc"))
                            rows = t["ps_p0" if h < 4 else "ps_p1"]
                            v.tensor_add(earg(p, 0), rows[:, (h % 4) * 128:(h % 4) * 128 + 128], t["maskS"][:])
                            v.engine_nop()
                            v.engine_nop().then_inc(s_ve.h, 1)
                        for q in qq:
                            p = q % 2
                            v.wait_ge(s_act.h, tk(q, "exp"))
                            v.wait_ge(s_mm.h, tk(q, "kk"))
                            v.tensor_mul(abf(p, 0), bank(p)[:, 0:128], earg(p, 1))
                            v.engine_nop()
                            v.engine_nop().then_inc(s_ve.h, 1)
                        for q in qq:
                            p = q % 2
                            v.wait_ge(s_mm.h, tk(q, "r0tr"))
                            v.tensor_copy(abf(p, 1), tbank(p, 1)[:, 0:128])
                            v.engine_nop()
                            v.engine_nop().then_inc(s_ve.h, 1)
                        for j in range(1, 7):
                            for q in qq:
                                p = q % 2
                                v.wait_ge(s_mm.h, tk(q, f"sq{j}"))
                                v.tensor_copy(sol(p, 8), bank(p)[:, 0:128])
                                v.engine_nop()
                                v.engine_nop().then_inc(s_ve.h, 1)
                        for q in qq:
                            p = q % 2
                            v.tensor_add(sol(p, 6), sol(p, 8), t["ident"][:])
                            v.engine_nop()
                            v.engine_nop().then_inc(s_ve.h, 1)
                        for j in range(5, 0, -1):
                            for q in qq:
                                p = q % 2
                                v.wait_ge(s_mm.h, tk(q, f"g{j}"))
                                v.tensor_add(sol(p, 6), sol(p, 6), bank(p)[:, 0:128])
                                v.engine_nop()
                                v.engine_nop().then_inc(s_ve.h, 1)
                        for q in qq:
                            p = q % 2
                            v.wait_ge(s_mm.h, tk(q, "tfin"))
                            v.tensor_sub(sol(p, 7), sol(p, 6), bank(p)[:, 0:128])
                            v.engine_nop()
                            v.engine_nop().then_inc(s_ve.h, 1)
                        for q in qq:
                            p = q % 2
                            v.wait_ge(s_mm.h, tk(q, "v0"))
                            v.tensor_copy(vw(p, 0), cbank(p)[:, 0:128])
                            v.engine_nop()
                            v.engine_nop().then_inc(s_ve.h, 1)
                        for q in qq:
                            p = q % 2
                            v.wait_ge(s_mm.h, tk(q, "w2tok"))
                            v.tensor_copy(abf(p, 0), cbank(p)[:, 0:128])
                            v.engine_nop()
                            v.engine_nop().then_inc(s_ve.h, 1)
                        for q in qq:
                            p = q % 2
                            h = q - c * HA
                            v.wait_ge(s_mm.h, tk(q, "pm"))
                            v.scalar_tensor_tensor(t["FM"][:, 1024 + h * 128:1024 + (h + 1) * 128],
                                                   t["FM"][:, 1024 + h * 128:1024 + (h + 1) * 128],
                                                   lamcol(c, h), bank(p)[:, 0:128], ALU.mult, ALU.subtract)
                            v.engine_nop()
                            v.engine_nop()
                            v.tensor_add(t["FM"][:, 1024 + h * 128:1024 + (h + 1) * 128],
                                         t["FM"][:, 1024 + h * 128:1024 + (h + 1) * 128],
                                         cbank(p)[:, 0:128])
                            v.engine_nop()
                            v.engine_nop()
                            v.tensor_copy(t["FMbf"][:, 1024 + h * 128:1024 + (h + 1) * 128],
                                          t["FM"][:, 1024 + h * 128:1024 + (h + 1) * 128])
                            v.engine_nop()
                            v.engine_nop().then_inc(s_ve.h, 1)
                        for q in qq:
                            p = q % 2
                            h = q - c * HA
                            v.wait_ge(s_mm.h, tk(q, "pf"))
                            v.scalar_tensor_tensor(t["FM"][:, h * 128:(h + 1) * 128],
                                                   t["FM"][:, h * 128:(h + 1) * 128],
                                                   lamcol(c, h), bank(p)[:, 0:128], ALU.mult, ALU.subtract)
                            v.engine_nop()
                            v.engine_nop()
                            v.tensor_copy(t["FMbf"][:, h * 128:(h + 1) * 128], t["FM"][:, h * 128:(h + 1) * 128])
                            v.engine_nop()
                            v.engine_nop().then_inc(s_ve.h, 1)
            s_ve.n = cve

            @blk.scalar
            def _(sc):
                for c in range(NACL):
                    for hp in range(HA // 2):
                        qq = (c * HA + 2 * hp, c * HA + 2 * hp + 1)
                        for q in qq:
                            p = q % 2
                            h = q - c * HA
                            sc.wait_ge(s_ve.h, tk(q, "ktil"))
                            sc.activation(ktk(p, 1), tbank(p, 0)[:, 0:128], AF.Copy,
                                          scale=tokcol(3, c, h)).then_inc(s_act.h, 1)
                        for q in qq:
                            p = q % 2
                            h = q - c * HA
                            sc.wait_ge(s_ve.h, tk(q, "arg"))
                            sc.activation(earg(p, 1), earg(p, 0), AF.Exp,
                                          bias=tokcol(1, c, h)).then_inc(s_act.h, 1)
                        for j in range(1, 6):
                            for q in qq:
                                p = q % 2
                                sc.wait_ge(s_mm.h, tk(q, f"sq{j}"))
                                sc.activation(sol(p, j - 1), cbank(p)[:, 0:128], AF.Copy).then_inc(s_act.h, 1)
                        for q in qq:
                            p = q % 2
                            sc.wait_ge(s_mm.h, tk(q, "w2t"))
                            sc.activation(vw(p, 1), cbank(p)[:, 0:128], AF.Copy).then_inc(s_act.h, 1)
                        for q in qq:
                            p = q % 2
                            sc.wait_ge(s_mm.h, tk(q, "pct"))
                            sc.activation(sol(p, 8), cbank(p)[:, 0:128], AF.Copy).then_inc(s_act.h, 1)
            s_act.n = cact

            @blk.gpsimd
            def _(g):
                for c in range(NACL):
                    for hp in range(HA // 2):
                        qq = (c * HA + 2 * hp, c * HA + 2 * hp + 1)
                        for q in qq:
                            g.wait_ge(s_ve.h, tk(q, "arg"))
                            g.engine_nop()
                            g.engine_nop().then_inc(s_g.h, 1)
                        for q in qq:
                            g.wait_ge(s_act.h, tk(q, "w2tev"))
                            g.dma_start(out=self.asc["vw"].ap()[q],
                                        in_=t["vw"][:, (q % 2) * 256:(q % 2) * 256 + 256]).then_inc(t["s_x"].h, 16)
            s_g.n = cg

    def a_relay(self):
        nc, t = self.nc, self.t
        s_mm, s_ve, s_st, s_cc, s_ld = t["s_mm"], t["s_ve"], t["s_st"], t["s_cc"], t["s_ld"]
        with nc.Block() as blk:
            mm_b, ve_b = s_mm.n, s_ve.n

            @blk.tensor
            def _(te):
                for h in range(HA):
                    if h >= 2:
                        te.wait_ge(s_ve.h, ve_b + h - 1)
                    te.transpose(t["ps_c" if h % 2 == 0 else "ps_row"][:, 0:128],
                                 t["FM"][:, h * 128:(h + 1) * 128], t["identf"][:]).then_inc(s_mm.h, 1)
            s_mm.n += HA

            @blk.vector
            def _(v):
                for h in range(HA):
                    v.wait_ge(s_mm.h, mm_b + h + 1)
                    v.tensor_copy(t["relst"][:, h * 128:(h + 1) * 128],
                                  t["ps_c" if h % 2 == 0 else "ps_row"][:, 0:128])
                    v.engine_nop()
                    v.engine_nop().then_inc(s_ve.h, 1)
                    v.tensor_copy(t["relst"][:, 1024 + h * 128:1024 + (h + 1) * 128],
                                  t["FM"][:, 1024 + h * 128:1024 + (h + 1) * 128])
            s_ve.n += HA

        with nc.Block() as blk:
            @blk.sync
            def _(s):
                s.dma_start(out=self.cc["rel_a_in"].ap(), in_=t["relst"][:]).then_inc(s_st.h, 16)
            s_st.n += 16

            @blk.gpsimd
            def _(g):
                g.wait_ge(s_st.h, s_st.n)
                g.collective_compute(
                    "AllGather", ALU.bypass,
                    replica_groups=[[0, 1, 2, 3], [4, 5, 6, 7]],
                    ins=[self.cc["rel_a_in"].ap()],
                    outs=[self.cc["rel_a_out"].ap()],
                ).then_inc(s_cc.h, 1)
            s_cc.n += 1

        with nc.Block() as blk:
            @blk.vector
            def _(v):
                v.memset(t["Sin"][:], 0.0)
                v.memset(t["Sbf"][:], 0.0)
        for j in range(3):
            with nc.Block() as blk:
                ld_b, mm_b, ve_b = s_ld.n, s_mm.n, s_ve.n

                @blk.sync
                def _(s):
                    s.wait_ge(s_cc.h, s_cc.n)
                    s.dma_start(out=t["relst"][:],
                                in_=self.cc["rel_a_out"].ap()[j * 128:(j + 1) * 128, :]).then_inc(s_ld.h, 16)
                s_ld.n += 16

                @blk.tensor
                def _(te):
                    te.wait_ge(s_ld.h, ld_b + 16)
                    for h in range(HA):
                        if h >= 2:
                            te.wait_ge(s_ve.h, ve_b + h - 1)
                        te.matmul(t["ps_c" if h % 2 == 0 else "ps_row"][:, 0:128],
                                  t["relst"][:, h * 128:(h + 1) * 128],
                                  t["Sin"][:, h * 128:(h + 1) * 128],
                                  start=True, stop=True).then_inc(s_mm.h, 1)
                s_mm.n += HA

                @blk.vector
                def _(v):
                    for h in range(HA):
                        v.wait_ge(s_mm.h, mm_b + h + 1)
                        ps = t["ps_c" if h % 2 == 0 else "ps_row"][:, 0:128]
                        dcol = t["earg"][:, 0:128]
                        v.tensor_add(dcol, ps, t["relst"][:, 1024 + h * 128:1024 + (h + 1) * 128])
                        v.engine_nop()
                        v.engine_nop()
                        v.tensor_sub(dcol, dcol, t["Sin"][:, h * 128:(h + 1) * 128])
                        v.engine_nop()
                        v.engine_nop()
                        v.scalar_tensor_tensor(t["Sin"][:, h * 128:(h + 1) * 128], dcol,
                                               t["rmask"][:, j:j + 1],
                                               t["Sin"][:, h * 128:(h + 1) * 128],
                                               ALU.mult, ALU.add)
                        v.engine_nop()
                        v.engine_nop().then_inc(s_ve.h, 1)
                s_ve.n += HA
        with nc.Block() as blk:
            @blk.vector
            def _(v):
                v.tensor_copy(t["Sbf"][:], t["Sin"][:])

    def a_pass2(self):
        nc, t = self.nc, self.t
        s_ld, s_ve, s_act, s_mm, s_g = t["s_ld"], t["s_ve"], t["s_act"], t["s_mm"], t["s_g"]
        NQ = NAC * HA
        ev = {}
        cmm, cve, cact, cld, cg = s_mm.n, s_ve.n, s_act.n, s_ld.n, s_g.n
        for c in range(NAC):
            cld += 16; ev[(c, "qfl")] = cld
            cmm += 1; ev[(c, "qbc")] = cmm
            cld += 16; ev[(c, "gg")] = cld
            for hp in range(HA // 2):
                qq = (c * HA + 2 * hp, c * HA + 2 * hp + 1)
                for q in qq:
                    cld += 16; ev[(q, "vwin")] = cld
                for name in ("w2s", "kq", "o12", "otr", "ktr", "smm"):
                    for q in qq:
                        cmm += 1; ev[(q, name)] = cmm
                for name in ("vp", "qarg", "aqbf", "ocmb", "ossum", "obf",
                             "oev", "ktil", "supd"):
                    for q in qq:
                        cve += 1; ev[(q, name)] = cve
                for name in ("qexp", "osq", "oexp", "gexp"):
                    for q in qq:
                        cact += 1; ev[(q, name)] = cact
                for q in qq:
                    cg += 1; ev[(q, "qargdone")] = cg
        self._p2ev = ev

        def tk(a, name):
            return ev[(a, name)]

        def bank(p):
            return t["ps_a" if p == 0 else "ps_b"]

        def obank(p):
            return t["ps_c" if p == 0 else "ps_row"]

        def tbank(p):
            return t["ps_t0" if p == 0 else "ps_t1"]

        def tokcol(qty, c, h, nrow=128):
            return t["toks"][0:nrow, qty * 128 + c * 8 + h: qty * 128 + c * 8 + h + 1]

        def lamcol(c, h):
            return t["lamCbc"][:, h * 16 + c: h * 16 + c + 1]

        def earg(p, i):
            return t["earg"][:, (p * 2 + i) * 128:(p * 2 + i) * 128 + 128]

        def abf(p, i):
            return t["abf"][:, (p * 2 + i) * 128:(p * 2 + i) * 128 + 128]

        def ktk(p, i):
            return t["ktk"][:, (p * 2 + i) * 128:(p * 2 + i) * 128 + 128]

        with nc.Block() as blk:
            @blk.sync
            def _(s):
                for c in range(NAC):
                    if c >= 2:
                        s.wait_ge(s_mm.h, tk(c - 1, "qbc"))
                    s.dma_start(out=t["rowAqfl"][0:1, (c % 2) * 512:(c % 2) * 512 + 512],
                                in_=t["rowAq"][0:8, c * 64:(c + 1) * 64]).then_inc(s_ld.h, 16)
                    if c >= 2:
                        s.wait_ge(s_ve.h, tk((c - 1) * HA - 1, "obf"))
                    s.dma_start(out=t["ggb"][0:64, (c % 2) * 1024:(c % 2) * 1024 + 1024],
                                in_=self.asc["gg"].ap()[c // 2][(c % 2) * 64:(c % 2) * 64 + 64, :]
                                ).then_inc(s_ld.h, 16)
                    for h in range(HA):
                        q = c * HA + h
                        if q >= 2:
                            s.wait_ge(s_ve.h, tk(q - 2, "vp"))
                        s.dma_start(out=t["vwin"][:, (q % 2) * 256:(q % 2) * 256 + 256],
                                    in_=self.asc["vw"].ap()[q]).then_inc(s_ld.h, 16)
            s_ld.n = cld

            @blk.tensor
            def _(te):
                for c in range(NAC):
                    te.wait_ge(s_ld.h, tk(c, "qfl"))
                    if c >= 1:
                        te.wait_ge(s_g.h, tk(c * HA - 1, "qargdone"))
                    te.matmul(t["ps_p0"][:, 0:512], t["ones_rowf"][:],
                              t["rowAqfl"][0:1, (c % 2) * 512:(c % 2) * 512 + 512],
                              start=True, stop=True).then_inc(s_mm.h, 1)
                    for hp in range(HA // 2):
                        qq = (c * HA + 2 * hp, c * HA + 2 * hp + 1)

                        def ksl_(q):
                            h = q - c * HA
                            return t["ak"][:, h * 2048 + c * 128:h * 2048 + (c + 1) * 128]

                        def qsl_(q):
                            h = q - c * HA
                            return t["aq"][:, h * 1024 + c * 64:h * 1024 + (c + 1) * 64]

                        for q in qq:
                            p = q % 2
                            h = q - c * HA
                            te.wait_ge(s_ld.h, tk(q, "vwin"))
                            if q >= 2:
                                te.wait_ge(s_ve.h, tk(q - 2, "supd"))
                            te.matmul(bank(p)[:, 0:128],
                                      t["vwin"][:, (q % 2) * 256 + 128:(q % 2) * 256 + 256],
                                      t["Sbf"][:, h * 128:(h + 1) * 128],
                                      start=True, stop=True).then_inc(s_mm.h, 1)
                        for q in qq:
                            p = q % 2
                            te.wait_ge(s_ve.h, tk(q, "vp"))
                            if q >= 2:
                                te.wait_ge(s_ve.h, tk(q - 2, "aqbf"))
                            te.matmul(bank(p)[:, 128:192], ksl_(q), qsl_(q),
                                      start=True, stop=True).then_inc(s_mm.h, 1)
                        for q in qq:
                            p = q % 2
                            h = q - c * HA
                            te.wait_ge(s_ve.h, tk(q, "aqbf"))
                            if q >= 2:
                                te.wait_ge(s_ve.h, tk(q - 2, "ocmb"))
                            te.matmul(obank(p)[0:64, 0:128], abf(p, 0)[:, 0:64],
                                      t["vp2"][:, p * 128:p * 128 + 128], start=True, stop=True)
                            te.matmul(obank(p)[0:64, 128:256], qsl_(q), t["Sbf"][:, h * 128:(h + 1) * 128],
                                      start=True, stop=True).then_inc(s_mm.h, 1)
                        for q in qq:
                            p = q % 2
                            te.wait_ge(s_ve.h, tk(q, "obf"))
                            if q >= 2:
                                te.wait_ge(s_ve.h, tk(q - 2, "oev"))
                            te.transpose(tbank(p)[0:128, 0:64], t["obf"][0:64, p * 128:p * 128 + 128],
                                         t["ident"][0:64, 0:64]).then_inc(s_mm.h, 1)
                        for q in qq:
                            p = q % 2
                            if q >= 2:
                                te.wait_ge(s_ve.h, tk(q - 2, "ktil"))
                            te.wait_ge(s_ve.h, tk(q, "oev"))
                            te.transpose(tbank(p)[0:128, 64:192], ksl_(q), t["ident"][:]).then_inc(s_mm.h, 1)
                        for q in qq:
                            p = q % 2
                            te.wait_ge(s_ve.h, tk(q, "ktil"))
                            te.matmul(bank(p)[:, 256:384], ktk(p, 0), t["vp2"][:, p * 128:p * 128 + 128],
                                      start=True, stop=True).then_inc(s_mm.h, 1)
            s_mm.n = cmm

            @blk.vector
            def _(v):
                for c in range(NAC):
                    for hp in range(HA // 2):
                        qq = (c * HA + 2 * hp, c * HA + 2 * hp + 1)
                        for q in qq:
                            p = q % 2
                            v.wait_ge(s_mm.h, tk(q, "w2s"))
                            v.tensor_sub(t["vp2"][:, p * 128:p * 128 + 128],
                                         t["vwin"][:, (q % 2) * 256:(q % 2) * 256 + 128],
                                         bank(p)[:, 0:128])
                            v.engine_nop()
                            v.engine_nop().then_inc(s_ve.h, 1)
                        for q in qq:
                            p = q % 2
                            h = q - c * HA
                            v.wait_ge(s_mm.h, tk(c, "qbc"))
                            v.tensor_add(earg(p, 0)[:, 0:64], t["ps_p0"][:, h * 64:(h + 1) * 64], t["maskQ"][:, 0:64])
                            v.engine_nop()
                            v.engine_nop().then_inc(s_ve.h, 1)
                        for q in qq:
                            p = q % 2
                            v.wait_ge(s_act.h, tk(q, "qexp"))
                            v.wait_ge(s_mm.h, tk(q, "kq"))
                            v.tensor_mul(abf(p, 0)[:, 0:64], bank(p)[:, 128:192], earg(p, 1)[:, 0:64])
                            v.engine_nop()
                            v.engine_nop().then_inc(s_ve.h, 1)
                        for q in qq:
                            p = q % 2
                            h = q - c * HA
                            v.wait_ge(s_mm.h, tk(q, "o12"))
                            v.tensor_copy(t["otf"][0:64, p * 128:p * 128 + 128], obank(p)[0:64, 0:128])
                            v.engine_nop()
                            v.engine_nop()
                            v.scalar_tensor_tensor(t["otf"][0:64, p * 128:p * 128 + 128],
                                                   obank(p)[0:64, 128:256], tokcol(5, c, h, 64),
                                                   t["otf"][0:64, p * 128:p * 128 + 128], ALU.mult, ALU.add)
                            v.engine_nop()
                            v.engine_nop().then_inc(s_ve.h, 1)
                        for q in qq:
                            p = q % 2
                            v.wait_ge(s_act.h, tk(q, "osq"))
                            v.reduce_sum(t["rmsc"][0:64, p * 4:p * 4 + 1], t["gtf"][0:64, p * 128:p * 128 + 128],
                                         axis=mybir.AxisListType.X)
                            v.engine_nop()
                            v.engine_nop().then_inc(s_ve.h, 1)
                        for q in qq:
                            p = q % 2
                            h = q - c * HA
                            v.wait_ge(s_act.h, tk(q, "gexp"))
                            ggsl = t["ggb"][0:64, (c % 2) * 1024 + h * 128:(c % 2) * 1024 + (h + 1) * 128]
                            gsl = t["gtf"][0:64, p * 128:p * 128 + 128]
                            osl = t["otf"][0:64, p * 128:p * 128 + 128]
                            v.tensor_scalar_add(gsl, gsl, 1.0)
                            v.engine_nop()
                            v.engine_nop()
                            v.reciprocal(gsl, gsl)
                            v.engine_nop()
                            v.engine_nop()
                            v.tensor_mul(gsl, gsl, ggsl)
                            v.engine_nop()
                            v.engine_nop()
                            v.tensor_mul(gsl, gsl, t["Wonw_bc"][0:64, :])
                            v.tensor_scalar(osl, osl, t["rmsc"][0:64, p * 4 + 1:p * 4 + 2], None, ALU.mult)
                            v.engine_nop()
                            v.engine_nop()
                            v.tensor_mul(t["obf"][0:64, p * 128:p * 128 + 128], osl, gsl)
                            v.engine_nop()
                            v.engine_nop().then_inc(s_ve.h, 1)
                        for q in qq:
                            p = q % 2
                            h = q - c * HA
                            v.wait_ge(s_mm.h, tk(q, "otr"))
                            v.tensor_copy(self.ofm_tile(h)[:, c * 64:(c + 1) * 64], tbank(p)[0:128, 0:64])
                            v.engine_nop()
                            v.engine_nop().then_inc(s_ve.h, 1)
                        for q in qq:
                            p = q % 2
                            h = q - c * HA
                            v.wait_ge(s_mm.h, tk(q, "ktr"))
                            v.tensor_scalar(ktk(p, 0), tbank(p)[0:128, 64:192], tokcol(4, c, h), None, ALU.mult)
                            v.engine_nop()
                            v.engine_nop().then_inc(s_ve.h, 1)
                        for q in qq:
                            p = q % 2
                            h = q - c * HA
                            v.wait_ge(s_mm.h, tk(q, "smm"))
                            v.scalar_tensor_tensor(t["Sin"][:, h * 128:(h + 1) * 128],
                                                   t["Sin"][:, h * 128:(h + 1) * 128], lamcol(c, h),
                                                   bank(p)[:, 256:384], ALU.mult, ALU.add)
                            v.engine_nop()
                            v.engine_nop()
                            v.tensor_copy(t["Sbf"][:, h * 128:(h + 1) * 128], t["Sin"][:, h * 128:(h + 1) * 128])
                            v.engine_nop()
                            v.engine_nop().then_inc(s_ve.h, 1)
            s_ve.n = cve

            @blk.scalar
            def _(sc):
                for c in range(NAC):
                    for hp in range(HA // 2):
                        qq = (c * HA + 2 * hp, c * HA + 2 * hp + 1)
                        for q in qq:
                            p = q % 2
                            h = q - c * HA
                            sc.wait_ge(s_ve.h, tk(q, "qarg"))
                            sc.activation(earg(p, 1)[:, 0:64], earg(p, 0)[:, 0:64], AF.Exp,
                                          bias=tokcol(1, c, h)).then_inc(s_act.h, 1)
                        for q in qq:
                            p = q % 2
                            sc.wait_ge(s_ve.h, tk(q, "ocmb"))
                            sc.activation(t["gtf"][0:64, p * 128:p * 128 + 128],
                                          t["otf"][0:64, p * 128:p * 128 + 128],
                                          AF.Square).then_inc(s_act.h, 1)
                        for q in qq:
                            p = q % 2
                            sc.wait_ge(s_ve.h, tk(q, "ossum"))
                            sc.activation(t["rmsc"][0:64, p * 4 + 2:p * 4 + 3],
                                          t["rmsc"][0:64, p * 4:p * 4 + 1], AF.Ln,
                                          bias=t["cst"][0:64, 0:1], scale=t["cst"][0:64, 1:2])
                            sc.activation(t["rmsc"][0:64, p * 4 + 1:p * 4 + 2],
                                          t["rmsc"][0:64, p * 4 + 2:p * 4 + 3], AF.Exp,
                                          scale=-0.5).then_inc(s_act.h, 1)
                        for q in qq:
                            p = q % 2
                            h = q - c * HA
                            sc.wait_ge(s_ld.h, tk(c, "gg"))
                            sc.activation(t["gtf"][0:64, p * 128:p * 128 + 128],
                                          t["ggb"][0:64, (c % 2) * 1024 + h * 128:(c % 2) * 1024 + (h + 1) * 128],
                                          AF.Exp, scale=-1.0).then_inc(s_act.h, 1)
            s_act.n = cact

            @blk.gpsimd
            def _(g):
                for c in range(NAC):
                    for hp in range(HA // 2):
                        qq = (c * HA + 2 * hp, c * HA + 2 * hp + 1)
                        for q in qq:
                            g.wait_ge(s_ve.h, tk(q, "qarg"))
                            g.engine_nop()
                            g.engine_nop().then_inc(s_g.h, 1)
            s_g.n = cg

    def a_wo(self):
        nc, t = self.nc, self.t
        s_ld = t["s_ld"]
        with nc.Block() as blk:
            @blk.sync
            def _(s):
                for c in range(NCH):
                    s.dma_start(out=t["x_res"][:, c * TPC:(c + 1) * TPC],
                                in_=self.x_park.ap()[c * 128:(c + 1) * 128, :]).then_inc(s_ld.h, 16)
                s.wait_ge(s_ld.h, s_ld.n + 16 * NCH)
            s_ld.n += 16 * NCH

        def rhs(i, tb):
            return self.ofm_tile(i)[:, tb * 512:tb * 512 + 512]

        def evac(v, oc, tb, ps):
            xsl = t["x_res"][:, oc * TPC + tb * 512: oc * TPC + tb * 512 + 512]
            v.tensor_add(xsl, xsl, ps)
            v.tensor_copy(t["x_bf"][:, oc * TPC + tb * 512: oc * TPC + tb * 512 + 512], xsl)

        self.proj_lhsT("Wo_a", 16, 128, list(range(8)), rhs, evac)

    def attn_b(self):
        nc, t, d = self.nc, self.t, self.d
        K = 1024

        def at(name, shape, dtype, offk):
            off = int(offk * K)
            nbytes = int(np.prod(shape[1:])) * (2 if dtype == BF16 else 4)
            assert off + nbytes <= self.asize, (name, off, nbytes, self.asize)
            h = self.nc.alloc_sbuf_tensor_at(name, shape, dtype, offset=self.abase + off)
            self.t[name] = h
            return h

        at("qb", [128, 16 * 1024], BF16, 0)
        at("kb", [128, 16 * 1024], BF16, 32)
        at("evscb", [128, 4 * 512], BF16, 32)
        at("Mb", [128, 16 * 256], F32, 64)
        at("Sbb", [128, 16 * 256], BF16, 80)
        at("vbin", [128, 2 * 4096], BF16, 88)
        at("xpin", [128, 2 * 512], F32, 88)
        at("xout", [128, 2 * 512], F32, 92)
        at("ofmb", [128, 32 * 512], BF16, 104)
        at("convwb", [128, 2 * 4 * 2048], BF16, 104)
        at("relbuf", [128, 4096], F32, 104)
        at("eargb", [128, 2 * 128], F32, 136)
        at("abfb", [128, 2 * 128], BF16, 137)
        at("ktkb", [128, 2 * 128], BF16, 137.5)
        at("otfb", [128, 2 * 256], F32, 138)
        at("obfb", [128, 2 * 256], BF16, 140)
        at("Lbp", [16, 1056], F32, 141)
        at("toksb", [128, 4 * 128], F32, 145.25)
        at("lamCbb", [128, 128], F32, 147.25)
        at("rows2b", [16, 32], F32, 147.75)
        at("bfl2", [1, 1024], F32, 148)
        at("lbfl", [1, 128], F32, 152)
        at("pscrb", [128, 2 * 1056], BF16, 97)
        at("gateb", [16, 1024], F32, 64)
        at("halob", [128, 128], F32, 101.25)
        at("tailb", [128, 96], F32, 36)

        self.bsc = {
            "vb": nc.dram_tensor("vb_st", [NBC, 128, 4096], BF16),
            "qpb": nc.dram_tensor("qpb_st", [16, 128, 1024], BF16),
            "kpb": nc.dram_tensor("kpb_st", [16, 128, 1024], BF16),
            "xp2": nc.dram_tensor("xp2_st", [D, TPC], F32),
        }

        self.b_park_and_proj()
        self.b_halo()
        self.b_conv()
        self.tap("qbt", lambda: t["qb"][:], [128, 16 * 1024], BF16)
        self.tap("kbt", lambda: t["kb"][:], [128, 16 * 1024], BF16)
        self.b_pads()
        self.tap("Lbp", lambda: t["Lbp"][0:16, :], [16, 1056], F32)
        self.tap("toksb", lambda: t["toksb"][:], [128, 512], F32)
        self.tap("lamCbb", lambda: t["lamCbb"][:], [128, 128], F32)
        self.tap("rows2b", lambda: t["rows2b"][0:16, :], [16, 32], F32)
        self.b_pass1()
        self.tap("Mb", lambda: t["Mb"][:], [128, 4096], F32)
        self.b_relay()
        self.tap("Sinb", lambda: t["Mb"][:], [128, 4096], F32)
        self.b_pass2()
        self.b_restore()

    def b_park_and_proj(self):
        nc, t, d = self.nc, self.t, self.d
        s_pk, s_ev, s_st, s_act, s_mm = t["s_pk"], t["s_ev"], t["s_st"], t["s_act"], t["s_mm"]

        with nc.Block() as blk:
            @blk.scalar
            def _(se):
                for c in range(NCH):
                    se.dma_start(out=self.x_park.ap()[c * 128:(c + 1) * 128, :],
                                 in_=t["x_res"][:, c * TPC:(c + 1) * TPC]).then_inc(s_pk.h, 16)
                se.wait_ge(s_pk.h, s_pk.n + 16 * NCH)
            s_pk.n += 16 * NCH

        self.proj_rhs("Wv_b", 8, "vb_b", slabname="qb", evname="evscb")

        def rhs(i, tb):
            return t["xn"][:, i * TPC + tb * 512: i * TPC + tb * 512 + 512]

        def mk_evac(base_slot):
            def evac(v, oc, tb, ps):
                gi = oc * 2 + tb
                v.tensor_copy(t["evscb"][:, (gi % 4) * 512:(gi % 4) * 512 + 512], ps)
                if tb == 1:
                    v.tensor_copy(t["tailb"][:, (base_slot + oc) * 3:(base_slot + oc) * 3 + 3],
                                  ps[:, 509:512])
            return evac

        def mk_store(stname):
            def store(se, oc, tb, ev_t):
                gi = oc * 2 + tb
                se.wait_ge(s_ev.h, ev_t)
                se.dma_start(out=self.bsc[stname].ap()[oc][:, tb * 512:tb * 512 + 512],
                             in_=t["evscb"][:, (gi % 4) * 512:(gi % 4) * 512 + 512]).then_inc(s_st.h, 16)
            return store

        self.proj_lhsT("Wq_b", 16, 128, list(range(NCH)), rhs, mk_evac(0), store_fn=mk_store("qpb"))
        self.proj_lhsT("Wk_b", 16, 128, list(range(NCH)), rhs, mk_evac(16), store_fn=mk_store("kpb"))

        def evac_g(v, oc, tb, ps):
            pass

        def scalar_g(sc, oc, tb, mm_t):
            sc.wait_ge(t["s_mm"].h, mm_t)
            ps = t["ps_p0" if tb % 2 == 0 else "ps_p1"]
            sl = slice(tb * 512, tb * 512 + 512)
            sc.activation(t["gateb"][0:16, sl], ps[0:16, 0:512], AF.Sigmoid,
                          bias=t["gbcol"][:])
            sc.activation(t["Lbp"][0:16, 1 + tb * 512:1 + tb * 512 + 512], ps[32:48, 0:512],
                          AF.Relu, bias=t["bbncol"][:], scale=-1.0)
            sc.activation(t["lnscr"][0:16, sl], ps[32:48, 0:512], AF.Abs, bias=t["bbcol"][:])
            sc.activation(t["lnscr"][0:16, sl], t["lnscr"][0:16, sl], AF.Exp, scale=-1.0)
            sc.activation(t["lnscr"][0:16, sl], t["lnscr"][0:16, sl], AF.Ln,
                          bias=t["cst"][0:16, 2:3]).then_inc(s_act.h, 1)

        self.t["gbcol"] = self.nc.alloc_sbuf_tensor_at("gbcol", [16, 1], F32, offset=self.abase + int(147.875 * 1024))
        self.t["bbcol"] = self.nc.alloc_sbuf_tensor_at("bbcol", [16, 1], F32, offset=self.abase + int(147.90625 * 1024))
        self.t["bbncol"] = self.nc.alloc_sbuf_tensor_at("bbncol", [16, 1], F32, offset=self.abase + int(147.9375 * 1024))
        self.t["lnscr"] = self.nc.alloc_sbuf_tensor_at("lnscr", [16, 1024], BF16 if False else F32,
                                                       offset=self.abase + int(88 * 1024))
        with nc.Block() as blk:
            s_ld = t["s_ld"]

            @blk.sync
            def _(s):
                s.dma_start(out=t["gbcol"][:], in_=d["bg_b"].ap().rearrange("o h -> h o")).then_inc(s_ld.h, 16)
                s.dma_start(out=t["bbcol"][:], in_=d["bb_b"].ap().rearrange("o h -> h o")).then_inc(s_ld.h, 16)
                s.wait_ge(s_ld.h, s_ld.n + 32)
            s_ld.n += 32

            @blk.vector
            def _(v):
                v.wait_ge(s_ld.h, s_ld.n)
                v.tensor_scalar_mul(t["bbncol"][:], t["bbcol"][:], -1.0)

        self.proj_lhsT("Wgb_w", 1, 48, list(range(NCH)), rhs, evac_g, scalar_fn=scalar_g)
        s_act.n += 2

        with nc.Block() as blk:
            s_ve = t["s_ve"]
            n_act = s_act.n

            @blk.vector
            def _(v):
                v.wait_ge(s_act.h, n_act)
                v.tensor_add(t["Lbp"][0:16, 1:1025], t["Lbp"][0:16, 1:1025], t["lnscr"][0:16, :])
                v.engine_nop()
                v.engine_nop()
                v.tensor_scalar_mul(t["Lbp"][0:16, 1:1025], t["Lbp"][0:16, 1:1025], -1.0)
                v.memset(t["Lbp"][0:16, 0:1], 0.0)
                v.memset(t["Lbp"][0:16, 1025:1056], 0.0)
                v.engine_nop()
                v.engine_nop()
                v.tensor_tensor_scan(t["Lbp"][0:16, 1:1025], t["Lbp"][0:16, 1:1025],
                                     t["Lbp"][0:16, 1:1025], 0.0, ALU.add, ALU.bypass)

    def b_halo(self):
        nc, t = self.nc, self.t
        s_st, s_cc, s_ld = t["s_st"], t["s_cc"], t["s_ld"]
        with nc.Block() as blk:
            @blk.sync
            def _(s):
                s.dma_start(out=self.cc["halo_b_in"].ap(), in_=t["tailb"][:, 0:96]).then_inc(s_st.h, 16)
            s_st.n += 16

            @blk.gpsimd
            def _(g):
                g.wait_ge(s_st.h, s_st.n)
                g.collective_compute(
                    "AllGather", ALU.bypass,
                    replica_groups=[[0, 1, 2, 3], [4, 5, 6, 7]],
                    ins=[self.cc["halo_b_in"].ap()],
                    outs=[self.cc["halo_b_out"].ap()],
                ).then_inc(s_cc.h, 1)
            s_cc.n += 1

        with nc.Block() as blk:
            @blk.sync
            def _(s):
                s.wait_ge(s_cc.h, s_cc.n)
                for j in range(3):
                    s.dma_start(out=t["otfb"][:, j * 128:j * 128 + 96],
                                in_=self.cc["halo_b_out"].ap()[j * 128:(j + 1) * 128, :]).then_inc(s_ld.h, 16)
            s_ld.n += 48

            @blk.vector
            def _(v):
                v.memset(t["halob"][:], 0.0)
                v.wait_ge(s_ld.h, s_ld.n)
                for j in range(3):
                    v.engine_nop()
                    v.engine_nop()
                    v.scalar_tensor_tensor(t["halob"][:, 0:96],
                                           t["otfb"][:, j * 128:j * 128 + 96],
                                           t["rmask"][:, 3 + j:4 + j], t["halob"][:, 0:96],
                                           ALU.mult, ALU.add)

    def b_conv(self):
        nc, t, d = self.nc, self.t, self.d
        s_ld, s_ve, s_act, s_mm, s_w = t["s_ld"], t["s_ve"], t["s_act"], t["s_mm"], t["s_w"]
        w_base = s_w.n
        with nc.Block() as blk:
            @blk.gpsimd
            def _(g):
                g.dma_start(out=t["convwb"][:, 0:8192],
                            in_=d["convW_q_b"].ap().rearrange("t i o -> i t o")).then_inc(s_w.h, 16)
                g.dma_start(out=t["convwb"][:, 8192:16384],
                            in_=d["convW_k_b"].ap().rearrange("t i o -> i t o")).then_inc(s_w.h, 16)
            s_w.n += 32

        items = [("q", g) for g in range(16)] + [("k", g) for g in range(16)]
        ld_b, ve_b, act_b, mm_b = s_ld.n, s_ve.n, s_act.n, s_mm.n
        with nc.Block() as blk:
            @blk.sync
            def _(s):
                for i, (qk, g_) in enumerate(items):
                    if i >= 2:
                        s.wait_ge(s_mm.h, mm_b + (i - 1) * 2)
                    st = self.bsc["qpb" if qk == "q" else "kpb"]
                    s.dma_start(out=t["pscrb"][:, (i % 2) * 1056 + 3:(i % 2) * 1056 + 3 + 1024],
                                in_=st.ap()[g_]).then_inc(s_ld.h, 16)
            s_ld.n += 16 * len(items)

            @blk.vector
            def _(v):
                for i, (qk, g_) in enumerate(items):
                    v.wait_ge(s_ld.h, ld_b + 16 * (i + 1))
                    if i >= 2:
                        v.wait_ge(s_mm.h, mm_b + (i - 1) * 2)
                    slot = (i % 2) * 1056
                    hs = (g_ if qk == "q" else 16 + g_) * 3
                    v.tensor_copy(t["pscrb"][:, slot:slot + 3], t["halob"][:, hs:hs + 3])
                    v.engine_nop()
                    v.engine_nop().then_inc(s_ve.h, 1)
            s_ve.n += len(items)

            @blk.tensor
            def _(te):
                te.wait_ge(s_w.h, w_base + 32)
                for i, (qk, g_) in enumerate(items):
                    te.wait_ge(s_ve.h, ve_b + i + 1)
                    if i >= 1:
                        te.wait_ge(s_act.h, act_b + (i - 1) * 2 + 2)
                    slot = (i % 2) * 1056
                    woff = (0 if qk == "q" else 8192) + g_ * 128
                    for tb in range(2):
                        ps = t["ps_p0" if tb == 0 else "ps_p1"][:, 0:512]
                        for tau in range(4):
                            ins = te.matmul(ps,
                                            t["convwb"][:, woff + tau * 2048:woff + tau * 2048 + 128],
                                            t["pscrb"][:, slot + tb * 512 + tau:slot + tb * 512 + tau + 512],
                                            start=(tau == 0), stop=(tau == 3))
                        ins.then_inc(s_mm.h, 1)
            s_mm.n += 2 * len(items)

            @blk.scalar
            def _(sc):
                for i, (qk, g_) in enumerate(items):
                    dst = t["qb" if qk == "q" else "kb"]
                    bcol = t["bqk_conv_b"][:, (0 if qk == "q" else 16) + g_:(0 if qk == "q" else 16) + g_ + 1]
                    for tb in range(2):
                        sc.wait_ge(s_mm.h, mm_b + i * 2 + tb + 1)
                        ps = t["ps_p0" if tb == 0 else "ps_p1"][:, 0:512]
                        sc.activation(dst[:, g_ * 1024 + tb * 512:g_ * 1024 + tb * 512 + 512],
                                      ps, AF.Silu, bias=bcol).then_inc(s_act.h, 1)
            s_act.n += 2 * len(items)


    def b_pads(self):
        nc, t = self.nc, self.t
        s_ld, s_ve, s_act, s_mm = t["s_ld"], t["s_ve"], t["s_act"], t["s_mm"]
        with nc.Block() as blk:
            @blk.vector
            def _(v):
                v.tensor_scalar_mul(t["rows2b"][0:16, 0:8], t["Lbp"][0:16, 0:1024:128], -1.0)
                v.tensor_copy(t["rows2b"][0:16, 8:16], t["Lbp"][0:16, 128:1025:128])
                v.engine_nop()
                v.engine_nop()
                v.engine_nop()
                v.tensor_add(t["rows2b"][0:16, 16:24], t["rows2b"][0:16, 8:16], t["rows2b"][0:16, 0:8])
                v.engine_nop()
                v.engine_nop().then_inc(t["s_ve"].h, 1)
            s_ve.n += 1

            @blk.scalar
            def _(sc):
                sc.wait_ge(s_ve.h, s_ve.n)
                sc.activation(t["rows2b"][0:16, 16:24], t["rows2b"][0:16, 16:24], AF.Exp).then_inc(s_act.h, 1)
            s_act.n += 1

        with nc.Block() as blk:
            @blk.sync
            def _(s):
                s.wait_ge(s_act.h, s_act.n)
                s.dma_start(out=t["lbfl"][0:1, :], in_=t["rows2b"][0:16, 16:24]).then_inc(s_ld.h, 16)
            s_ld.n += 16

            @blk.tensor
            def _(te):
                te.wait_ge(s_ld.h, s_ld.n)
                te.matmul(t["ps_row"][:, 0:128], t["ones_rowf"][:], t["lbfl"][0:1, :],
                          start=True, stop=True).then_inc(s_mm.h, 1)
            s_mm.n += 1

            @blk.vector
            def _(v):
                v.wait_ge(s_mm.h, s_mm.n)
                v.tensor_copy(t["lamCbb"][:], t["ps_row"][:, 0:128])

        with nc.Block() as blk:
            act_b, mm_b, ve_b = s_act.n, s_mm.n, s_ve.n

            @blk.scalar
            def _(sc):
                for c in range(NBC):
                    if c >= 1:
                        sc.wait_ge(s_mm.h, mm_b + 4 * c)
                    sc.activation(t["lnscr"][0:16, 0:128], t["Lbp"][0:16, 1 + c * 128:1 + (c + 1) * 128],
                                  AF.Exp, bias=t["rows2b"][0:16, c:c + 1])
                    sc.activation(t["lnscr"][0:16, 128:256], t["Lbp"][0:16, 1 + c * 128:1 + (c + 1) * 128],
                                  AF.Exp, scale=-1.0, bias=t["rows2b"][0:16, 8 + c:9 + c]).then_inc(s_act.h, 1)
            s_act.n += NBC

            @blk.vector
            def _(v0):
                for c in range(NBC):
                    v0.wait_ge(s_act.h, act_b + c + 1)
                    v0.tensor_scalar_mul(t["lnscr"][0:16, 256:384],
                                         t["Lbp"][0:16, 1 + c * 128:1 + (c + 1) * 128], -1.0)
                    v0.engine_nop()
                    v0.engine_nop().then_inc(s_ve.h, 1)
                    for qi in range(4):
                        v0.wait_ge(s_mm.h, mm_b + 4 * c + qi + 1)
                        bank = ["ps_c", "ps_row", "ps_c", "ps_row"][qi]
                        v0.tensor_copy(t["toksb"][:, qi * 128 + c * 16: qi * 128 + c * 16 + 16],
                                       t[bank][0:128, 0:16])
                        v0.engine_nop()
                        v0.engine_nop().then_inc(s_ve.h, 1)
            s_ve.n += 5 * NBC

            @blk.tensor
            def _(te):
                for c in range(NBC):
                    te.wait_ge(s_ve.h, ve_b + 5 * c + 1)
                    srcs = [
                        (t["gateb"][0:16, c * 128:(c + 1) * 128], "ps_c"),
                        (t["lnscr"][0:16, 0:128], "ps_row"),
                        (t["lnscr"][0:16, 128:256], "ps_c"),
                        (t["lnscr"][0:16, 256:384], "ps_row"),
                    ]
                    te.wait_ge(s_act.h, act_b + c + 1)
                    for qi, (ap, bank) in enumerate(srcs):
                        te.wait_ge(s_ve.h, ve_b + 5 * c + 1 + max(0, qi - 1))
                        te.transpose(t[bank][0:128, 0:16], ap, t["identf"][0:16, 0:16]).then_inc(s_mm.h, 1)
            s_mm.n += 4 * NBC

    def b_pass1(self):
        nc, t = self.nc, self.t
        s_ld, s_ve, s_mm = t["s_ld"], t["s_ve"], t["s_mm"]
        NQ = NBC * HB
        ev = {}
        cmm, cve, cld = s_mm.n, s_ve.n, s_ld.n
        for c in range(NBC):
            cld += 16; ev[(c, "vb")] = cld
            for h in range(HB):
                q = c * HB + h
                cmm += 1; ev[(q, "ktr")] = cmm
                cve += 1; ev[(q, "ktil")] = cve
                cmm += 1; ev[(q, "smm")] = cmm
                cve += 1; ev[(q, "mupd")] = cve
        self._bp1ev = ev

        def tk(a, name):
            return ev[(a, name)]

        def tokcol(qty, c, h):
            return t["toksb"][:, qty * 128 + c * 16 + h: qty * 128 + c * 16 + h + 1]

        with nc.Block() as blk:
            @blk.vector
            def _(v):
                v.memset(t["Mb"][:], 0.0)

        with nc.Block() as blk:
            @blk.sync
            def _(s):
                for c in range(NBC):
                    if c >= 2:
                        s.wait_ge(s_mm.h, tk((c - 1) * HB - 1, "smm"))
                    s.dma_start(out=t["vbin"][:, (c % 2) * 4096:(c % 2) * 4096 + 4096],
                                in_=self.bsc["vb"].ap()[c]).then_inc(s_ld.h, 16)
            s_ld.n = cld

            @blk.tensor
            def _(te):
                for c in range(NBC):
                    te.wait_ge(s_ld.h, tk(c, "vb"))
                    for h in range(HB):
                        q = c * HB + h
                        p = q % 2
                        ksl = t["kb"][:, h * 1024 + c * 128:h * 1024 + (c + 1) * 128]
                        if q >= 1:
                            te.wait_ge(s_ve.h, tk(q - 1, "ktil"))
                        te.transpose(t["ps_t0" if p == 0 else "ps_t1"][:, 0:128], ksl,
                                     t["ident"][:]).then_inc(s_mm.h, 1)
                        te.wait_ge(s_ve.h, tk(q, "ktil"))
                        if q >= 2:
                            te.wait_ge(s_ve.h, tk(q - 2, "mupd"))
                        te.matmul((t["ps_a"] if p == 0 else t["ps_b"])[:, 0:256],
                                  t["ktkb"][:, p * 128:p * 128 + 128],
                                  t["vbin"][:, (c % 2) * 4096 + h * 256:(c % 2) * 4096 + (h + 1) * 256],
                                  start=True, stop=True).then_inc(s_mm.h, 1)
            s_mm.n = cmm

            @blk.vector
            def _(v):
                for c in range(NBC):
                    for h in range(HB):
                        q = c * HB + h
                        p = q % 2
                        v.wait_ge(s_mm.h, tk(q, "ktr"))
                        v.tensor_scalar(t["ktkb"][:, p * 128:p * 128 + 128],
                                        t["ps_t0" if p == 0 else "ps_t1"][:, 0:128],
                                        tokcol(2, c, h), None, ALU.mult)
                        v.engine_nop()
                        v.engine_nop().then_inc(s_ve.h, 1)
                        v.wait_ge(s_mm.h, tk(q, "smm"))
                        v.scalar_tensor_tensor(t["Mb"][:, h * 256:(h + 1) * 256],
                                               t["Mb"][:, h * 256:(h + 1) * 256],
                                               t["lamCbb"][:, h * 8 + c:h * 8 + c + 1],
                                               (t["ps_a"] if p == 0 else t["ps_b"])[:, 0:256],
                                               ALU.mult, ALU.add)
                        v.engine_nop()
                        v.engine_nop().then_inc(s_ve.h, 1)
            s_ve.n = cve

    def b_relay(self):
        nc, t = self.nc, self.t
        s_st, s_cc, s_ld, s_ve, s_mm = t["s_st"], t["s_cc"], t["s_ld"], t["s_ve"], t["s_mm"]
        with nc.Block() as blk:
            @blk.sync
            def _(s):
                s.dma_start(out=t["lbfl"][0:1, 0:16],
                            in_=t["Lbp"][0:16, 1024:1025]).then_inc(s_ld.h, 16)
            s_ld.n += 16

            @blk.scalar
            def _(sc):
                sc.wait_ge(s_ld.h, s_ld.n)
                sc.activation(t["lbfl"][0:1, 16:32], t["lbfl"][0:1, 0:16], AF.Exp).then_inc(t["s_act"].h, 1)
            t["s_act"].n += 1

            @blk.tensor
            def _(te):
                te.wait_ge(t["s_act"].h, t["s_act"].n)
                te.matmul(t["ps_row"][:, 0:16], t["ones_rowf"][:], t["lbfl"][0:1, 16:32],
                          start=True, stop=True).then_inc(s_mm.h, 1)
            s_mm.n += 1

            @blk.vector
            def _(v):
                v.wait_ge(s_mm.h, s_mm.n)
                v.tensor_copy(t["eargb"][:, 0:16], t["ps_row"][:, 0:16])
        with nc.Block() as blk:
            @blk.sync
            def _(s):
                s.dma_start(out=self.cc["rel_b_in"].ap()[:, 0:4096], in_=t["Mb"][:]).then_inc(s_st.h, 16)
                s.dma_start(out=self.cc["rel_b_in"].ap()[:, 4096:4112], in_=t["eargb"][:, 0:16]).then_inc(s_st.h, 16)
            s_st.n += 32

            @blk.gpsimd
            def _(g):
                g.wait_ge(s_st.h, s_st.n)
                g.collective_compute(
                    "AllGather", ALU.bypass,
                    replica_groups=[[0, 1, 2, 3], [4, 5, 6, 7]],
                    ins=[self.cc["rel_b_in"].ap()],
                    outs=[self.cc["rel_b_out"].ap()],
                ).then_inc(s_cc.h, 1)
            s_cc.n += 1

        with nc.Block() as blk:
            @blk.vector
            def _(v):
                v.memset(t["Mb"][:], 0.0)
                v.memset(t["Sbb"][:], 0.0)
        for j in range(3):
            with nc.Block() as blk:
                ld_b = s_ld.n

                @blk.sync
                def _(s):
                    s.wait_ge(s_cc.h, s_cc.n)
                    s.dma_start(out=t["relbuf"][:],
                                in_=self.cc["rel_b_out"].ap()[j * 128:(j + 1) * 128, 0:4096]).then_inc(s_ld.h, 16)
                    s.dma_start(out=t["eargb"][:, 0:16],
                                in_=self.cc["rel_b_out"].ap()[j * 128:(j + 1) * 128, 4096:4112]).then_inc(s_ld.h, 16)
                s_ld.n += 32

                @blk.vector
                def _(v):
                    v.wait_ge(s_ld.h, s_ld.n)
                    for h in range(HB):
                        Ssl = t["Mb"][:, h * 256:(h + 1) * 256]
                        d1 = t["otfb"][:, 0:256]
                        v.scalar_tensor_tensor(d1, Ssl, t["eargb"][:, h:h + 1],
                                               t["relbuf"][:, h * 256:(h + 1) * 256], ALU.mult, ALU.add)
                        v.engine_nop()
                        v.engine_nop()
                        v.tensor_sub(d1, d1, Ssl)
                        v.engine_nop()
                        v.engine_nop()
                        v.scalar_tensor_tensor(Ssl, d1, t["rmask"][:, j:j + 1], Ssl, ALU.mult, ALU.add)
        with nc.Block() as blk:
            @blk.vector
            def _(v):
                v.tensor_copy(t["Sbb"][:], t["Mb"][:])

    def b_pass2(self):
        nc, t = self.nc, self.t
        for blkc in range(2):
            self._b_pass2_block(blkc)
            self.b_wo(blkc)

    def _b_pass2_block(self, blkc):
        nc, t = self.nc, self.t
        s_ld, s_ve, s_act, s_mm, s_g = t["s_ld"], t["s_ve"], t["s_act"], t["s_mm"], t["s_g"]
        cs = list(range(blkc * 4, blkc * 4 + 4))
        ev = {}
        cmm, cve, cact, cld, cg = s_mm.n, s_ve.n, s_act.n, s_ld.n, s_g.n
        for c in cs:
            cld += 16; ev[(c, "vb")] = cld
            cld += 16; ev[(c, "lb0")] = cld
            ev[(c, "lb1")] = None
            for grp in range(4):
                if grp == 2:
                    cld += 16; ev[(c, "lb1")] = cld
                cmm += 1; ev[(c, f"rbc{grp}")] = cmm
                for k2 in range(2):
                    qq = (c * HB + grp * 4 + 2 * k2, c * HB + grp * 4 + 2 * k2 + 1)
                    for name in ("amm", "o12", "otr", "ktr", "smm"):
                        for q in qq:
                            cmm += 1; ev[(q, name)] = cmm
                    for name in ("aarg", "abf", "ocmb", "oev", "ktil", "supd"):
                        for q in qq:
                            cve += 1; ev[(q, name)] = cve
                    for q in qq:
                        cact += 1; ev[(q, "aexp")] = cact
                    for q in qq:
                        cg += 1; ev[(q, "aargdone")] = cg

        def tk(a, name):
            return ev[(a, name)]

        def tokcol(qty, c, h):
            return t["toksb"][:, qty * 128 + c * 16 + h: qty * 128 + c * 16 + h + 1]

        def bank(p):
            return t["ps_a" if p == 0 else "ps_b"]

        def obank(p):
            return t["ps_c" if p == 0 else "ps_row"]

        def tb_(p):
            return t["ps_t0" if p == 0 else "ps_t1"]

        def heads(c):
            for grp in range(4):
                for h in range(grp * 4, grp * 4 + 4):
                    yield grp, h

        with nc.Block() as blk:
            @blk.sync
            def _(s):
                for c in cs:
                    if c - blkc * 4 >= 2:
                        s.wait_ge(s_ve.h, tk((c - 1) * HB - 1, "supd"))
                    s.dma_start(out=t["vbin"][:, (c % 2) * 4096:(c % 2) * 4096 + 4096],
                                in_=self.bsc["vb"].ap()[c]).then_inc(s_ld.h, 16)
                    if c - blkc * 4 >= 1:
                        s.wait_ge(s_mm.h, tk(c - 1, "rbc3"))
                    s.dma_start(out=t["bfl2"][0:1, 0:1024],
                                in_=t["Lbp"][0:8, 1 + c * 128:1 + (c + 1) * 128]).then_inc(s_ld.h, 16)
                    s.wait_ge(s_mm.h, tk(c, "rbc1"))
                    s.dma_start(out=t["bfl2"][0:1, 0:1024],
                                in_=t["Lbp"][8:16, 1 + c * 128:1 + (c + 1) * 128]).then_inc(s_ld.h, 16)
            s_ld.n = cld

            @blk.tensor
            def _(te):
                for c in cs:
                    te.wait_ge(s_ld.h, tk(c, "vb"))

                    def ksl_(q):
                        h = q - c * HB
                        return t["kb"][:, h * 1024 + c * 128:h * 1024 + (c + 1) * 128]

                    def qsl_(q):
                        h = q - c * HB
                        return t["qb"][:, h * 1024 + c * 128:h * 1024 + (c + 1) * 128]

                    def vsl_(q):
                        h = q - c * HB
                        return t["vbin"][:, (c % 2) * 4096 + h * 256:(c % 2) * 4096 + (h + 1) * 256]

                    for grp in range(4):
                        te.wait_ge(s_ld.h, tk(c, "lb0" if grp < 2 else "lb1"))
                        if (c, grp) != (cs[0], 0):
                            prevq = c * HB + grp * 4 - 1 if grp > 0 else c * HB - 1
                            if (prevq, "aargdone") in ev:
                                te.wait_ge(s_g.h, tk(prevq, "aargdone"))
                        te.matmul(t["ps_p0"][:, 0:512], t["ones_rowf"][:],
                                  t["bfl2"][0:1, (grp % 2) * 512:(grp % 2) * 512 + 512],
                                  start=True, stop=True).then_inc(s_mm.h, 1)
                        for k2 in range(2):
                            qq = (c * HB + grp * 4 + 2 * k2, c * HB + grp * 4 + 2 * k2 + 1)
                            for q in qq:
                                p = q % 2
                                if q - cs[0] * HB >= 2:
                                    te.wait_ge(s_ve.h, tk(q - 2, "supd"))
                                te.matmul(bank(p)[:, 0:128], ksl_(q), qsl_(q),
                                          start=True, stop=True).then_inc(s_mm.h, 1)
                            for q in qq:
                                p = q % 2
                                h = q - c * HB
                                te.wait_ge(s_ve.h, tk(q, "abf"))
                                if q - cs[0] * HB >= 2:
                                    te.wait_ge(s_ve.h, tk(q - 2, "ocmb"))
                                te.matmul(obank(p)[:, 0:256], t["abfb"][:, p * 128:p * 128 + 128],
                                          vsl_(q), start=True, stop=True)
                                te.matmul(obank(p)[:, 256:512], qsl_(q), t["Sbb"][:, h * 256:(h + 1) * 256],
                                          start=True, stop=True).then_inc(s_mm.h, 1)
                            for q in qq:
                                p = q % 2
                                te.wait_ge(s_ve.h, tk(q, "ocmb"))
                                if q - cs[0] * HB >= 2:
                                    te.wait_ge(s_ve.h, tk(q - 2, "oev"))
                                    te.wait_ge(s_ve.h, tk(q - 2, "ktil"))
                                for half in range(2):
                                    ins = te.transpose(tb_(p)[:, half * 128:half * 128 + 128],
                                                       t["obfb"][:, p * 256 + half * 128:p * 256 + (half + 1) * 128],
                                                       t["ident"][:])
                                ins.then_inc(s_mm.h, 1)
                            for q in qq:
                                p = q % 2
                                te.wait_ge(s_ve.h, tk(q, "oev"))
                                te.transpose(tb_(p)[:, 256:384], ksl_(q), t["ident"][:]).then_inc(s_mm.h, 1)
                            for q in qq:
                                p = q % 2
                                te.wait_ge(s_ve.h, tk(q, "ktil"))
                                te.matmul(bank(p)[:, 128:384], t["ktkb"][:, p * 128:p * 128 + 128], vsl_(q),
                                          start=True, stop=True).then_inc(s_mm.h, 1)
            s_mm.n = cmm

            @blk.vector
            def _(v):
                for c in cs:
                    for grp in range(4):
                        for k2 in range(2):
                            qq = (c * HB + grp * 4 + 2 * k2, c * HB + grp * 4 + 2 * k2 + 1)
                            for q in qq:
                                p = q % 2
                                h = q - c * HB
                                v.wait_ge(s_mm.h, tk(c, f"rbc{grp}"))
                                v.tensor_add(t["eargb"][:, p * 128:p * 128 + 128],
                                             t["ps_p0"][:, (h % 4) * 128:(h % 4) * 128 + 128], t["maskI"][:])
                                v.engine_nop()
                                v.engine_nop().then_inc(s_ve.h, 1)
                            for q in qq:
                                p = q % 2
                                v.wait_ge(s_act.h, tk(q, "aexp"))
                                v.wait_ge(s_mm.h, tk(q, "amm"))
                                v.tensor_mul(t["abfb"][:, p * 128:p * 128 + 128], bank(p)[:, 0:128],
                                             t["eargb"][:, p * 128:p * 128 + 128])
                                v.engine_nop()
                                v.engine_nop().then_inc(s_ve.h, 1)
                            for q in qq:
                                p = q % 2
                                h = q - c * HB
                                v.wait_ge(s_mm.h, tk(q, "o12"))
                                osl = t["otfb"][:, p * 256:p * 256 + 256]
                                v.tensor_copy(osl, obank(p)[:, 0:256])
                                v.engine_nop()
                                v.engine_nop()
                                v.scalar_tensor_tensor(osl, obank(p)[:, 256:512], tokcol(1, c, h),
                                                       osl, ALU.mult, ALU.add)
                                v.engine_nop()
                                v.engine_nop()
                                v.tensor_scalar(t["obfb"][:, p * 256:p * 256 + 256], osl,
                                                tokcol(0, c, h), None, ALU.mult)
                                v.engine_nop()
                                v.engine_nop().then_inc(s_ve.h, 1)
                            for q in qq:
                                p = q % 2
                                h = q - c * HB
                                v.wait_ge(s_mm.h, tk(q, "otr"))
                                base = (2 * h) * 512 + (c % 4) * 128
                                v.tensor_copy(t["ofmb"][:, base:base + 128], tb_(p)[:, 0:128])
                                v.tensor_copy(t["ofmb"][:, base + 512:base + 512 + 128], tb_(p)[:, 128:256])
                                v.engine_nop()
                                v.engine_nop().then_inc(s_ve.h, 1)
                            for q in qq:
                                p = q % 2
                                h = q - c * HB
                                v.wait_ge(s_mm.h, tk(q, "ktr"))
                                v.tensor_scalar(t["ktkb"][:, p * 128:p * 128 + 128], tb_(p)[:, 256:384],
                                                tokcol(2, c, h), None, ALU.mult)
                                v.engine_nop()
                                v.engine_nop().then_inc(s_ve.h, 1)
                            for q in qq:
                                p = q % 2
                                h = q - c * HB
                                v.wait_ge(s_mm.h, tk(q, "smm"))
                                Ssl = t["Mb"][:, h * 256:(h + 1) * 256]
                                v.scalar_tensor_tensor(Ssl, Ssl, t["lamCbb"][:, h * 8 + c:h * 8 + c + 1],
                                                       bank(p)[:, 128:384], ALU.mult, ALU.add)
                                v.engine_nop()
                                v.engine_nop()
                                v.tensor_copy(t["Sbb"][:, h * 256:(h + 1) * 256], Ssl)
                                v.engine_nop()
                                v.engine_nop().then_inc(s_ve.h, 1)
            s_ve.n = cve

            @blk.scalar
            def _(sc):
                for c in cs:
                    for grp in range(4):
                        for k2 in range(2):
                            qq = (c * HB + grp * 4 + 2 * k2, c * HB + grp * 4 + 2 * k2 + 1)
                            for q in qq:
                                p = q % 2
                                h = q - c * HB
                                sc.wait_ge(s_ve.h, tk(q, "aarg"))
                                sc.activation(t["eargb"][:, p * 128:p * 128 + 128],
                                              t["eargb"][:, p * 128:p * 128 + 128], AF.Exp,
                                              bias=tokcol(3, c, h)).then_inc(s_act.h, 1)
            s_act.n = cact

            @blk.gpsimd
            def _(g):
                for c in cs:
                    for grp in range(4):
                        for k2 in range(2):
                            qq = (c * HB + grp * 4 + 2 * k2, c * HB + grp * 4 + 2 * k2 + 1)
                            for q in qq:
                                g.wait_ge(s_ve.h, tk(q, "aarg"))
                                g.engine_nop()
                                g.engine_nop().then_inc(s_g.h, 1)
            s_g.n = cg

    def b_wo(self, blkc):
        nc, t = self.nc, self.t
        s_pk, s_ev, s_st = t["s_pk"], t["s_ev"], t["s_st"]

        def rhs(i, tb):
            return t["ofmb"][:, i * 512:(i + 1) * 512]

        ev_base = s_ev.n
        pk_base = s_pk.n
        st_base = s_st.n

        def scalar_xp(sc, oc, tb, mm_t):
            if oc >= 2:
                sc.wait_ge(s_ev.h, ev_base + oc - 1)
            sc.dma_start(out=t["xpin"][:, (oc % 2) * 512:(oc % 2) * 512 + 512],
                         in_=self.x_park.ap()[oc * 128:(oc + 1) * 128,
                                              blkc * 512:blkc * 512 + 512]).then_inc(s_pk.h, 16)

        def evac2(v, oc, tb, ps):
            v.wait_ge(s_pk.h, pk_base + 16 * (oc + 1))
            if oc >= 2:
                v.wait_ge(s_st.h, st_base + 16 * (oc - 1))
            v.tensor_add(t["xout"][:, (oc % 2) * 512:(oc % 2) * 512 + 512],
                         t["xpin"][:, (oc % 2) * 512:(oc % 2) * 512 + 512], ps)

        def store(se, oc, tb, ev_t):
            se.wait_ge(s_ev.h, ev_t)
            se.dma_start(out=self.bsc["xp2"].ap()[oc * 128:(oc + 1) * 128,
                                                  blkc * 512:blkc * 512 + 512],
                         in_=t["xout"][:, (oc % 2) * 512:(oc % 2) * 512 + 512]).then_inc(s_st.h, 16)

        self.proj_lhsT("Wo_b", 16, 128, list(range(32)), rhs, evac2, nt=1, tw=512,
                       scalar_fn=scalar_xp, store_fn=store)
        s_pk.n += 16 * 16

    def b_restore(self):
        nc, t = self.nc, self.t
        s_ld, s_act = t["s_ld"], t["s_act"]
        with nc.Block() as blk:
            ld_b = s_ld.n

            @blk.sync
            def _(s):
                for c in range(NCH):
                    s.dma_start(out=t["x_res"][:, c * TPC:(c + 1) * TPC],
                                in_=self.bsc["xp2"].ap()[c * 128:(c + 1) * 128, :]).then_inc(s_ld.h, 16)
            s_ld.n += 16 * NCH

            @blk.scalar
            def _(sc):
                for c in range(NCH):
                    sc.wait_ge(s_ld.h, ld_b + 16 * (c + 1))
                    sc.copy(t["x_bf"][:, c * TPC:(c + 1) * TPC], t["x_res"][:, c * TPC:(c + 1) * TPC])

    def finish(self):
        nc, t = self.nc, self.t
        s_st = t["s_st"]
        with nc.Block() as blk:
            @blk.sync
            def _(s):
                for c in range(NCH):
                    s.dma_start(out=self.y.ap()[c * 128:(c + 1) * 128, :],
                                in_=t["x_res"][:, c * TPC:(c + 1) * TPC]).then_inc(s_st.h, 16)
                s.wait_ge(s_st.h, s_st.n + 16 * NCH)
            s_st.n += 16 * NCH


def _lhsT_swizzle(w, ocols=128):
    O, Din = w.shape
    n_oc = max(1, O // ocols)
    wt = np.ascontiguousarray(w.T)
    h = wt.reshape(Din // 128, 128, n_oc, ocols).transpose(2, 1, 0, 3)
    return np.ascontiguousarray(h.reshape(n_oc, 128, (Din // 128) * ocols))


def _colchunks(a):
    C, k = a.shape
    return np.ascontiguousarray(a.reshape(C // 128, 128, k).transpose(1, 0, 2).reshape(128, -1))


def _colvec(a):
    return np.ascontiguousarray(a.reshape(-1, 128).T)


def _rhs_swizzle(w, og=512):
    O, Din = w.shape
    wt = np.ascontiguousarray(w.T)
    h = wt.reshape(Din // 128, 128, O // og, og).transpose(2, 1, 0, 3)
    return np.ascontiguousarray(h.reshape(O // og, 128, (Din // 128) * og))


def host_prep(inputs):
    f = np.ascontiguousarray
    x = inputs["L0_attn_a_x"]
    Wba = np.concatenate([inputs["L0_attn_a_Wb"],
                          np.zeros((16, 2048), np.float32),
                          inputs["L0_attn_a_Wa"]], 0)
    Wgb = np.concatenate([inputs["L0_attn_b_Wg"],
                          np.zeros((16, 2048), np.float32),
                          inputs["L0_attn_b_Wb"]], 0)
    shared = {
        "Wq_a": _lhsT_swizzle(inputs["L0_attn_a_Wq"]),
        "Wk_a": _lhsT_swizzle(inputs["L0_attn_a_Wk"]),
        "Wv_a": _lhsT_swizzle(inputs["L0_attn_a_Wv"]),
        "Wba_w": _lhsT_swizzle(Wba, 40),
        "Wo_a": _lhsT_swizzle(inputs["L0_attn_a_Wo"]),
        "ffn_w": _lhsT_swizzle(inputs["L0_ffn_weight"]),
        "Wq_b": _lhsT_swizzle(inputs["L0_attn_b_Wq"]),
        "Wk_b": _lhsT_swizzle(inputs["L0_attn_b_Wk"]),
        "Wgb_w": _lhsT_swizzle(Wgb, 48),
        "Wo_b": _lhsT_swizzle(inputs["L0_attn_b_Wo"]),
        "Wg_a": _rhs_swizzle(inputs["L0_attn_a_Wg"]),
        "Wv_b": _rhs_swizzle(inputs["L0_attn_b_Wv"]),
        "conv_q_a": _colchunks(inputs["L0_attn_a_conv_q"]),
        "conv_k_a": _colchunks(inputs["L0_attn_a_conv_k"]),
        "conv_v_a": _colchunks(inputs["L0_attn_a_conv_v"]),
        "convW_q_b": f(inputs["L0_attn_b_Wq_conv"].transpose(2, 1, 0)),
        "convW_k_b": f(inputs["L0_attn_b_Wk_conv"].transpose(2, 1, 0)),
        "bq_conv_b": _colvec(inputs["L0_attn_b_bq_conv"]),
        "bk_conv_b": _colvec(inputs["L0_attn_b_bk_conv"]),
        "bg_b": f(inputs["L0_attn_b_bg"][None]),
        "bb_b": f(inputs["L0_attn_b_bb"][None]),
        "A_log": f(inputs["L0_attn_a_A_log"][:, None]),
        "dt_bias": f(inputs["L0_attn_a_dt_bias"][:, None]),
        "o_norm_w": f(inputs["L0_attn_a_o_norm_w"][None]),
        "ffn_scale": _colvec(inputs["L0_ffn_scale_weight"]),
        "ffn_bias": _colvec(inputs["L0_ffn_bias"]),
    }
    in_maps = []
    for core in range(8):
        b, r = core // 4, core % 4
        m = dict(shared)
        m["xs"] = f(x[b, r * TPC:(r + 1) * TPC, :].T)
        ri = np.zeros(8, np.float32)
        for j in range(3):
            ri[j] = 1.0 if j < r else 0.0
            ri[3 + j] = 1.0 if j == r - 1 else 0.0
        ri[6], ri[7] = r, b
        m["rankinfo"] = ri[None]
        in_maps.append(m)
    return in_maps


def kernel(**inputs):
    in_maps = host_prep(inputs)
    bld = Builder()
    nc = bld.build()
    res = run_bass_kernel_spmd(nc, in_maps, core_ids=list(range(8)))
    out = np.empty((B, L, D), np.float32)
    for core in range(8):
        b, r = core // 4, core % 4
        out[b, r * TPC:(r + 1) * TPC, :] = res.results[core]["y"].T
    return out
